# version 11 (final)
# baseline (speedup 1.0000x reference)
import numpy as np
from contextlib import ExitStack

import concourse.bass as bass
import concourse.mybir as mybir
from concourse.bass_utils import run_bass_kernel_spmd

F32 = mybir.dt.float32
BF16 = mybir.dt.bfloat16
I8 = mybir.dt.int8
I32 = mybir.dt.int32
AF = mybir.ActivationFunctionType
ALU = mybir.AluOpType
AX = mybir.AxisListType.X

B, L, D = 4, 4096, 1024
H, DH = 16, 64
HS = DH + 1
C_SHIFT = 16.0
ESHIFT = float(np.exp(-C_SHIFT))
MAGIC = 12582912.0
GP = 1366
NKT = 11
KTGS = [(0, 4), (4, 4), (8, 3)]


DEBUG_S1 = False


def _nk(kt):
    return 128 if kt < 10 else 86


class Cnt:
    def __init__(self):
        self.v = {}

    def bump(self, name, n=1):
        self.v[name] = self.v.get(name, 0) + n
        return self.v[name]

    def get(self, name):
        return self.v.get(name, 0)


def host_consts():
    ident = np.eye(128, dtype=np.float32)
    bm = np.ones((128, 132), dtype=np.float32)
    for r in range(128):
        for c in range(r, min(r + 5, 132)):
            bm[r, c] = 2.0
    cc = np.zeros(NKT * 128, dtype=np.float32)
    for p in range(GP):
        c = max(0, 2 - p) + max(0, p - (GP - 3))
        cc[p] = c * ESHIFT
    cc = np.ascontiguousarray(cc.reshape(NKT, 128).T)
    sel4 = np.zeros((4, 512), dtype=np.float32)
    for j in range(4):
        sel4[j, 128 * j:128 * j + 128] = 1.0
    return {"c_ident": ident, "c_band": bm, "c_ccorr": cc, "c_sel4": sel4}


def _band_window(kt, q0, nq):
    w0 = 128 * kt - 2
    c0 = max(0, w0 - q0)
    c1 = min(nq, w0 + 132 - q0)
    if c1 <= c0:
        return None
    return (c0, c1, c0 + q0 - w0)



def emit_l0_attn(nc, st, io, cnt):
    x = io["x"]
    attnout = io["attnout"]

    sb = lambda name, shape, dt: st.enter_context(nc.sbuf_tensor(name, shape, dt))
    sem = lambda name: st.enter_context(nc.semaphore(name))

    wq_sb = sb("s1_wq", [128, 8, 1024], BF16)
    wk_sb = sb("s1_wk", [128, 8, 1024], BF16)
    wv_sb = sb("s1_wv", [128, 8, 1024], BF16)
    identb = sb("s1_identb", [128, 128], BF16)
    band = sb("s1_band", [128, 132], BF16)
    ccorr = sb("s1_ccorr", [128, NKT], F32)
    negc = sb("s1_negc", [128, 1], F32)

    x_tm = sb("s1_xtm", [128, NKT, 1024], BF16)
    x_fm = sb("s1_xfm", [128, 8, NKT * 128], BF16)
    k_fm = sb("s1_kfm", [128, 8, NKT * 128], BF16)
    q_fm = sb("s1_qfm", [128, 8, NKT * 128], BF16)
    v_tm = sb("s1_vtm", [128, NKT, H, HS], BF16)
    a_buf = sb("s1_abuf", [128, 2, 2, 512], BF16)
    o_stage = sb("s1_ostage", [128, 4, 1024], F32)
    dbg_den = sb("s1_dbgden", [128, 4, 16], F32)
    dbg_num = sb("s1_dbgnum", [128, 4, 16], F32)
    dcol = sb("s1_dcol", [128, 8], F32)
    rcol = sb("s1_rcol", [128, 8], F32)

    s_w = sem("s1_w")
    s_x = sem("s1_x")
    s_t = sem("s1_t")
    s_v = sem("s1_v")
    s_a = sem("s1_a")
    s_b = sem("s1_b")
    s_e = sem("s1_e")
    s_o = sem("s1_o")
    s_dbg = sem("s1_dbg")

    with nc.Block("s1_ld") as blk:
        @blk.gpsimd
        def _(e):
            for w_sb, nm in ((wq_sb, "wq"), (wk_sb, "wk"), (wv_sb, "wv")):
                e.dma_start(w_sb[:], io[nm].rearrange("(c p) j -> p c j", p=128)
                            ).then_inc(s_w, 16)
                cnt.bump("s_w", 16)
            e.dma_start(identb[:], io["c_ident"]).then_inc(s_w, 16)
            e.dma_start(band[:], io["c_band"]).then_inc(s_w, 16)
            e.dma_start(ccorr[:], io["c_ccorr"]).then_inc(s_w, 16)
            cnt.bump("s_w", 48)
            e.wait_ge(s_w, cnt.get("s_w"))

        @blk.vector
        def _(e):
            e.memset(negc[:], -C_SHIFT)
            if DEBUG_S1:
                e.memset(x_fm[:], 0.0)
                e.memset(k_fm[:], 0.0)
                e.memset(q_fm[:], 0.0)
                e.memset(v_tm[:, :, :, 0:64], 0.0)
                e.memset(a_buf[:], 0.0)
                e.memset(o_stage[:], 0.0)
                e.memset(dbg_den[:], 0.0)
                e.memset(dbg_num[:], 0.0)
            e.memset(v_tm[:, :, :, 64:65], 1.0)

    for s in range(3):
        n_x = 1366 if s == 0 else 1365
        n_q = 1366 if s == 0 else 1365

        with nc.Block(f"s1_g{s}_ldx") as blk:
            @blk.vector
            def _(e):
                e.memset(x_tm[:, 10, :], 0.0).then_inc(s_v, 1)
                cnt.bump("s_v")

            @blk.gpsimd
            def _(e):
                e.wait_ge(s_v, cnt.get("s_v"))
                for t in range(NKT):
                    rows = min(128, n_x - 128 * t)
                    base = s + 3 * 128 * t
                    e.dma_start(
                        x_tm[0:rows, t, :],
                        x[base: base + 3 * (rows - 1) + 1: 3, :],
                    ).then_inc(s_x, 16)
                    cnt.bump("s_x", 16)
                e.wait_ge(s_x, cnt.get("s_x"))

        tquads = [(0, 4), (4, 4), (8, 3)]
        with nc.Block(f"s1_g{s}_tr") as blk, \
             nc.psum_tensor(f"s1_pst0_{s}", [128, 512], BF16) as pst0, \
             nc.psum_tensor(f"s1_pst1_{s}", [128, 512], BF16) as pst1:
            psts = [pst0, pst1]
            t0_base = cnt.get("s_t")
            v0_base = cnt.get("s_v")

            @blk.tensor
            def _(e):
                qi = 0
                for f in range(8):
                    for (t0, tn) in tquads:
                        if qi >= 2:
                            e.wait_ge(s_v, v0_base + qi - 1)
                        ps = psts[qi % 2]
                        ins = None
                        for i in range(tn):
                            ins = e.transpose(
                                ps[:, 128 * i:128 * i + 128],
                                x_tm[:, t0 + i, 128 * f:128 * f + 128],
                                identb[:],
                            )
                        ins.then_inc(s_t, 1)
                        cnt.bump("s_t")
                        qi += 1

            @blk.vector
            def _(e):
                qj = 0
                for f in range(8):
                    for (t0, tn) in tquads:
                        e.wait_ge(s_t, t0_base + qj + 1)
                        e.tensor_copy(
                            x_fm[:, f, 128 * t0:128 * (t0 + tn)],
                            psts[qj % 2][:, 0:128 * tn],
                        ).then_inc(s_v, 1)
                        cnt.bump("s_v")
                        qj += 1

        nsubs = [(0, 512), (512, 512), (1024, GP - 1024)]
        jobs = []
        for (w_sb, dst) in ((wk_sb, k_fm), (wq_sb, q_fm)):
            for m in range(8):
                for (o0, on) in nsubs:
                    jobs.append(("proj", w_sb, dst, m, o0, on))
        for t in range(NKT):
            for half in range(2):
                jobs.append(("v", t, half, _nk(t), None, None))

        with nc.Block(f"s1_g{s}_proj") as blk, \
             nc.psum_tensor(f"s1_psp0_{s}", [128, 512], F32) as psp0, \
             nc.psum_tensor(f"s1_psp1_{s}", [128, 512], F32) as psp1:
            psps = [psp0, psp1]
            t0_base = cnt.get("s_t")
            v0_base = cnt.get("s_v")

            @blk.tensor
            def _(e):
                for ji, job in enumerate(jobs):
                    ps = psps[ji % 2]
                    if ji >= 2:
                        e.wait_ge(s_v, v0_base + ji - 1)
                    ins = None
                    if job[0] == "proj":
                        _, w_sb, dst, m, o0, on = job
                        for kc in range(8):
                            ins = e.matmul(
                                ps[:, 0:on],
                                w_sb[:, kc, 128 * m:128 * m + 128],
                                x_fm[:, kc, o0:o0 + on],
                                start=(kc == 0), stop=(kc == 7),
                            )
                    else:
                        _, t, half, rows, _, _ = job
                        for kc in range(8):
                            ins = e.matmul(
                                ps[0:rows, 0:512],
                                x_fm[:, kc, 128 * t:128 * t + rows],
                                wv_sb[:, kc, 512 * half:512 * half + 512],
                                start=(kc == 0), stop=(kc == 7),
                            )
                    ins.then_inc(s_t, 1)
                    cnt.bump("s_t")

            @blk.vector
            def _(e):
                for ji, job in enumerate(jobs):
                    ps = psps[ji % 2]
                    e.wait_ge(s_t, t0_base + ji + 1)
                    if job[0] == "proj":
                        _, w_sb, dst, m, o0, on = job
                        e.tensor_copy(dst[:, m, o0:o0 + on], ps[:, 0:on]
                                      ).then_inc(s_v, 1)
                    else:
                        _, t, half, rows, _, _ = job
                        e.tensor_copy(
                            v_tm[0:rows, t, 8 * half:8 * half + 8, 0:64],
                            ps[0:rows, 0:512].rearrange("p (h c) -> p h c", h=8),
                        ).then_inc(s_v, 1)
                    cnt.bump("s_v")

        if DEBUG_S1 and s == 1:
            with nc.Block("s1_dbgden") as blk:
                @blk.sync
                def _(e):
                    e.dma_start(io["dbg_den"], dbg_den[:].rearrange("p a b -> p (a b)")).then_inc(s_o, 16)
                    e.dma_start(io["dbg_num"], dbg_num[:].rearrange("p a b -> p (a b)")).then_inc(s_o, 16)
                    cnt.bump("s_o", 32)
                    e.wait_ge(s_o, cnt.get("s_o"))
        if DEBUG_S1 and s == 0:
            with nc.Block("s1_dbg") as blk:
                @blk.gpsimd
                def _(e):
                    e.dma_start(io["dbg_k"], k_fm[:]).then_inc(s_dbg, 16)
                    e.dma_start(io["dbg_q"], q_fm[:]).then_inc(s_dbg, 16)
                    e.dma_start(io["dbg_v"], v_tm[:].rearrange("p a b c -> p (a b c)")).then_inc(s_dbg, 16)
                    e.dma_start(io["dbg_xfm"], x_fm[:]).then_inc(s_dbg, 16)
                    cnt.bump("s_dbg", 64)
                    e.wait_ge(s_dbg, cnt.get("s_dbg"))

        S1G = [(0, 2), (2, 2), (4, 2), (6, 2), (8, 2), (10, 1)]
        for (q0, nq) in [(0, 512), (512, 512), (1024, n_q - 1024)]:
            nqs = (nq + 127) // 128
            t_base = cnt.get("s_t")
            a_base = cnt.get("s_a")
            b_base = cnt.get("s_b")
            e_base = cnt.get("s_e")
            NG1 = len(S1G)
            NID = 16 * NG1

            def t_after_qk(idx):
                return t_base + (1 if idx == 0 else 2 * idx)

            def t_after_av(idx):
                return t_base + (2 * idx + 3 if idx < NID - 1 else 2 * NID)

            band_cum = [b_base]
            for h in range(16):
                for (k0, kn) in S1G:
                    nb = sum(1 for i in range(kn)
                             if _band_window(k0 + i, q0, nq) is not None)
                    band_cum.append(band_cum[-1] + nb)

            with nc.Block(f"s1_g{s}_att{q0}") as blk, \
                 nc.psum_tensor(f"s1_psS0_{s}_{q0}", [128, 1024], F32) as psS0, \
                 nc.psum_tensor(f"s1_psS1_{s}_{q0}", [128, 1024], F32) as psS1, \
                 nc.psum_tensor(f"s1_psO_{s}_{q0}", [128, 2048], F32) as psO:
                psS = [psS0, psS1]

                @blk.tensor
                def _(e):
                    def emit_qk(idx):
                        h, g = divmod(idx, NG1)
                        hc, hp = h // 2, 64 * (h % 2)
                        k0, kn = S1G[g]
                        par = idx % 2
                        if idx >= 2:
                            e.wait_ge(s_a, a_base + idx - 1)
                        ins = None
                        for i in range(kn):
                            kt = k0 + i
                            nk = _nk(kt)
                            ins = e.matmul(
                                psS[par][0:nk, 512 * i:512 * i + nq],
                                k_fm[hp:hp + DH, hc, 128 * kt:128 * kt + nk],
                                q_fm[hp:hp + DH, hc, q0:q0 + nq],
                                start=True, stop=True,
                            )
                        ins.then_inc(s_t, 1)
                        cnt.bump("s_t")

                    def emit_av(idx):
                        h, g = divmod(idx, NG1)
                        k0, kn = S1G[g]
                        par = idx % 2
                        e.wait_ge(s_a, a_base + idx + 1)
                        e.wait_ge(s_b, band_cum[idx + 1])
                        if g == 0 and h > 0:
                            e.wait_ge(s_e, e_base + h)
                        ins = None
                        for i in range(kn):
                            kt = k0 + i
                            nk = _nk(kt)
                            for j in range(nqs):
                                mj = min(128, nq - 128 * j)
                                ins = e.matmul(
                                    psO[0:mj, 512 * j:512 * j + HS],
                                    a_buf[0:nk, par, i, 128 * j:128 * j + mj],
                                    v_tm[0:nk, kt, h, :],
                                    start=(kt == 0), stop=(kt == 10),
                                    skip_group_check=True,
                                )
                        ins.then_inc(s_t, 1)
                        cnt.bump("s_t")

                    emit_qk(0)
                    for idx in range(1, NID):
                        emit_qk(idx)
                        emit_av(idx - 1)
                    emit_av(NID - 1)

                @blk.scalar
                def _(e):
                    for idx in range(NID):
                        h, g = divmod(idx, NG1)
                        k0, kn = S1G[g]
                        par = idx % 2
                        e.wait_ge(s_t, t_after_qk(idx))
                        if nq == 512 and kn == 2:
                            e.activation(
                                a_buf[:, par, :, :].rearrange("p a b -> p (a b)"),
                                psS[par][:], AF.Exp, bias=negc[:], scale=0.125,
                            ).then_inc(s_a, 1)
                        else:
                            ins = None
                            for i in range(kn):
                                nk = _nk(k0 + i)
                                ins = e.activation(
                                    a_buf[0:nk, par, i, 0:nq],
                                    psS[par][0:nk, 512 * i:512 * i + nq],
                                    AF.Exp, bias=negc[0:nk, :], scale=0.125,
                                )
                            ins.then_inc(s_a, 1)
                        cnt.bump("s_a")

                @blk.vector
                def _(e):
                    for idx in range(NID):
                        h, g = divmod(idx, NG1)
                        k0, kn = S1G[g]
                        par = idx % 2
                        nb = band_cum[idx + 1] - band_cum[idx]
                        if nb:
                            e.wait_ge(s_a, a_base + idx + 1)
                            for i in range(kn):
                                kt = k0 + i
                                w = _band_window(kt, q0, nq)
                                if w is None:
                                    continue
                                c0, c1, m0 = w
                                nk = _nk(kt)
                                e.tensor_mul(
                                    a_buf[0:nk, par, i, c0:c1],
                                    a_buf[0:nk, par, i, c0:c1],
                                    band[0:nk, m0:m0 + (c1 - c0)],
                                ).then_inc(s_b, 1)
                                cnt.bump("s_b")
                        if g == NG1 - 1:
                            e.wait_ge(s_t, t_after_av(idx))
                            hp8 = 4 * (h % 2)
                            ins = None
                            for j in range(nqs):
                                mj = min(128, nq - 128 * j)
                                qt = (q0 + 128 * j) // 128
                                ins = e.tensor_add(dcol[0:mj, hp8 + j:hp8 + j + 1],
                                                   psO[0:mj, 512 * j + 64:512 * j + 65],
                                                   ccorr[0:mj, qt:qt + 1])
                            ins.then_inc(s_v, 1)
                            e.wait_ge(s_v, cnt.bump("s_v"))
                            ins = None
                            for j in range(nqs):
                                mj = min(128, nq - 128 * j)
                                ins = e.reciprocal(rcol[0:mj, hp8 + j:hp8 + j + 1],
                                                   dcol[0:mj, hp8 + j:hp8 + j + 1])
                            ins.then_inc(s_v, 1)
                            e.wait_ge(s_v, cnt.bump("s_v"))
                            ins = None
                            for j in range(nqs):
                                mj = min(128, nq - 128 * j)
                                ins = e.tensor_scalar_mul(
                                    o_stage[0:mj, j, 64 * h:64 * h + 64],
                                    psO[0:mj, 512 * j:512 * j + 64],
                                    rcol[0:mj, hp8 + j:hp8 + j + 1],
                                )
                            ins.then_inc(s_e, 1)
                            cnt.bump("s_e")

            with nc.Block(f"s1_g{s}_st{q0}") as blk:
                @blk.sync
                def _(e):
                    for j in range(nqs):
                        mj = min(128, nq - 128 * j)
                        base = s + 3 * (q0 + 128 * j)
                        e.dma_start(
                            attnout[base: base + 3 * (mj - 1) + 1: 3, :],
                            o_stage[0:mj, j, :],
                        ).then_inc(s_o, 16)
                        cnt.bump("s_o", 16)
                    e.wait_ge(s_o, cnt.get("s_o"))




NT2 = 32
NCH = 8
HT = 32


def emit_l0_ffn(nc, st, io, cnt):
    sb = lambda name, shape, dt: st.enter_context(nc.sbuf_tensor(name, shape, dt))
    sem = lambda name: st.enter_context(nc.semaphore(name))

    w2q = sb("s2_w2q", [128, 32, 1024], BF16)
    cwm = sb("s2_cwm", [128, 32, 3], F32)
    cbb = sb("s2_cb", [128, 32], F32)
    identb = sb("s2_identb", [128, 128], BF16)
    identf = sb("s2_identf", [128, 128], F32)
    onesf = sb("s2_onesf", [128, 128], F32)
    epscol = sb("s2_eps", [128, 1], F32)
    bcast4 = sb("s2_bcast4", [128, 4], F32)

    s_d = sem("s2_d")
    s_g = sem("s2_g")
    s_t = sem("s2_t")
    s_v = sem("s2_v")
    s_a = sem("s2_a")
    s_c = sem("s2_c")
    s_zl = [sem("s2_zl0"), sem("s2_zl1")]
    s_xs = [sem("s2_xs0"), sem("s2_xs1")]
    s_w1 = [sem(f"s2_w1_{i}") for i in range(4)]
    s_zz = sem("s2_zz")
    s_xq = sem("s2_xq")
    s_x2 = [sem("s2_x2a"), sem("s2_x2b")]

    NEL1 = float(4096 * 1024)

    with nc.Block("s2_wq") as blk, \
         nc.sbuf_tensor("s2_wbuf", [128, 2, 4096], F32) as wbuf, \
         nc.sbuf_tensor("s2_asum", [128, 16], F32) as asum, \
         nc.sbuf_tensor("s2_tot", [1, 8], F32) as tot, \
         nc.sbuf_tensor("s2_xs", [128, 2, 4096], F32) as xsb, \
         nc.sbuf_tensor("s2_i8", [128, 2, 4096], I8) as i8b, \
         nc.psum_tensor("s2_ps_row", [1, 16], F32) as ps_row, \
         nc.psum_tensor("s2_ps_bc", [128, 4], F32) as ps_bc:

        w2r = io["w2t"].rearrange("(c p) j -> p c j", p=128)

        def wsrc(i):
            if i < 8:
                return io["w1t"][128 * i:128 * i + 128, :]
            g = i - 8
            return w2r[:, 4 * g:4 * g + 4, :]

        def wdst(i, buf):
            v = buf[:, i % 2, :]
            if i < 8:
                return v
            return v.rearrange("p (a b) -> p a b", a=4)

        d_base = cnt.get("s_d")
        v_base = cnt.get("s_v")
        t_base = cnt.get("s_t")
        a_base = cnt.get("s_a")
        g_base = cnt.get("s_g")

        @blk.sync
        def _(e):
            e.dma_start(cwm[:].rearrange("p a b -> p (a b)"), io["cw"]).then_inc(s_d, 16)
            e.dma_start(cbb[:], io["cb"]).then_inc(s_d, 16)
            e.wait_ge(s_d, cnt.bump("s_d", 32))
            for i in range(16):
                if i >= 2:
                    e.wait_ge(s_a, a_base + i - 1)
                e.dma_start(wdst(i, wbuf), wsrc(i)).then_inc(s_d, 16)
                e.wait_ge(s_d, cnt.bump("s_d", 16))
            for i in range(16):
                if i >= 2:
                    e.wait_ge(s_v, v_base + 20 + 5 * (i - 1))
                else:
                    e.wait_ge(s_v, v_base + 15 + i)
                e.dma_start(wdst(i, wbuf), wsrc(i)).then_inc(s_d, 16)
                e.wait_ge(s_d, cnt.bump("s_d", 16))
                if i < 8:
                    e.wait_ge(s_v, v_base + 20 + 5 * i + 4)
                    e.dma_start(
                        io["w1qd"][:, :, 128 * i:128 * i + 128].rearrange("m p j -> p m j"),
                        i8b[:, i % 2, :].rearrange("p (m j) -> p m j", m=32),
                    ).then_inc(s_d, 16)
                    e.wait_ge(s_d, cnt.bump("s_d", 16))
                    cnt.v[f"s2_w1sp_{i}"] = cnt.get("s_d")

        @blk.gpsimd
        def _(e):
            e.dma_start(identb[:], io["c_ident"]).then_inc(s_g, 16)
            e.dma_start(onesf[:], io["c_ones"]).then_inc(s_g, 16)
            e.dma_start(identf[:], io["c_ident"]).then_inc(s_g, 16)
            cnt.bump("s_g", 48)

        @blk.scalar
        def _(e):
            for i in range(16):
                e.wait_ge(s_d, d_base + 32 + 16 * (i + 1))
                if i >= 2:
                    e.wait_ge(s_v, v_base + i - 1)
                e.activation(xsb[:, i % 2, :], wbuf[:, i % 2, :], AF.Abs
                             ).then_inc(s_a, 1)
                cnt.bump("s_a")

        @blk.vector
        def _(e):
            e.memset(epscol[:], 1e-5)
            for i in range(16):
                e.wait_ge(s_a, a_base + i + 1)
                e.reduce_sum(asum[:, i:i + 1], xsb[:, i % 2, :], axis=AX
                             ).then_inc(s_v, 1)
                cnt.bump("s_v")

        @blk.tensor
        def _(e):
            e.wait_ge(s_v, v_base + 16)
            e.wait_ge(s_g, g_base + 48)
            e.matmul(ps_row[:, 0:16], onesf[:, 0:1], asum[:], start=True,
                     stop=True).then_inc(s_t, 1)
            cnt.bump("s_t")

        @blk.vector
        def _(e):
            e.wait_ge(s_t, t_base + 1)
            e.reduce_sum(tot[0:1, 0:1], ps_row[0:1, 0:8], axis=AX)
            e.reduce_sum(tot[0:1, 1:2], ps_row[0:1, 8:16], axis=AX
                         ).then_inc(s_v, 1)
            e.wait_ge(s_v, cnt.bump("s_v"))
            e.tensor_scalar(tot[0:1, 2:3], tot[0:1, 0:1], 1.0 / NEL1, 1e-5,
                            ALU.mult, ALU.max)
            e.tensor_scalar(tot[0:1, 3:4], tot[0:1, 1:2], 1.0 / NEL1, 1e-5,
                            ALU.mult, ALU.max).then_inc(s_v, 1)
            e.wait_ge(s_v, cnt.bump("s_v"))
            e.reciprocal(tot[0:1, 4:5], tot[0:1, 2:3])
            e.reciprocal(tot[0:1, 5:6], tot[0:1, 3:4]).then_inc(s_v, 1)
            cnt.bump("s_v")

        @blk.tensor
        def _(e):
            e.wait_ge(s_v, v_base + 19)
            e.matmul(ps_bc[:, 0:1], onesf[0:1, :], tot[0:1, 4:5], start=True, stop=True)
            e.matmul(ps_bc[:, 1:2], onesf[0:1, :], tot[0:1, 2:3], start=True, stop=True)
            e.matmul(ps_bc[:, 2:3], onesf[0:1, :], tot[0:1, 5:6], start=True, stop=True)
            e.matmul(ps_bc[:, 3:4], onesf[0:1, :], tot[0:1, 3:4],
                     start=True, stop=True).then_inc(s_t, 1)
            cnt.bump("s_t")

        @blk.scalar
        def _(e):
            e.wait_ge(s_t, t_base + 2)
            e.copy(bcast4[:], ps_bc[:]).then_inc(s_a, 1)
            cnt.bump("s_a")

        @blk.vector
        def _(e):
            e.wait_ge(s_a, a_base + 17)
            e.wait_ge(s_g, g_base + 48)
            e.tensor_scalar_mul(cwm[:].rearrange("p a b -> p (a b)"),
                                cwm[:].rearrange("p a b -> p (a b)"),
                                bcast4[:, 1:2]).then_inc(s_v, 1)
            cnt.bump("s_v")
            for i in range(16):
                scol = bcast4[:, 0:1] if i < 8 else bcast4[:, 2:3]
                e.wait_ge(s_d, d_base + 32 + 256 + 16 * (i + 1) + 16 * min(i, 8))
                e.tensor_scalar_mul(xsb[:, i % 2, :], wbuf[:, i % 2, :], scol
                                    ).then_inc(s_v, 1)
                e.wait_ge(s_v, cnt.bump("s_v"))
                e.tensor_scalar(xsb[:, i % 2, :], xsb[:, i % 2, :], 1.0, -1.0,
                                ALU.min, ALU.max).then_inc(s_v, 1)
                e.wait_ge(s_v, cnt.bump("s_v"))
                e.tensor_scalar(xsb[:, i % 2, :], xsb[:, i % 2, :], MAGIC, -MAGIC,
                                ALU.add, ALU.add).then_inc(s_v, 1)
                e.wait_ge(s_v, cnt.bump("s_v"))
                if 2 <= i < 10:
                    e.wait_ge(s_d, cnt.get(f"s2_w1sp_{i - 2}"))
                e.tensor_copy(i8b[:, i % 2, :], xsb[:, i % 2, :]).then_inc(s_v, 1)
                e.wait_ge(s_v, cnt.bump("s_v"))
                if i < 8:
                    e.tensor_copy(xsb[:, i % 2, 0:1], xsb[:, i % 2, 0:1]
                                  ).then_inc(s_v, 1)
                    cnt.bump("s_v")
                else:
                    g = i - 8
                    dst = w2q[:, 4 * g:4 * g + 4, :].rearrange("p a b -> p (a b)")
                    e.tensor_copy(dst, i8b[:, i % 2, :]).then_inc(s_v, 1)
                    cnt.bump("s_v")

    with nc.Block("s2_zq") as blk, \
         nc.sbuf_tensor("s2_zb", [128, 2, 1024], F32) as zb, \
         nc.sbuf_tensor("s2_zln", [128, 2, 1024], F32) as zlnb, \
         nc.sbuf_tensor("s2_st6", [128, 12], F32) as st6, \
         nc.sbuf_tensor("s2_mv", [128, 2], F32) as mv, \
         nc.sbuf_tensor("s2_stdc", [128, 1], F32) as stdc, \
         nc.sbuf_tensor("s2_rstd", [128, 1], F32) as rstd, \
         nc.sbuf_tensor("s2_nmr", [128, 1], F32) as nmr, \
         nc.sbuf_tensor("s2_am", [128, 4], F32) as am, \
         nc.sbuf_tensor("s2_xs1", [128, 1024], F32) as xs1, \
         nc.sbuf_tensor("s2_xqbf", [128, 2, 1024], BF16) as xqbf, \
         nc.sbuf_tensor("s2_xqev", [128, 2, 1024], BF16) as xqev, \
         nc.psum_tensor("s2_ps_tr", [128, 1024], BF16) as ps_tr:

        for t in range(NT2):
            par = t % 2

            @blk.gpsimd
            def _(e):
                if t >= 1:
                    e.wait_ge(s_g, cnt.get(f"s2_zld_{t - 1}"))
                if t >= 2:
                    e.wait_ge(s_v, cnt.get(f"s2_zcons_{t - 2}"))
                e.dma_start(zb[:, par, :], io["x"][128 * t:128 * t + 128, :]
                            ).then_inc(s_g, 16)
                e.wait_ge(s_g, cnt.bump("s_g", 16))
                e.dma_start(zb[:, par, :], io["attnout"][128 * t:128 * t + 128, :],
                            accum_op=ALU.add).then_inc(s_g, 16)
                cnt.bump("s_g", 16)
                cnt.v[f"s2_zld_{t}"] = cnt.get("s_g")

            @blk.vector
            def _(e):
                e.wait_ge(s_g, cnt.get(f"s2_zld_{t}"))
                e.bn_stats(st6[:, 0:6], zb[:, par, 0:512])
                e.bn_stats(st6[:, 6:12], zb[:, par, 512:1024]).then_inc(s_v, 1)
                e.wait_ge(s_v, cnt.bump("s_v"))
                e.bn_aggr(mv[:], st6[:]).then_inc(s_v, 1)
                cnt.bump("s_v")
                cnt.v[f"s2_bn_{t}"] = cnt.get("s_v")

            @blk.scalar
            def _(e):
                e.wait_ge(s_v, cnt.get(f"s2_bn_{t}"))
                e.activation(stdc[:], mv[:, 1:2], AF.Sqrt, bias=epscol[:]
                             ).then_inc(s_a, 1)
                cnt.bump("s_a")
                cnt.v[f"s2_std_{t}"] = cnt.get("s_a")

            @blk.vector
            def _(e):
                e.wait_ge(s_a, cnt.get(f"s2_std_{t}"))
                if t >= 2:
                    e.wait_ge(s_zl[par], cnt.get(f"s2_zlnst_{t - 2}"))
                e.reciprocal(rstd[:], stdc[:]).then_inc(s_v, 1)
                e.wait_ge(s_v, cnt.bump("s_v"))
                e.scalar_tensor_tensor(nmr[:], mv[:, 0:1], -1.0, rstd[:],
                                       ALU.mult, ALU.mult).then_inc(s_v, 1)
                e.wait_ge(s_v, cnt.bump("s_v"))
                e.tensor_scalar(zlnb[:, par, :], zb[:, par, :], rstd[:], nmr[:],
                                ALU.mult, ALU.add).then_inc(s_v, 1)
                e.wait_ge(s_v, cnt.bump("s_v"))
                cnt.v[f"s2_zcons_{t}"] = cnt.get("s_v")
                cnt.v[f"s2_zln_{t}"] = cnt.get("s_v")
                e.reduce_max(am[:, 0:1], zlnb[:, par, :], axis=AX,
                             apply_absolute_value=True).then_inc(s_v, 1)
                e.wait_ge(s_v, cnt.bump("s_v"))
                e.tensor_scalar_max(am[:, 1:2], am[:, 0:1], 1e-5).then_inc(s_v, 1)
                e.wait_ge(s_v, cnt.bump("s_v"))
                e.reciprocal(am[:, 2:3], am[:, 1:2])
                e.tensor_scalar_mul(am[:, 3:4], am[:, 1:2], 1.0 / 127.0
                                    ).then_inc(s_v, 1)
                e.wait_ge(s_v, cnt.bump("s_v"))
                e.tensor_scalar(xs1[:], zlnb[:, par, :], am[:, 2:3], 127.0,
                                ALU.mult, ALU.mult).then_inc(s_v, 1)
                e.wait_ge(s_v, cnt.bump("s_v"))
                e.tensor_scalar(xs1[:], xs1[:], MAGIC, -MAGIC, ALU.add, ALU.add
                                ).then_inc(s_v, 1)
                e.wait_ge(s_v, cnt.bump("s_v"))
                e.tensor_scalar_mul(xqbf[:, par, :], xs1[:], am[:, 3:4]
                                    ).then_inc(s_v, 1)
                cnt.bump("s_v")
                cnt.v[f"s2_xqbf_{t}"] = cnt.get("s_v")

            @blk.sync
            def _(e):
                e.wait_ge(s_v, cnt.get(f"s2_zln_{t}"))
                e.dma_start(io["zln"][128 * t:128 * t + 128, :], zlnb[:, par, :]
                            ).then_inc(s_zl[par], 16)
                cnt.bump(f"s2zl{par}", 16)
                cnt.v[f"s2_zlnst_{t}"] = cnt.get(f"s2zl{par}")

            @blk.tensor
            def _(e):
                e.wait_ge(s_v, cnt.get(f"s2_xqbf_{t}"))
                if t >= 2:
                    e.wait_ge(s_v, cnt.get(f"s2_trev_{t - 2}"))
                ins = None
                for f in range(8):
                    ins = e.transpose(ps_tr[:, 128 * f:128 * f + 128],
                                      xqbf[:, par, 128 * f:128 * f + 128], identb[:])
                ins.then_inc(s_t, 1)
                cnt.bump("s_t")
                cnt.v[f"s2_tr_{t}"] = cnt.get("s_t")

            @blk.vector
            def _(e):
                e.wait_ge(s_t, cnt.get(f"s2_tr_{t}"))
                if t >= 2:
                    e.wait_ge(s_xs[par], cnt.get(f"s2_xqst_{t - 2}"))
                e.tensor_copy(xqev[:, par, :], ps_tr[:]).then_inc(s_v, 1)
                cnt.bump("s_v")
                cnt.v[f"s2_trev_{t}"] = cnt.get("s_v")

            @blk.sync
            def _(e):
                e.wait_ge(s_v, cnt.get(f"s2_trev_{t}"))
                e.dma_start(
                    io["xqf"].rearrange("(a p) j -> p a j", p=128)[:, :, 128 * t:128 * t + 128],
                    xqev[:, par, :].rearrange("p (a b) -> p a b", a=8),
                ).then_inc(s_xs[par], 16)
                cnt.bump(f"s2xs{par}", 16)
                cnt.v[f"s2_xqst_{t}"] = cnt.get(f"s2xs{par}")

        @blk.sync
        def _(e):
            e.wait_ge(s_zl[0], cnt.get("s2zl0"))
            e.wait_ge(s_zl[1], cnt.get("s2zl1"))
            e.wait_ge(s_xs[0], cnt.get("s2xs0"))
            e.wait_ge(s_xs[1], cnt.get("s2xs1"))
            e.wait_ge(s_d, cnt.get("s_d"))

        @blk.gpsimd
        def _(e):
            e.wait_ge(s_g, cnt.get("s_g"))

    with ExitStack() as stc:
        blk = stc.enter_context(nc.Block("s2_ffnC"))
        sbc = lambda nm, shp, dt: stc.enter_context(nc.sbuf_tensor(nm, shp, dt))
        psc_ = lambda nm, shp, dt: stc.enter_context(nc.psum_tensor(nm, shp, dt))
        xqc = sbc("s2_xqc", [128, 8, 514], BF16)
        w1s = sbc("s2_w1s", [128, 4, 8, 128], BF16)
        ca = sbc("s2_ca", [128, 2, 512], BF16)
        hbufH = sbc("s2_hbufH", [128, 2, 2], BF16)
        hs = sbc("s2_hs", [128, 32, 512], BF16)
        hq = sbc("s2_hqb", [128, 32, 512], BF16)
        rp = sbc("s2_rp", [128, 4, 512], BF16)
        rn = sbc("s2_rn", [128, 2, 512], BF16)
        rpf = sbc("s2_rpf", [128, 512], F32)
        am4 = sbc("s2_am4", [128, 2, 16], F32)
        scT = sbc("s2_sct", [4, 128], F32)
        sel4 = sbc("s2_sel4", [4, 512], F32)
        bscale = sbc("s2_bscale", [128, 512], F32)
        xs8 = sbc("s2_xs8", [128, 2, 512], F32)
        zz = sbc("s2_zz", [128, 4, 1024], F32)
        st6c = sbc("s2_st6b", [128, 12], F32)
        mvc = sbc("s2_mvb", [128, 4, 2], F32)
        ln2c = sbc("s2_ln2", [128, 4, 4], F32)
        ph0 = psc_("s2_ph0", [128, 512], F32)
        ph1 = psc_("s2_ph1", [128, 512], F32)
        phh = psc_("s2_phh", [128, 2], F32)
        psM = psc_("s2_psm", [128, 512], F32)
        pf0 = psc_("s2_pf0", [128, 1024], F32)
        pf1 = psc_("s2_pf1", [128, 1024], F32)
        phs = [ph0, ph1]
        pfs = [pf0, pf1]

        @blk.gpsimd
        def _(e):
            e.dma_start(sel4[:], io["c_sel4"]).then_inc(s_g, 16)
            cnt.bump("s_g", 16)
            cnt.v["s2P_sel4"] = cnt.get("s_g")

        for c in range(NCH + 1):
            head = c < NCH
            tail = c >= 1
            cp = c - 1
            t0 = 512 * c
            lo = (c == 0)
            hi = (c == NCH - 1)
            src0 = max(t0 - 1, 0)
            src1 = min(t0 + 513, 4096)
            dst0 = src0 - (t0 - 1)

            v_base = cnt.get("s_v")
            t_base = cnt.get("s_t")
            a_base = cnt.get("s_a")
            M0 = v_base + 1
            F2 = 4 if tail else 0
            PINC = 7 if tail else 4
            EPI0 = (M0 + 16 * PINC + 3) if head else (v_base + 48)

            QOFF = 3 if tail else 0

            def conv3mark(m, M0=M0, PINC=PINC, QOFF=QOFF):
                return M0 + PINC * (m // 2) + QOFF + 2

            def maxes(m, M0=M0, PINC=PINC, QOFF=QOFF):
                return M0 + PINC * (m // 2) + QOFF + 3 + (m % 2)

            def qmark(g, M0=M0, PINC=PINC):
                return M0 + PINC * g + 3

            def hvmark(m, a_base=a_base):
                return a_base + 8 * (m // 2) + 1 + 2 * (m % 2)

            def camark(m, a_base=a_base):
                return a_base + 8 * (m // 2) + 2 + 2 * (m % 2)

            def silumark(m, a_base=a_base):
                return a_base + 8 * (m // 2) + 5 + (m % 2)

            def absmark(m, a_base=a_base):
                return a_base + 8 * (m // 2) + 7 + (m % 2)

            def sttmark(tt, EPI0=EPI0):
                return EPI0 + 3 * tt + 1

            @blk.sync
            def _(e):
                if not head:
                    return
                if c >= 1:
                    e.wait_ge(s_t, cnt.get(f"s2P_ff1_{c - 1}_31"))
                e.dma_start(
                    xqc[:, :, dst0:dst0 + (src1 - src0)],
                    io["xqf"].rearrange("(a p) j -> p a j", p=128)[:, :, src0:src1],
                ).then_inc(s_xq, 16)
                cnt.bump("s2xq", 16)
                cnt.v[f"s2P_xqc_{c}"] = cnt.get("s2xq")

            @blk.gpsimd
            def _(e):
                if not head:
                    return
                if c == 0:
                    e.wait_ge(s_g, cnt.get("s2P_sel4"))
                for m in range(HT):
                    slot = m % 4
                    if m >= 4:
                        e.wait_ge(s_t, t_base + (m - 4) + 1)
                    elif c >= 1:
                        e.wait_ge(s_t, cnt.get(f"s2P_ff1_{c - 1}_{28 + m}"))
                    e.dma_start(
                        w1s[:, slot, :, :],
                        io["w1qd"][m].rearrange("p (k j) -> p k j", k=8),
                    ).then_inc(s_w1[slot], 16)
                    cnt.bump(f"s2w1_{slot}", 16)
                    cnt.v[f"s2P_w1_{c}_{m}"] = cnt.get(f"s2w1_{slot}")

            @blk.vector
            def _(e):
                if tail and not head:
                    e.wait_ge(s_a, cnt.get(f"s2P_bsc_{cp}"))
                    if cp >= 1:
                        e.wait_ge(s_t, cnt.get(f"s2P_ffn2_{cp - 1}_3"))
                    for g in range(16):
                        e.wait_ge(s_a, cnt.get(f"s2P_abs_{cp}_{2 * g + 1}"))
                        ins = None
                        for i in range(2):
                            ins = e.tensor_mul(xs8[:, i, :], hs[:, 2 * g + i, :],
                                               bscale[:])
                        ins.then_inc(s_v, 1)
                        e.wait_ge(s_v, cnt.bump("s_v"))
                        ins = None
                        for i in range(2):
                            ins = e.tensor_scalar(xs8[:, i, :], xs8[:, i, :], MAGIC,
                                                  -MAGIC, ALU.add, ALU.add)
                        ins.then_inc(s_v, 1)
                        e.wait_ge(s_v, cnt.bump("s_v"))
                        ins = None
                        for i in range(2):
                            ins = e.tensor_copy(hq[:, 2 * g + i, :], xs8[:, i, :])
                        ins.then_inc(s_v, 1)
                        e.wait_ge(s_v, cnt.bump("s_v"))
                        cnt.v[f"s2P_q_{cp}_{g}"] = cnt.get("s_v")
                    cnt.v[f"s2P_qlast_{cp}"] = cnt.get("s_v")

            @blk.tensor
            def _(e):
                if head:
                    e.wait_ge(s_xq, cnt.get(f"s2P_xqc_{c}"))
                    if lo or hi:
                        e.wait_ge(s_v, M0)
                    for m in range(HT):
                        e.wait_ge(s_w1[m % 4], cnt.get(f"s2P_w1_{c}_{m}"))
                        if m >= 2:
                            e.wait_ge(s_v, conv3mark(m - 2))
                        elif c >= 1:
                            e.wait_ge(s_v, cnt.get(f"s2P_conv_{c - 1}_15"))
                        for kc in range(8):
                            e.matmul(phs[m % 2][:], w1s[:, m % 4, kc, :],
                                     xqc[:, kc, 1:513], start=(kc == 0), stop=(kc == 7))
                        if m >= 1:
                            e.wait_ge(s_a, hvmark(m - 1))
                        elif c >= 1:
                            e.wait_ge(s_a, cnt.get(f"s2P_hv_{c - 1}_31"))
                        ins = None
                        for kc in range(8):
                            ins = e.matmul(phh[:], w1s[:, m % 4, kc, :],
                                           xqc[:, kc, 0:514:513],
                                           start=(kc == 0), stop=(kc == 7))
                        ins.then_inc(s_t, 1)
                        cnt.bump("s_t")
                        cnt.v[f"s2P_ff1_{c}_{m}"] = cnt.get("s_t")
                if tail:
                    qlast = (M0 + 16 * PINC) if head else (v_base + 48)
                    for tt in range(4):
                        if tt == 0:
                            e.wait_ge(s_v, qlast)
                        if tt >= 2:
                            e.wait_ge(s_v, sttmark(tt - 2))
                        elif cp >= 1:
                            e.wait_ge(s_v, cnt.get(f"s2P_stt_{cp - 1}_{tt + 2}"))
                        ins = None
                        for half in range(2):
                            for k in range(HT):
                                ins = e.matmul(pfs[tt % 2][:, 512 * half:512 * half + 512],
                                               hq[:, k, 128 * tt:128 * tt + 128],
                                               w2q[:, k, 512 * half:512 * half + 512],
                                               start=(k == 0), stop=(k == HT - 1))
                        ins.then_inc(s_t, 1)
                        cnt.bump("s_t")
                        cnt.v[f"s2P_ffn2_{cp}_{tt}"] = cnt.get("s_t")

            @blk.scalar
            def _(e):
                if not head:
                    return
                for p in range(HT // 2):
                    m0, m1 = 2 * p, 2 * p + 1
                    for m in (m0, m1):
                        e.wait_ge(s_t, t_base + m + 1)
                        if m >= 2:
                            e.wait_ge(s_v, conv3mark(m - 2))
                        elif c >= 1:
                            e.wait_ge(s_v, cnt.get(f"s2P_conv_{c - 1}_15"))
                        e.copy(hbufH[:, m % 2, :], phh[:]).then_inc(s_a, 1)
                        cnt.bump("s_a")
                        cnt.v[f"s2P_hv_{c}_{m}"] = cnt.get("s_a")
                        if m >= 2:
                            e.wait_ge(s_a, silumark(m - 2))
                        elif c >= 1:
                            e.wait_ge(s_a, cnt.get(f"s2P_silu_{c - 1}_{m + 30}"))
                        e.activation(ca[:, m % 2, :], phs[m % 2][:], AF.Copy,
                                     scale=cwm[:, m, 1:2]).then_inc(s_a, 1)
                        cnt.bump("s_a")
                        cnt.v[f"s2P_ca_{c}_{m}"] = cnt.get("s_a")
                    for m in (m0, m1):
                        e.wait_ge(s_v, conv3mark(m))
                        if c >= 1:
                            e.wait_ge(s_v, qmark(m // 2))
                        e.activation(hs[:, m, :], ca[:, m % 2, :], AF.Silu,
                                     bias=cbb[:, m:m + 1]).then_inc(s_a, 1)
                        cnt.bump("s_a")
                        cnt.v[f"s2P_silu_{c}_{m}"] = cnt.get("s_a")
                    for m in (m0, m1):
                        if m >= 2:
                            e.wait_ge(s_v, maxes(m - 2))
                        elif c >= 1:
                            e.wait_ge(s_v, cnt.get(f"s2P_max_{c - 1}_{30 + m}"))
                        e.wait_ge(s_a, silumark(m))
                        e.activation(rn[:, m % 2, :], hs[:, m, :], AF.Abs
                                     ).then_inc(s_a, 1)
                        cnt.bump("s_a")
                        cnt.v[f"s2P_abs_{c}_{m}"] = cnt.get("s_a")

            @blk.vector
            def _(e):
                if head:
                    if c >= 1:
                        e.wait_ge(s_t, cnt.get(f"s2P_ptrF_{c - 1}"))
                    ins = e.memset(rp[:], 0.0)
                    if lo or hi:
                        e.wait_ge(s_xq, cnt.get(f"s2P_xqc_{c}"))
                        ins = e.memset(xqc[:, :, 0:1] if lo else xqc[:, :, 513:514],
                                       0.0)
                    ins.then_inc(s_v, 1)
                    cnt.bump("s_v")
                    if tail:
                        e.wait_ge(s_a, cnt.get(f"s2P_bsc_{cp}"))
                        if cp >= 1:
                            e.wait_ge(s_t, cnt.get(f"s2P_ffn2_{cp - 1}_3"))
                    for p in range(HT // 2):
                        m0, m1 = 2 * p, 2 * p + 1
                        if tail:
                            g = p
                            e.wait_ge(s_a, cnt.get(f"s2P_abs_{cp}_{2 * g + 1}"))
                            ins = None
                            for i in range(2):
                                ins = e.tensor_mul(xs8[:, i, :], hs[:, 2 * g + i, :],
                                                   bscale[:])
                            ins.then_inc(s_v, 1)
                            e.wait_ge(s_v, cnt.bump("s_v"))
                            ins = None
                            for i in range(2):
                                ins = e.tensor_scalar(xs8[:, i, :], xs8[:, i, :],
                                                      MAGIC, -MAGIC, ALU.add, ALU.add)
                            ins.then_inc(s_v, 1)
                            e.wait_ge(s_v, cnt.bump("s_v"))
                            ins = None
                            for i in range(2):
                                ins = e.tensor_copy(hq[:, 2 * g + i, :], xs8[:, i, :])
                            ins.then_inc(s_v, 1)
                            e.wait_ge(s_v, cnt.bump("s_v"))
                            cnt.v[f"s2P_q_{cp}_{g}"] = cnt.get("s_v")
                        ins = None
                        for m in (m0, m1):
                            e.wait_ge(s_a, camark(m))
                            e.scalar_tensor_tensor(ca[:, m % 2, 0:511],
                                                   phs[m % 2][:, 1:512],
                                                   cwm[:, m, 2:3],
                                                   ca[:, m % 2, 0:511],
                                                   ALU.mult, ALU.add)
                            ins = e.scalar_tensor_tensor(ca[:, m % 2, 511:512],
                                                         hbufH[:, m % 2, 1:2],
                                                         cwm[:, m, 2:3],
                                                         ca[:, m % 2, 511:512],
                                                         ALU.mult, ALU.add)
                        ins.then_inc(s_v, 1)
                        e.wait_ge(s_v, cnt.bump("s_v"))
                        ins = None
                        for m in (m0, m1):
                            e.scalar_tensor_tensor(ca[:, m % 2, 1:512],
                                                   phs[m % 2][:, 0:511],
                                                   cwm[:, m, 0:1],
                                                   ca[:, m % 2, 1:512],
                                                   ALU.mult, ALU.add)
                            ins = e.scalar_tensor_tensor(ca[:, m % 2, 0:1],
                                                         hbufH[:, m % 2, 0:1],
                                                         cwm[:, m, 0:1],
                                                         ca[:, m % 2, 0:1],
                                                         ALU.mult, ALU.add)
                        ins.then_inc(s_v, 1)
                        cnt.bump("s_v")
                        cnt.v[f"s2P_conv_{c}_{p}"] = cnt.get("s_v")
                        for m in (m0, m1):
                            e.wait_ge(s_a, absmark(m))
                            e.tensor_max(rp[:, m % 4, :], rp[:, m % 4, :],
                                         rn[:, m % 2, :]).then_inc(s_v, 1)
                            cnt.bump("s_v")
                            cnt.v[f"s2P_max_{c}_{m}"] = cnt.get("s_v")
                    if tail:
                        cnt.v[f"s2P_qlast_{cp}"] = cnt.get("s_v")
                    e.wait_ge(s_v, maxes(HT - 1))
                    e.tensor_max(rp[:, 0, :], rp[:, 0, :], rp[:, 1, :])
                    e.tensor_max(rp[:, 2, :], rp[:, 2, :], rp[:, 3, :]).then_inc(s_v, 1)
                    e.wait_ge(s_v, cnt.bump("s_v"))
                    e.tensor_max(rp[:, 0, :], rp[:, 0, :], rp[:, 2, :]).then_inc(s_v, 1)
                    e.wait_ge(s_v, cnt.bump("s_v"))
                    if c >= 1:
                        e.wait_ge(s_t, cnt.get(f"s2P_ptrF_{c - 1}"))
                    e.tensor_copy(rpf[:], rp[:, 0, :]).then_inc(s_v, 1)
                    e.wait_ge(s_v, cnt.bump("s_v"))
                    cnt.v[f"s2P_cast_{c}"] = cnt.get("s_v")
                if tail:
                    for tt in range(4):
                        e.wait_ge(s_t, cnt.get(f"s2P_ffn2_{cp}_{tt}"))
                        e.wait_ge(s_zz, cnt.get(f"s2P_zz_{cp}"))
                        e.scalar_tensor_tensor(zz[:, tt, 0:512],
                                               pfs[tt % 2][:, 0:512],
                                               am4[:, cp % 2, 12 + tt:13 + tt],
                                               zz[:, tt, 0:512], ALU.mult, ALU.add)
                        e.scalar_tensor_tensor(zz[:, tt, 512:1024],
                                               pfs[tt % 2][:, 512:1024],
                                               am4[:, cp % 2, 12 + tt:13 + tt],
                                               zz[:, tt, 512:1024], ALU.mult, ALU.add
                                               ).then_inc(s_v, 1)
                        e.wait_ge(s_v, cnt.bump("s_v"))
                        cnt.v[f"s2P_stt_{cp}_{tt}"] = cnt.get("s_v")
                        e.bn_stats(st6c[:, 0:6], zz[:, tt, 0:512])
                        e.bn_stats(st6c[:, 6:12], zz[:, tt, 512:1024]
                                   ).then_inc(s_v, 1)
                        e.wait_ge(s_v, cnt.bump("s_v"))
                        if cp >= 1:
                            e.wait_ge(s_a, cnt.get(f"s2P_sqrt_{cp - 1}_{tt}"))
                        e.bn_aggr(mvc[:, tt, :], st6c[:]).then_inc(s_v, 1)
                        cnt.bump("s_v")
                        cnt.v[f"s2P_bn2_{cp}_{tt}"] = cnt.get("s_v")
                if head:
                    e.wait_ge(s_t, t_base + 32 + F2 + 1)
                    ins = None
                    for j in range(4):
                        ins = e.reduce_max(am4[:, c % 2, j:j + 1],
                                           psM[:, 128 * j:128 * j + 128], axis=AX)
                    ins.then_inc(s_v, 1)
                    e.wait_ge(s_v, cnt.bump("s_v"))
                    cnt.v[f"s2P_am4r_{c}"] = cnt.get("s_v")
                    e.tensor_scalar_max(am4[:, c % 2, 4:8], am4[:, c % 2, 0:4], 1e-5
                                        ).then_inc(s_v, 1)
                    e.wait_ge(s_v, cnt.bump("s_v"))
                    e.reciprocal(am4[:, c % 2, 8:12], am4[:, c % 2, 4:8])
                    e.tensor_scalar(am4[:, c % 2, 12:16], am4[:, c % 2, 4:8],
                                    bcast4[:, 3:4], 1.0 / 127.0, ALU.mult, ALU.mult
                                    ).then_inc(s_v, 1)
                    e.wait_ge(s_v, cnt.bump("s_v"))
                    e.tensor_scalar_mul(am4[:, c % 2, 8:12], am4[:, c % 2, 8:12],
                                        127.0).then_inc(s_v, 1)
                    cnt.bump("s_v")
                    cnt.v[f"s2P_am4_{c}"] = cnt.get("s_v")

            @blk.tensor
            def _(e):
                if not head:
                    return
                e.wait_ge(s_v, cnt.get(f"s2P_cast_{c}"))
                if c >= 1:
                    e.wait_ge(s_a, cnt.get(f"s2P_bsc_{c - 1}"))
                ins = None
                for j in range(4):
                    ins = e.transpose(psM[:, 128 * j:128 * j + 128],
                                      rpf[:, 128 * j:128 * j + 128], identf[:])
                ins.then_inc(s_t, 1)
                cnt.bump("s_t")
                cnt.v[f"s2P_ptrF_{c}"] = cnt.get("s_t")
                e.wait_ge(s_v, cnt.get(f"s2P_am4_{c}"))
                e.transpose(psM[0:4, 0:128], am4[:, c % 2, 8:12], identf[:]
                            ).then_inc(s_t, 1)
                cnt.bump("s_t")
                cnt.v[f"s2P_psc_{c}"] = cnt.get("s_t")
                e.wait_ge(s_a, a_base + 128 + F2 + 1)
                ins = None
                for j in range(4):
                    ins = e.matmul(psM[:, 128 * j:128 * j + 128],
                                   sel4[0:4, 128 * j:128 * j + 128],
                                   scT[0:4, 0:128], start=(j == 0), stop=(j == 3),
                                   skip_group_check=True)
                ins.then_inc(s_t, 1)
                cnt.bump("s_t")
                cnt.v[f"s2P_pbc_{c}"] = cnt.get("s_t")

            @blk.scalar
            def _(e):
                if tail:
                    for tt in range(4):
                        e.wait_ge(s_v, cnt.get(f"s2P_bn2_{cp}_{tt}"))
                        if cp >= 1:
                            e.wait_ge(s_v, cnt.get(f"s2P_ln_{cp - 1}_{tt}"))
                        e.activation(ln2c[:, tt, 0:1], mvc[:, tt, 1:2], AF.Sqrt,
                                     bias=epscol[:]).then_inc(s_a, 1)
                        cnt.bump("s_a")
                        cnt.v[f"s2P_sqrt_{cp}_{tt}"] = cnt.get("s_a")
                if head:
                    e.wait_ge(s_t, t_base + 32 + F2 + 2)
                    e.copy(scT[:], psM[0:4, 0:128]).then_inc(s_a, 1)
                    cnt.bump("s_a")
                    e.wait_ge(s_t, t_base + 32 + F2 + 3)
                    if c >= 1:
                        e.wait_ge(s_v, cnt.get(f"s2P_qlast_{c - 1}"))
                    e.copy(bscale[:], psM[:, 0:512]).then_inc(s_a, 1)
                    cnt.bump("s_a")
                    cnt.v[f"s2P_bsc_{c}"] = cnt.get("s_a")

            @blk.vector
            def _(e):
                if not tail:
                    return
                for tt in range(4):
                    e.wait_ge(s_a, cnt.get(f"s2P_sqrt_{cp}_{tt}"))
                    e.reciprocal(ln2c[:, tt, 1:2], ln2c[:, tt, 0:1]).then_inc(s_v, 1)
                    e.wait_ge(s_v, cnt.bump("s_v"))
                    e.scalar_tensor_tensor(ln2c[:, tt, 2:3], mvc[:, tt, 0:1], -1.0,
                                           ln2c[:, tt, 1:2], ALU.mult, ALU.mult
                                           ).then_inc(s_v, 1)
                    e.wait_ge(s_v, cnt.bump("s_v"))
                    e.tensor_scalar(zz[:, tt, :], zz[:, tt, :],
                                    ln2c[:, tt, 1:2], ln2c[:, tt, 2:3], ALU.mult,
                                    ALU.add).then_inc(s_v, 1)
                    cnt.bump("s_v")
                    cnt.v[f"s2P_ln_{cp}_{tt}"] = cnt.get("s_v")

            @blk.sync
            def _(e):
                if tail:
                    for tt in range(4):
                        e.wait_ge(s_v, cnt.get(f"s2P_ln_{cp}_{tt}"))
                        r0 = 512 * cp + 128 * tt
                        e.dma_start(io["x2"][r0:r0 + 128, :], zz[:, tt, :]
                                    ).then_inc(s_x2[tt % 2], 16)
                        cnt.bump(f"s2x2{tt % 2}", 16)
                if head:
                    if tail:
                        e.wait_ge(s_x2[0], cnt.get("s2x20"))
                        e.wait_ge(s_x2[1], cnt.get("s2x21"))
                    e.dma_start(
                        zz[:],
                        io["zln"].rearrange("(a p) j -> p a j", p=128)[:, 4 * c:4 * c + 4, :],
                    ).then_inc(s_zz, 16)
                    cnt.bump("s2zz", 16)
                    cnt.v[f"s2P_zz_{c}"] = cnt.get("s2zz")

        @blk.sync
        def _(e):
            e.wait_ge(s_x2[0], cnt.get("s2x20"))
            e.wait_ge(s_x2[1], cnt.get("s2x21"))
            e.wait_ge(s_zz, cnt.get("s2zz"))
            e.wait_ge(s_xq, cnt.get("s2xq"))

        @blk.gpsimd
        def _(e):
            e.wait_ge(s_g, cnt.get("s_g"))
            for k in range(4):
                e.wait_ge(s_w1[k], cnt.get(f"s2w1_{k}"))



NKT3 = 32
KTG3 = [(4 * i, 4) for i in range(8)]
TOWN = 2048


def emit_l1_attn(nc, st, io, cnt):
    sb = lambda name, shape, dt: st.enter_context(nc.sbuf_tensor(name, shape, dt))
    sem = lambda name: st.enter_context(nc.semaphore(name))

    k_fm = sb("s3_kfm", [128, 8, 4096], BF16)
    v_tm = sb("s3_vtm", [128, 32, H, HS], BF16)
    bq_sb = sb("s3_bq", [128, 8], F32)
    bk_sb = sb("s3_bk", [128, 8], F32)
    bbo = sb("s3_bbo", [128, 1024], F32)
    identb = sb("s3_identb", [128, 128], BF16)
    onesf = sb("s3_onesf", [1, 128], F32)
    negc = sb("s3_negc", [128, 1], F32)
    epscol = sb("s3_eps", [128, 1], F32)

    s_d = sem("s3_d")
    s_g = sem("s3_g")
    s_t = sem("s3_t")
    s_v = sem("s3_v")
    s_a = sem("s3_a")
    s_e = sem("s3_e")
    s_x3 = [sem("s3_x3a"), sem("s3_x3b")]

    with nc.Block("s3_ld") as blk, \
         nc.sbuf_tensor("s3_brow", [1, 1024], F32) as brow, \
         nc.psum_tensor("s3_psb", [128, 1024], F32) as psb:
        @blk.gpsimd
        def _(e):
            e.dma_start(identb[:], io["c_ident"]).then_inc(s_g, 16)
            e.wait_ge(s_g, cnt.bump("s_g", 16))

        @blk.sync
        def _(e):
            e.dma_start(bq_sb[:], io["bq8"]).then_inc(s_d, 16)
            e.dma_start(bk_sb[:], io["bk8"]).then_inc(s_d, 16)
            e.dma_start(brow[0:1, 0:1024], io["bo"][None, :]).then_inc(s_d, 16)
            e.dma_start(onesf[:], io["c_ones"][0:1, :]).then_inc(s_d, 16)
            e.wait_ge(s_d, cnt.bump("s_d", 64))

        @blk.vector
        def _(e):
            e.memset(negc[:], -C_SHIFT)
            e.memset(epscol[:], 1e-5)
            e.memset(v_tm[:, :, :, 64:65], 1.0)

        @blk.tensor
        def _(e):
            e.wait_ge(s_d, cnt.get("s_d"))
            e.matmul(psb[:, 0:512], onesf[:], brow[0:1, 0:512], start=True, stop=True)
            e.matmul(psb[:, 512:1024], onesf[:], brow[0:1, 512:1024],
                     start=True, stop=True).then_inc(s_t, 1)
            cnt.bump("s_t")
            cnt.v["s3_bo1"] = cnt.get("s_t")

        @blk.scalar
        def _(e):
            e.wait_ge(s_t, cnt.get("s3_bo1"))
            e.copy(bbo[:], psb[:]).then_inc(s_a, 1)
            cnt.bump("s_a")

    with nc.Block("s3_kv") as blk, \
         nc.sbuf_tensor("s3_wk", [128, 8, 1024], BF16) as wk_sb, \
         nc.sbuf_tensor("s3_wv", [128, 8, 1024], BF16) as wv_sb, \
         nc.sbuf_tensor("s3_bbv", [128, 1024], F32) as bbv, \
         nc.sbuf_tensor("s3_brow2", [1, 1024], F32) as brow, \
         nc.sbuf_tensor("s3_xtm", [128, 2, 4, 1024], BF16) as xtm, \
         nc.sbuf_tensor("s3_xfm", [128, 2, 8, 512], BF16) as xfm, \
         nc.psum_tensor("s3_ptr", [128, 1024], BF16) as ptr, \
         nc.psum_tensor("s3_pp0", [128, 512], F32) as pp0, \
         nc.psum_tensor("s3_pp1", [128, 512], F32) as pp1:
        pps = [pp0, pp1]

        @blk.sync
        def _(e):
            e.dma_start(brow[0:1, 0:1024], io["bv"][None, :]).then_inc(s_d, 16)
            e.wait_ge(s_d, cnt.bump("s_d", 16))
            cnt.v["s3_bvrow"] = cnt.get("s_d")

        @blk.tensor
        def _(e):
            e.wait_ge(s_d, cnt.get("s3_bvrow"))
            e.matmul(pp0[:], onesf[:], brow[0:1, 0:512], start=True, stop=True)
            e.matmul(pp1[:], onesf[:], brow[0:1, 512:1024],
                     start=True, stop=True).then_inc(s_t, 1)
            cnt.bump("s_t")
            cnt.v["s3_bv1"] = cnt.get("s_t")

        @blk.scalar
        def _(e):
            e.wait_ge(s_t, cnt.get("s3_bv1"))
            e.copy(bbv[:, 0:512], pp0[:])
            e.copy(bbv[:, 512:1024], pp1[:]).then_inc(s_a, 1)
            cnt.bump("s_a")
            cnt.v["s3_bv2"] = cnt.get("s_a")

        t_kv = cnt.get("s_t")

        @blk.gpsimd
        def _(e):
            e.dma_start(wk_sb[:], io["wkt"].rearrange("(c p) j -> p c j", p=128)
                        ).then_inc(s_g, 16)
            e.dma_start(wv_sb[:], io["wvt"].rearrange("(c p) j -> p c j", p=128)
                        ).then_inc(s_g, 16)
            e.wait_ge(s_g, cnt.bump("s_g", 32))
            for cch in range(8):
                if cch >= 2:
                    e.wait_ge(s_t, t_kv + 20 * (cch - 2) + 4)
                e.dma_start(
                    xtm[:, cch % 2, :, :],
                    io["x2"].rearrange("(a p) j -> p a j", p=128)[:, 4 * cch:4 * cch + 4, :],
                ).then_inc(s_g, 16)
                e.wait_ge(s_g, cnt.bump("s_g", 16))
                cnt.v[f"s3_xld_{cch}"] = cnt.get("s_g")

        for cch in range(8):
            par = cch % 2

            @blk.tensor
            def _(e, cch=cch, par=par):
                e.wait_ge(s_g, cnt.get(f"s3_xld_{cch}"))
                v_prek = cnt.get("s_v")
                for tl in range(4):
                    if tl >= 1:
                        e.wait_ge(s_v, v_prek + tl)
                    elif cch >= 1:
                        e.wait_ge(s_v, cnt.get(f"s3_xev_{cch - 1}"))
                    ins = None
                    for f in range(8):
                        ins = e.transpose(ptr[:, 128 * f:128 * f + 128],
                                          xtm[:, par, tl, 128 * f:128 * f + 128],
                                          identb[:])
                    ins.then_inc(s_t, 1)
                    cnt.bump("s_t")
                    cnt.v[f"s3_xtr1_{cch}_{tl}"] = cnt.get("s_t")
                cnt.v[f"s3_xtr_{cch}"] = cnt.get("s_t")


            @blk.vector
            def _(e, cch=cch, par=par):
                for tl in range(4):
                    e.wait_ge(s_t, cnt.get(f"s3_xtr1_{cch}_{tl}"))
                    e.tensor_copy(
                        xfm[:, par, :, 128 * tl:128 * tl + 128],
                        ptr[:].rearrange("p (a b) -> p a b", a=8),
                    ).then_inc(s_v, 1)
                    cnt.bump("s_v")
                cnt.v[f"s3_xev_{cch}"] = cnt.get("s_v")

            vb_k = cnt.get("s_v")

            @blk.tensor
            def _(e, cch=cch, par=par, vb_k=vb_k):
                e.wait_ge(s_v, cnt.get(f"s3_xev_{cch}"))
                for m in range(8):
                    pi = m % 2
                    if m >= 2:
                        e.wait_ge(s_v, vb_k + m - 1)
                    elif cch >= 1:
                        e.wait_ge(s_v, cnt.get(f"s3_vev_{cch - 1}_{m + 6}"))
                    else:
                        e.wait_ge(s_a, cnt.get("s3_bv2"))
                    ins = None
                    for kc in range(8):
                        ins = e.matmul(pps[pi][:], wk_sb[:, kc, 128 * m:128 * m + 128],
                                       xfm[:, par, kc, :], start=(kc == 0), stop=(kc == 7))
                    ins.then_inc(s_t, 1)
                    cnt.bump("s_t")
                    cnt.v[f"s3_kp_{cch}_{m}"] = cnt.get("s_t")

            @blk.vector
            def _(e, cch=cch, par=par):
                for m in range(8):
                    e.wait_ge(s_t, cnt.get(f"s3_kp_{cch}_{m}"))
                    e.tensor_scalar(k_fm[:, m, 512 * cch:512 * cch + 512],
                                    pps[m % 2][:], bk_sb[:, m:m + 1], None,
                                    ALU.add).then_inc(s_v, 1)
                    cnt.bump("s_v")
                    cnt.v[f"s3_kev_{cch}_{m}"] = cnt.get("s_v")

            @blk.tensor
            def _(e, cch=cch, par=par, vb_k=vb_k):
                for tl in range(4):
                    for half in range(2):
                        j = 2 * tl + half
                        pi = j % 2
                        if j >= 2:
                            e.wait_ge(s_v, vb_k + 8 + j - 1)
                        else:
                            e.wait_ge(s_v, vb_k + 6 + j + 1)
                        ins = None
                        for kc in range(8):
                            ins = e.matmul(
                                pps[pi][:],
                                xfm[:, par, kc, 128 * tl:128 * tl + 128],
                                wv_sb[:, kc, 512 * half:512 * half + 512],
                                start=(kc == 0), stop=(kc == 7))
                        ins.then_inc(s_t, 1)
                        cnt.bump("s_t")
                        cnt.v[f"s3_vp_{cch}_{j}"] = cnt.get("s_t")

            @blk.vector
            def _(e, cch=cch, par=par):
                for tl in range(4):
                    for half in range(2):
                        j = 2 * tl + half
                        e.wait_ge(s_t, cnt.get(f"s3_vp_{cch}_{j}"))
                        e.tensor_tensor(
                            v_tm[:, 4 * cch + tl, 8 * half:8 * half + 8, 0:64],
                            pps[j % 2][:].rearrange("p (h c) -> p h c", h=8),
                            bbv[:, 512 * half:512 * half + 512]
                            .rearrange("p (h c) -> p h c", h=8),
                            ALU.add).then_inc(s_v, 1)
                        cnt.bump("s_v")
                        cnt.v[f"s3_vev_{cch}_{j}"] = cnt.get("s_v")


    wq_sb = sb("s3_wq2", [128, 8, 1024], BF16)
    wo_sb = sb("s3_wo2", [128, 8, 1024], BF16)
    with nc.Block("s3_ld2") as blk:
        @blk.gpsimd
        def _(e):
            e.dma_start(wq_sb[:], io["wqt"].rearrange("(c p) j -> p c j", p=128)
                        ).then_inc(s_g, 16)
            e.dma_start(wo_sb[:], io["wot"].rearrange("(c p) j -> p c j", p=128)
                        ).then_inc(s_g, 16)
            e.wait_ge(s_g, cnt.bump("s_g", 32))

    for qc in range(4):
        q0 = 512 * qc
        with ExitStack() as stc:
            sbc = lambda nm, shp, dt: stc.enter_context(nc.sbuf_tensor(nm, shp, dt))
            qfm = sbc(f"s3_qfm_{qc}", [128, 8, 512], BF16)
            a_buf = sbc(f"s3_abuf_{qc}", [128, 2, 2, 512], BF16)
            o_tm = sbc(f"s3_otm_{qc}", [128, 4, 1024], BF16)
            o_fm = sbc(f"s3_ofm_{qc}", [128, 8, 256], BF16)
            dcol = sbc(f"s3_dcol_{qc}", [128, 8], F32)
            rcol = sbc(f"s3_rcol_{qc}", [128, 8], F32)
            stA = stc.enter_context(ExitStack())
            blk = stA.enter_context(nc.Block(f"s3_prep_{qc}"))
            xqtm = stA.enter_context(nc.sbuf_tensor(f"s3_xqtm_{qc}", [128, 4, 1024], BF16))
            xqfm = stA.enter_context(nc.sbuf_tensor(f"s3_xqfm_{qc}", [128, 8, 512], BF16))
            ptr2 = stA.enter_context(nc.psum_tensor(f"s3_ptr2_{qc}", [128, 1024], BF16))
            psP = stA.enter_context(nc.psum_tensor(f"s3_psP_{qc}", [128, 1024], F32))

            @blk.gpsimd
            def _(e):
                e.dma_start(
                    xqtm[:],
                    io["x2q"].rearrange("(a p) j -> p a j", p=128)[:, 4 * qc:4 * qc + 4, :],
                ).then_inc(s_g, 16)
                e.wait_ge(s_g, cnt.bump("s_g", 16))
                cnt.v[f"s3_xqld_{qc}"] = cnt.get("s_g")

            v_pre3 = cnt.get("s_v")

            @blk.tensor
            def _(e):
                e.wait_ge(s_g, cnt.get(f"s3_xqld_{qc}"))
                ins = None
                for tl in range(4):
                    if tl >= 1:
                        e.wait_ge(s_v, v_pre3 + tl)
                    for f in range(8):
                        ins = e.transpose(ptr2[:, 128 * f:128 * f + 128],
                                          xqtm[:, tl, 128 * f:128 * f + 128],
                                          identb[:])
                    ins.then_inc(s_t, 1)
                    cnt.bump("s_t")
                    cnt.v[f"s3_qtr_{qc}_{tl}"] = cnt.get("s_t")

            @blk.vector
            def _(e):
                for tl in range(4):
                    e.wait_ge(s_t, cnt.get(f"s3_qtr_{qc}_{tl}"))
                    e.tensor_copy(
                        xqfm[:, :, 128 * tl:128 * tl + 128],
                        ptr2[:].rearrange("p (a b) -> p a b", a=8),
                    ).then_inc(s_v, 1)
                    cnt.bump("s_v")
                cnt.v[f"s3_qev_{qc}"] = cnt.get("s_v")

            qb_pe = cnt.get("s_v")

            @blk.tensor
            def _(e):
                e.wait_ge(s_v, cnt.get(f"s3_qev_{qc}"))
                for m in range(8):
                    pi = m % 2
                    if m >= 2:
                        e.wait_ge(s_v, qb_pe + m - 1)
                    ins = None
                    for kc in range(8):
                        ins = e.matmul(psP[:, 512 * pi:512 * pi + 512],
                                       wq_sb[:, kc, 128 * m:128 * m + 128],
                                       xqfm[:, kc, :], start=(kc == 0), stop=(kc == 7))
                    ins.then_inc(s_t, 1)
                    cnt.bump("s_t")
                    cnt.v[f"s3_qp_{qc}_{m}"] = cnt.get("s_t")

            @blk.vector
            def _(e):
                for m in range(8):
                    e.wait_ge(s_t, cnt.get(f"s3_qp_{qc}_{m}"))
                    e.tensor_scalar(qfm[:, m, :], psP[:, 512 * (m % 2):512 * (m % 2) + 512],
                                    bq_sb[:, m:m + 1], None, ALU.add).then_inc(s_v, 1)
                    cnt.bump("s_v")
                    cnt.v[f"s3_qpe_{qc}_{m}"] = cnt.get("s_v")

            stA.close()
            x2own = sbc(f"s3_x2own_{qc}", [128, 2, 1024], F32)
            x3p = sbc(f"s3_x3p_{qc}", [128, 2, 1024], F32)
            st6 = sbc(f"s3_st6_{qc}", [128, 12], F32)
            mv = sbc(f"s3_mv_{qc}", [128, 2], F32)
            lnc = sbc(f"s3_lnc_{qc}", [128, 4], F32)
            stB = stc.enter_context(ExitStack())
            blk = stB.enter_context(nc.Block(f"s3_attn_{qc}"))
            psS = [stB.enter_context(nc.psum_tensor(f"s3_psS{i}_{qc}", [128, 1024], F32))
                   for i in range(2)]
            psO = stB.enter_context(nc.psum_tensor(f"s3_psO_{qc}", [128, 2048], F32))

            @blk.sync
            def _(e):
                for tl in range(2):
                    e.dma_start(
                        x2own[:, tl, :],
                        io["x2q"][512 * qc + 128 * tl:512 * qc + 128 * tl + 128, :],
                    ).then_inc(s_d, 16)
                    e.wait_ge(s_d, cnt.bump("s_d", 16))
                    cnt.v[f"s3_x2own_{qc}_{tl}"] = cnt.get("s_d")

            t_base = cnt.get("s_t")
            a_base = cnt.get("s_a")
            e_base = cnt.get("s_e")
            NID = 8 * 32

            def t_after_qk(idx):
                return t_base + (1 if idx == 0 else 2 * idx)

            def t_after_av(idx):
                return t_base + (2 * idx + 3 if idx < NID - 1 else 2 * NID)

            @blk.tensor
            def _(e):
                e.wait_ge(s_v, cnt.get(f"s3_qpe_{qc}_7"))

                def emit_qk(idx):
                    hp, kt = divmod(idx, 32)
                    par = idx % 2
                    if idx >= 2:
                        e.wait_ge(s_a, a_base + idx - 1)
                    ins = None
                    for hh in range(2):
                        r0 = 64 * hh
                        ins = e.matmul(
                            psS[par][:, 512 * hh:512 * hh + 512],
                            k_fm[r0:r0 + DH, hp, 128 * kt:128 * kt + 128],
                            qfm[r0:r0 + DH, hp, :],
                            start=True, stop=True,
                        )
                    ins.then_inc(s_t, 1)
                    cnt.bump("s_t")

                def emit_av(idx):
                    hp, kt = divmod(idx, 32)
                    par = idx % 2
                    e.wait_ge(s_a, a_base + idx + 1)
                    if kt == 0 and hp > 0:
                        e.wait_ge(s_e, e_base + hp)
                    ins = None
                    for hh in range(2):
                        for j in range(4):
                            ins = e.matmul(
                                psO[0:128, 512 * j + 65 * hh:512 * j + 65 * hh + HS],
                                a_buf[:, par, hh, 128 * j:128 * j + 128],
                                v_tm[:, kt, 2 * hp + hh, :],
                                start=(kt == 0 and hh == 0),
                                stop=(kt == 31 and hh == 1),
                                skip_group_check=True,
                            )
                    ins.then_inc(s_t, 1)
                    cnt.bump("s_t")

                emit_qk(0)
                for idx in range(1, NID):
                    emit_qk(idx)
                    emit_av(idx - 1)
                emit_av(NID - 1)

            @blk.scalar
            def _(e):
                for idx in range(NID):
                    par = idx % 2
                    e.wait_ge(s_t, t_after_qk(idx))
                    e.activation(
                        a_buf[:, par, :, :].rearrange("p a b -> p (a b)"),
                        psS[par][:], AF.Exp, bias=negc[:], scale=0.125,
                    ).then_inc(s_a, 1)
                    cnt.bump("s_a")

            @blk.vector
            def _(e):
                for hp in range(8):
                    e.wait_ge(s_t, t_after_av(hp * 32 + 31))
                    ins = None
                    for k8 in range(8):
                        hh, j = divmod(k8, 4)
                        ins = e.tensor_copy(
                            dcol[:, k8:k8 + 1],
                            psO[:, 512 * j + 65 * hh + 64:512 * j + 65 * hh + 65])
                    ins.then_inc(s_v, 1)
                    e.wait_ge(s_v, cnt.bump("s_v"))
                    ins = None
                    for k8 in range(8):
                        ins = e.reciprocal(rcol[:, k8:k8 + 1], dcol[:, k8:k8 + 1])
                    ins.then_inc(s_v, 1)
                    e.wait_ge(s_v, cnt.bump("s_v"))
                    ins = None
                    for k8 in range(8):
                        hh, j = divmod(k8, 4)
                        ins = e.tensor_scalar_mul(
                            o_tm[:, j, 64 * (2 * hp + hh):64 * (2 * hp + hh) + 64],
                            psO[:, 512 * j + 65 * hh:512 * j + 65 * hh + 64],
                            rcol[:, k8:k8 + 1])
                    ins.then_inc(s_e, 1)
                    cnt.bump("s_e")
                cnt.v[f"s3_oall_{qc}"] = cnt.get("s_e")

            stB.close()
            blk = stc.enter_context(nc.Block(f"s3_out_{qc}"))
            ptr2 = stc.enter_context(nc.psum_tensor(f"s3_ptr3_{qc}", [128, 1024], BF16))
            psP = stc.enter_context(nc.psum_tensor(f"s3_psP2_{qc}", [128, 1024], F32))

            for tl in range(4):
                par3 = tl % 2

                @blk.tensor
                def _(e, tl=tl, par3=par3):
                    if tl == 0:
                        e.wait_ge(s_e, cnt.get(f"s3_oall_{qc}"))
                    else:
                        e.wait_ge(s_v, cnt.get(f"s3_ofme_{qc}_{tl - 1}"))
                    if tl >= 2:
                        e.wait_ge(s_t, cnt.get(f"s3_op_{qc}_{tl - 2}"))
                    ins = None
                    for f in range(8):
                        ins = e.transpose(ptr2[:, 128 * f:128 * f + 128],
                                          o_tm[:, tl, 128 * f:128 * f + 128],
                                          identb[:])
                    ins.then_inc(s_t, 1)
                    cnt.bump("s_t")
                    cnt.v[f"s3_otr_{qc}_{tl}"] = cnt.get("s_t")

                @blk.vector
                def _(e, tl=tl, par3=par3):
                    e.wait_ge(s_t, cnt.get(f"s3_otr_{qc}_{tl}"))
                    e.tensor_copy(
                        o_fm[:, :, 128 * par3:128 * par3 + 128],
                        ptr2[:].rearrange("p (a b) -> p a b", a=8),
                    ).then_inc(s_v, 1)
                    cnt.bump("s_v")
                    cnt.v[f"s3_ofme_{qc}_{tl}"] = cnt.get("s_v")

                @blk.tensor
                def _(e, tl=tl, par3=par3):
                    e.wait_ge(s_v, cnt.get(f"s3_ofme_{qc}_{tl}"))
                    if tl >= 1:
                        e.wait_ge(s_v, cnt.get(f"s3_opev_{qc}_{tl - 1}"))
                    ins = None
                    for half in range(2):
                        for kc in range(8):
                            ins = e.matmul(psP[:, 512 * half:512 * half + 512],
                                           o_fm[:, kc, 128 * par3:128 * par3 + 128],
                                           wo_sb[:, kc, 512 * half:512 * half + 512],
                                           start=(kc == 0), stop=(kc == 7))
                    ins.then_inc(s_t, 1)
                    cnt.bump("s_t")
                    cnt.v[f"s3_op_{qc}_{tl}"] = cnt.get("s_t")

                @blk.vector
                def _(e, tl=tl, par3=par3):
                    e.wait_ge(s_t, cnt.get(f"s3_op_{qc}_{tl}"))
                    e.wait_ge(s_d, cnt.get(f"s3_x2own_{qc}_{tl}"))
                    if tl >= 2:
                        e.wait_ge(s_x3[tl % 2], cnt.get(f"s3_x3st_{qc}_{tl - 2}"))
                    e.tensor_add(x3p[:, par3, :], psP[:], bbo[:]).then_inc(s_v, 1)
                    e.wait_ge(s_v, cnt.bump("s_v"))
                    e.tensor_add(x3p[:, par3, :], x3p[:, par3, :],
                                 x2own[:, tl % 2, :]).then_inc(s_v, 1)
                    cnt.bump("s_v")
                    cnt.v[f"s3_opev_{qc}_{tl}"] = cnt.get("s_v")
                    e.wait_ge(s_v, cnt.get("s_v"))
                    e.bn_stats(st6[:, 0:6], x3p[:, par3, 0:512])
                    e.bn_stats(st6[:, 6:12], x3p[:, par3, 512:1024]).then_inc(s_v, 1)
                    e.wait_ge(s_v, cnt.bump("s_v"))
                    e.bn_aggr(mv[:], st6[:]).then_inc(s_v, 1)
                    cnt.bump("s_v")
                    cnt.v[f"s3_bn_{qc}_{tl}"] = cnt.get("s_v")

                @blk.scalar
                def _(e, tl=tl):
                    e.wait_ge(s_v, cnt.get(f"s3_bn_{qc}_{tl}"))
                    e.activation(lnc[:, 0:1], mv[:, 1:2], AF.Sqrt, bias=epscol[:]
                                 ).then_inc(s_a, 1)
                    cnt.bump("s_a")
                    cnt.v[f"s3_sd_{qc}_{tl}"] = cnt.get("s_a")

                @blk.vector
                def _(e, tl=tl, par3=par3):
                    e.wait_ge(s_a, cnt.get(f"s3_sd_{qc}_{tl}"))
                    e.reciprocal(lnc[:, 1:2], lnc[:, 0:1]).then_inc(s_v, 1)
                    e.wait_ge(s_v, cnt.bump("s_v"))
                    e.scalar_tensor_tensor(lnc[:, 2:3], mv[:, 0:1], -1.0,
                                           lnc[:, 1:2], ALU.mult, ALU.mult
                                           ).then_inc(s_v, 1)
                    e.wait_ge(s_v, cnt.bump("s_v"))
                    e.tensor_scalar(x3p[:, par3, :], x3p[:, par3, :], lnc[:, 1:2],
                                    lnc[:, 2:3], ALU.mult, ALU.add).then_inc(s_v, 1)
                    cnt.bump("s_v")
                    cnt.v[f"s3_x3n_{qc}_{tl}"] = cnt.get("s_v")

                @blk.sync
                def _(e, tl=tl, par3=par3):
                    e.wait_ge(s_v, cnt.get(f"s3_x3n_{qc}_{tl}"))
                    r0 = 512 * qc + 128 * tl
                    e.dma_start(io["x3"][r0:r0 + 128, :], x3p[:, par3, :]
                                ).then_inc(s_x3[tl % 2], 16)
                    cnt.bump(f"s3x3{tl % 2}", 16)
                    cnt.v[f"s3_x3st_{qc}_{tl}"] = cnt.get(f"s3x3{tl % 2}")
                    e.wait_ge(s_x3[tl % 2], cnt.get(f"s3x3{tl % 2}"))
                    if tl + 2 <= 3:
                        t2 = tl + 2
                        e.dma_start(
                            x2own[:, t2 % 2, :],
                            io["x2q"][512 * qc + 128 * t2:512 * qc + 128 * t2 + 128, :],
                        ).then_inc(s_d, 16)
                        e.wait_ge(s_d, cnt.bump("s_d", 16))
                        cnt.v[f"s3_x2own_{qc}_{t2}"] = cnt.get("s_d")



def emit_l1_ffn(nc, st, io, cnt):
    sb = lambda name, shape, dt: st.enter_context(nc.sbuf_tensor(name, shape, dt))
    sem = lambda name: st.enter_context(nc.semaphore(name))

    wg_sb = sb("s4_wg", [128, 8, 3072], BF16)
    wu_sb = sb("s4_wu", [128, 8, 3072], BF16)
    wd_sb = sb("s4_wd", [128, 24, 1024], BF16)
    identb = sb("s4_identb", [128, 128], BF16)
    epscol = sb("s4_eps", [128, 1], F32)

    s_d = sem("s4_d")
    s_g = sem("s4_g")
    s_t = sem("s4_t")
    s_v = sem("s4_v")
    s_a = sem("s4_a")
    s_y = [sem("s4_ya"), sem("s4_yb")]

    with nc.Block("s4_ld") as blk:
        @blk.gpsimd
        def _(e):
            e.dma_start(wg_sb[:], io["wgt"].rearrange("(c p) j -> p c j", p=128)
                        ).then_inc(s_g, 16)
            e.dma_start(wu_sb[:], io["wut"].rearrange("(c p) j -> p c j", p=128)
                        ).then_inc(s_g, 16)
            e.dma_start(wd_sb[:], io["wdt"].rearrange("(c p) j -> p c j", p=128)
                        ).then_inc(s_g, 16)
            e.dma_start(identb[:], io["c_ident"]).then_inc(s_g, 16)
            e.wait_ge(s_g, cnt.bump("s_g", 64))

        @blk.vector
        def _(e):
            e.memset(epscol[:], 1e-5)

    DSC = 1.0 / (127.0 * 127.0)

    for c in range(4):
        t0 = 512 * c
        with ExitStack() as stc:
            blk = stc.enter_context(nc.Block(f"s4_ffn_{c}"))
            sbc = lambda nm, shp, dt: stc.enter_context(nc.sbuf_tensor(nm, shp, dt))
            pscm = lambda nm, shp, dt: stc.enter_context(nc.psum_tensor(nm, shp, dt))
            x3tm = sbc(f"s4_x3tm_{c}", [128, 2, 1024], F32)
            x3bf = sbc(f"s4_x3bf_{c}", [128, 4, 1024], BF16)
            x3fm = sbc(f"s4_x3fm_{c}", [128, 8, 512], BF16)
            gs = sbc(f"s4_gs_{c}", [128, 2, 512], F32)
            h2 = sbc(f"s4_h2_{c}", [128, 24, 512], BF16)
            yp = sbc(f"s4_yp_{c}", [128, 2, 1024], F32)
            st6 = sbc(f"s4_st6_{c}", [128, 12], F32)
            mv = sbc(f"s4_mv_{c}", [128, 2], F32)
            lnc = sbc(f"s4_lnc_{c}", [128, 4], F32)
            ptr = pscm(f"s4_ptr_{c}", [128, 1024], BF16)
            pg0 = pscm(f"s4_pg0_{c}", [128, 512], F32)
            pg1 = pscm(f"s4_pg1_{c}", [128, 512], F32)
            pu0 = pscm(f"s4_pu0_{c}", [128, 512], F32)
            pu1 = pscm(f"s4_pu1_{c}", [128, 512], F32)
            pd = pscm(f"s4_pd_{c}", [128, 1024], F32)
            pgs = [pg0, pg1]
            pus = [pu0, pu1]

            @blk.sync
            def _(e):
                for tl in range(2):
                    e.dma_start(
                        x3tm[:, tl, :],
                        io["x3"][512 * c + 128 * tl:512 * c + 128 * tl + 128, :],
                    ).then_inc(s_d, 16)
                    e.wait_ge(s_d, cnt.bump("s_d", 16))
                    cnt.v[f"s4_xld_{c}_{tl}"] = cnt.get("s_d")

            @blk.gpsimd
            def _(e):
                e.dma_start(
                    x3bf[:],
                    io["x3"].rearrange("(a p) j -> p a j", p=128)[:, 4 * c:4 * c + 4, :],
                ).then_inc(s_g, 16)
                e.wait_ge(s_g, cnt.bump("s_g", 16))
                cnt.v[f"s4_xbf_{c}"] = cnt.get("s_g")

            v_pre = cnt.get("s_v")

            @blk.tensor
            def _(e):
                e.wait_ge(s_g, cnt.get(f"s4_xbf_{c}"))
                ins = None
                for tl in range(4):
                    if tl >= 1:
                        e.wait_ge(s_v, v_pre + tl)
                    for f in range(8):
                        ins = e.transpose(ptr[:, 128 * f:128 * f + 128],
                                          x3bf[:, tl, 128 * f:128 * f + 128],
                                          identb[:])
                    ins.then_inc(s_t, 1)
                    cnt.bump("s_t")
                    cnt.v[f"s4_tr_{c}_{tl}"] = cnt.get("s_t")

            @blk.vector
            def _(e):
                for tl in range(4):
                    e.wait_ge(s_t, cnt.get(f"s4_tr_{c}_{tl}"))
                    e.tensor_copy(
                        x3fm[:, :, 128 * tl:128 * tl + 128],
                        ptr[:].rearrange("p (a b) -> p a b", a=8),
                    ).then_inc(s_v, 1)
                    cnt.bump("s_v")
                cnt.v[f"s4_fm_{c}"] = cnt.get("s_v")

            vb_h2 = cnt.get("s_v")

            def h2mark(m):
                return vb_h2 + m + 1

            @blk.tensor
            def _(e):
                e.wait_ge(s_v, cnt.get(f"s4_fm_{c}"))
                for m in range(24):
                    pi = m % 2
                    if m >= 2:
                        e.wait_ge(s_v, h2mark(m - 2))
                    for kc in range(8):
                        e.matmul(pgs[pi][:], wg_sb[:, kc, 128 * m:128 * m + 128],
                                 x3fm[:, kc, :], start=(kc == 0), stop=(kc == 7))
                    ins = None
                    for kc in range(8):
                        ins = e.matmul(pus[pi][:], wu_sb[:, kc, 128 * m:128 * m + 128],
                                       x3fm[:, kc, :], start=(kc == 0), stop=(kc == 7))
                    ins.then_inc(s_t, 1)
                    cnt.bump("s_t")
                    cnt.v[f"s4_gu_{c}_{m}"] = cnt.get("s_t")

            @blk.scalar
            def _(e):
                for m in range(24):
                    e.wait_ge(s_t, cnt.get(f"s4_gu_{c}_{m}"))
                    if m >= 2:
                        e.wait_ge(s_v, h2mark(m - 2))
                    e.activation(gs[:, m % 2, :], pgs[m % 2][:], AF.Silu,
                                 scale=1.0 / 127.0).then_inc(s_a, 1)
                    cnt.bump("s_a")
                    cnt.v[f"s4_sil_{c}_{m}"] = cnt.get("s_a")

            @blk.vector
            def _(e):
                for m in range(24):
                    e.wait_ge(s_a, cnt.get(f"s4_sil_{c}_{m}"))
                    e.tensor_mul(h2[:, m, :], gs[:, m % 2, :], pus[m % 2][:]
                                 ).then_inc(s_v, 1)
                    cnt.bump("s_v")
                    cnt.v[f"s4_h2_{c}_{m}"] = cnt.get("s_v")

            for tl in range(4):
                par4 = tl % 2

                @blk.tensor
                def _(e, tl=tl):
                    if tl == 0:
                        e.wait_ge(s_v, cnt.get(f"s4_h2_{c}_23"))
                    else:
                        e.wait_ge(s_v, cnt.get(f"s4_ev_{c}_{tl - 1}"))
                    ins = None
                    for half in range(2):
                        for k in range(24):
                            ins = e.matmul(pd[:, 512 * half:512 * half + 512],
                                           h2[:, k, 128 * tl:128 * tl + 128],
                                           wd_sb[:, k, 512 * half:512 * half + 512],
                                           start=(k == 0), stop=(k == 23))
                    ins.then_inc(s_t, 1)
                    cnt.bump("s_t")
                    cnt.v[f"s4_dp_{c}_{tl}"] = cnt.get("s_t")

                @blk.vector
                def _(e, tl=tl, par4=par4):
                    e.wait_ge(s_t, cnt.get(f"s4_dp_{c}_{tl}"))
                    if tl >= 2:
                        e.wait_ge(s_y[tl % 2], cnt.get(f"s4_yst_{c}_{tl - 2}"))
                    e.wait_ge(s_d, cnt.get(f"s4_xld_{c}_{tl}"))
                    e.scalar_tensor_tensor(yp[:, par4, 0:512], pd[:, 0:512], DSC,
                                           x3tm[:, tl % 2, 0:512], ALU.mult, ALU.add)
                    e.scalar_tensor_tensor(yp[:, par4, 512:1024], pd[:, 512:1024], DSC,
                                           x3tm[:, tl % 2, 512:1024], ALU.mult, ALU.add
                                           ).then_inc(s_v, 1)
                    cnt.bump("s_v")
                    cnt.v[f"s4_ev_{c}_{tl}"] = cnt.get("s_v")
                    e.wait_ge(s_v, cnt.get("s_v"))
                    e.bn_stats(st6[:, 0:6], yp[:, par4, 0:512])
                    e.bn_stats(st6[:, 6:12], yp[:, par4, 512:1024]).then_inc(s_v, 1)
                    e.wait_ge(s_v, cnt.bump("s_v"))
                    e.bn_aggr(mv[:], st6[:]).then_inc(s_v, 1)
                    cnt.bump("s_v")
                    cnt.v[f"s4_bn_{c}_{tl}"] = cnt.get("s_v")

                @blk.scalar
                def _(e, tl=tl):
                    e.wait_ge(s_v, cnt.get(f"s4_bn_{c}_{tl}"))
                    e.activation(lnc[:, 0:1], mv[:, 1:2], AF.Sqrt, bias=epscol[:]
                                 ).then_inc(s_a, 1)
                    cnt.bump("s_a")
                    cnt.v[f"s4_sd_{c}_{tl}"] = cnt.get("s_a")

                @blk.vector
                def _(e, tl=tl, par4=par4):
                    e.wait_ge(s_a, cnt.get(f"s4_sd_{c}_{tl}"))
                    e.reciprocal(lnc[:, 1:2], lnc[:, 0:1]).then_inc(s_v, 1)
                    e.wait_ge(s_v, cnt.bump("s_v"))
                    e.scalar_tensor_tensor(lnc[:, 2:3], mv[:, 0:1], -1.0,
                                           lnc[:, 1:2], ALU.mult, ALU.mult
                                           ).then_inc(s_v, 1)
                    e.wait_ge(s_v, cnt.bump("s_v"))
                    e.tensor_scalar(yp[:, par4, :], yp[:, par4, :], lnc[:, 1:2],
                                    lnc[:, 2:3], ALU.mult, ALU.add).then_inc(s_v, 1)
                    cnt.bump("s_v")
                    cnt.v[f"s4_yn_{c}_{tl}"] = cnt.get("s_v")

                @blk.sync
                def _(e, tl=tl, par4=par4):
                    e.wait_ge(s_v, cnt.get(f"s4_yn_{c}_{tl}"))
                    r0 = 512 * c + 128 * tl
                    e.dma_start(io["y"][r0:r0 + 128, :], yp[:, par4, :]
                                ).then_inc(s_y[tl % 2], 16)
                    cnt.bump(f"s4y{tl % 2}", 16)
                    cnt.v[f"s4_yst_{c}_{tl}"] = cnt.get(f"s4y{tl % 2}")
                    e.wait_ge(s_y[tl % 2], cnt.get(f"s4y{tl % 2}"))
                    if tl + 2 <= 3:
                        t2 = tl + 2
                        e.dma_start(
                            x3tm[:, t2 % 2, :],
                            io["x3"][512 * c + 128 * t2:512 * c + 128 * t2 + 128, :],
                        ).then_inc(s_d, 16)
                        e.wait_ge(s_d, cnt.bump("s_d", 16))
                        cnt.v[f"s4_xld_{c}_{t2}"] = cnt.get("s_d")



def build_stage1():
    nc = bass.Bass("TRN2", target_bir_lowering=False)
    io = {}
    io["x"] = nc.dram_tensor("x", [L, D], F32, kind="ExternalInput").ap()
    io["wq"] = nc.dram_tensor("wq", [D, D], F32, kind="ExternalInput").ap()
    io["wk"] = nc.dram_tensor("wk", [D, D], F32, kind="ExternalInput").ap()
    io["wv"] = nc.dram_tensor("wv", [D, D], F32, kind="ExternalInput").ap()
    for nm, arr in host_consts().items():
        io[nm] = nc.dram_tensor(nm, list(arr.shape), F32, kind="ExternalInput").ap()
    io["attnout"] = nc.dram_tensor("attnout", [L, D], F32, kind="ExternalOutput").ap()
    if DEBUG_S1:
        io["dbg_k"] = nc.dram_tensor("dbg_k", [128, 8 * NKT * 128], F32, kind="ExternalOutput").ap()
        io["dbg_q"] = nc.dram_tensor("dbg_q", [128, 8 * NKT * 128], F32, kind="ExternalOutput").ap()
        io["dbg_v"] = nc.dram_tensor("dbg_v", [128, NKT * H * HS], F32, kind="ExternalOutput").ap()
        io["dbg_xfm"] = nc.dram_tensor("dbg_xfm", [128, 8 * NKT * 128], F32, kind="ExternalOutput").ap()
        io["dbg_den"] = nc.dram_tensor("dbg_den", [128, 64], F32, kind="ExternalOutput").ap()
        io["dbg_num"] = nc.dram_tensor("dbg_num", [128, 64], F32, kind="ExternalOutput").ap()
    cnt = Cnt()
    with ExitStack() as st:
        emit_l0_attn(nc, st, io, cnt)
    return nc


def build_stage2():
    nc = bass.Bass("TRN2", target_bir_lowering=False)
    io = {}
    io["x"] = nc.dram_tensor("x", [L, D], F32, kind="ExternalInput").ap()
    io["attnout"] = nc.dram_tensor("attnout", [L, D], F32, kind="ExternalInput").ap()
    io["w1t"] = nc.dram_tensor("w1t", [1024, 4096], F32, kind="ExternalInput").ap()
    io["w2t"] = nc.dram_tensor("w2t", [4096, 1024], F32, kind="ExternalInput").ap()
    io["cw"] = nc.dram_tensor("cw", [128, 96], F32, kind="ExternalInput").ap()
    io["cb"] = nc.dram_tensor("cb", [128, 32], F32, kind="ExternalInput").ap()
    for nm, arr in host_consts().items():
        io[nm] = nc.dram_tensor(nm, list(arr.shape), F32, kind="ExternalInput").ap()
    io["c_ones"] = nc.dram_tensor("c_ones", [128, 128], F32, kind="ExternalInput").ap()
    io["zln"] = nc.dram_tensor("zln", [L, D], F32, kind="Internal").ap()
    io["xqf"] = nc.dram_tensor("xqf", [1024, 4096], BF16, kind="Internal").ap()
    io["w1qd"] = nc.dram_tensor("w1qd", [32, 128, 1024], I8, kind="Internal").ap()
    io["scr"] = nc.dram_tensor("scr", [512], F32, kind="Internal").ap()
    io["x2"] = nc.dram_tensor("x2", [L, D], F32, kind="ExternalOutput").ap()
    cnt = Cnt()
    with ExitStack() as st:
        emit_l0_ffn(nc, st, io, cnt)
    return nc


def host_stage2_weights(inputs):
    w1 = inputs["L0_ffn_w1"]
    w2 = inputs["L0_ffn_w2"]
    cw = inputs["L0_ffn_conv_w"]
    cb = inputs["L0_ffn_conv_b"]
    return {
        "w1t": np.ascontiguousarray(w1.T),
        "w2t": np.ascontiguousarray(w2.T),
        "cw": np.ascontiguousarray(
            cw.reshape(32, 128, 3).transpose(1, 0, 2).reshape(128, 96)),
        "cb": np.ascontiguousarray(cb.reshape(32, 128).T),
        "c_ones": np.ones((128, 128), np.float32),
    }


def build_stage3():
    nc = bass.Bass("TRN2", target_bir_lowering=False)
    io = {}
    io["x2"] = nc.dram_tensor("x2", [L, D], F32, kind="ExternalInput").ap()
    io["x2q"] = nc.dram_tensor("x2q", [2048, D], F32, kind="ExternalInput").ap()
    for nm in ("wqt", "wkt", "wvt", "wot"):
        io[nm] = nc.dram_tensor(nm, [D, D], F32, kind="ExternalInput").ap()
    io["bq8"] = nc.dram_tensor("bq8", [128, 8], F32, kind="ExternalInput").ap()
    io["bk8"] = nc.dram_tensor("bk8", [128, 8], F32, kind="ExternalInput").ap()
    io["bv"] = nc.dram_tensor("bv", [D], F32, kind="ExternalInput").ap()
    io["bo"] = nc.dram_tensor("bo", [D], F32, kind="ExternalInput").ap()
    for nm, arr in host_consts().items():
        io[nm] = nc.dram_tensor(nm, list(arr.shape), F32, kind="ExternalInput").ap()
    io["c_ones"] = nc.dram_tensor("c_ones", [128, 128], F32, kind="ExternalInput").ap()
    io["x3"] = nc.dram_tensor("x3", [2048, D], F32, kind="ExternalOutput").ap()
    cnt = Cnt()
    with ExitStack() as st:
        emit_l1_attn(nc, st, io, cnt)
    return nc


def build_stage4():
    nc = bass.Bass("TRN2", target_bir_lowering=False)
    io = {}
    io["x3"] = nc.dram_tensor("x3", [2048, D], F32, kind="ExternalInput").ap()
    io["wgt"] = nc.dram_tensor("wgt", [1024, 3072], I32, kind="ExternalInput").ap()
    io["wut"] = nc.dram_tensor("wut", [1024, 3072], I32, kind="ExternalInput").ap()
    io["wdt"] = nc.dram_tensor("wdt", [3072, 1024], I32, kind="ExternalInput").ap()
    for nm, arr in host_consts().items():
        io[nm] = nc.dram_tensor(nm, list(arr.shape), F32, kind="ExternalInput").ap()
    io["y"] = nc.dram_tensor("y", [2048, D], F32, kind="ExternalOutput").ap()
    cnt = Cnt()
    with ExitStack() as st:
        emit_l1_ffn(nc, st, io, cnt)
    return nc


def build_fused():
    nc = bass.Bass("TRN2", target_bir_lowering=False)
    io = {}
    io["x"] = nc.dram_tensor("x", [L, D], F32, kind="ExternalInput").ap()
    io["wq"] = nc.dram_tensor("wq", [D, D], F32, kind="ExternalInput").ap()
    io["wk"] = nc.dram_tensor("wk", [D, D], F32, kind="ExternalInput").ap()
    io["wv"] = nc.dram_tensor("wv", [D, D], F32, kind="ExternalInput").ap()
    io["w1t"] = nc.dram_tensor("w1t", [1024, 4096], F32, kind="ExternalInput").ap()
    io["w2t"] = nc.dram_tensor("w2t", [4096, 1024], F32, kind="ExternalInput").ap()
    io["cw"] = nc.dram_tensor("cw", [128, 96], F32, kind="ExternalInput").ap()
    io["cb"] = nc.dram_tensor("cb", [128, 32], F32, kind="ExternalInput").ap()
    for nm in ("wqt", "wkt", "wvt", "wot"):
        io[nm] = nc.dram_tensor(nm, [D, D], F32, kind="ExternalInput").ap()
    io["bq8"] = nc.dram_tensor("bq8", [128, 8], F32, kind="ExternalInput").ap()
    io["bk8"] = nc.dram_tensor("bk8", [128, 8], F32, kind="ExternalInput").ap()
    io["bv"] = nc.dram_tensor("bv", [D], F32, kind="ExternalInput").ap()
    io["bo"] = nc.dram_tensor("bo", [D], F32, kind="ExternalInput").ap()
    io["wgt"] = nc.dram_tensor("wgt", [1024, 3072], I32, kind="ExternalInput").ap()
    io["wut"] = nc.dram_tensor("wut", [1024, 3072], I32, kind="ExternalInput").ap()
    io["wdt"] = nc.dram_tensor("wdt", [3072, 1024], I32, kind="ExternalInput").ap()
    for nm, arr in host_consts().items():
        io[nm] = nc.dram_tensor(nm, list(arr.shape), F32, kind="ExternalInput").ap()
    io["c_ones"] = nc.dram_tensor("c_ones", [128, 128], F32, kind="ExternalInput").ap()
    io["attnout"] = nc.dram_tensor("attnout", [L, D], F32, kind="Internal").ap()
    io["zln"] = nc.dram_tensor("zln", [L, D], F32, kind="Internal").ap()
    io["xqf"] = nc.dram_tensor("xqf", [1024, 4096], BF16, kind="Internal").ap()
    io["w1qd"] = nc.dram_tensor("w1qd", [32, 128, 1024], I8, kind="Internal").ap()
    io["scr"] = nc.dram_tensor("scr", [512], F32, kind="Internal").ap()
    io["x2"] = nc.dram_tensor("x2", [L, D], F32, kind="Internal").ap()
    io["x2q"] = nc.dram_tensor("x2q", [2048, D], F32, kind="Internal").ap()
    io["x3"] = nc.dram_tensor("x3", [2048, D], F32, kind="Internal").ap()
    io["y"] = nc.dram_tensor("y", [2048, D], F32, kind="ExternalOutput").ap()

    with ExitStack() as st1:
        emit_l0_attn(nc, st1, io, Cnt())
    with ExitStack() as st2:
        emit_l0_ffn(nc, st2, io, Cnt())

    with nc.semaphore("fx_sem") as s_fx, nc.Block("fuse_x2q") as blk:
        @blk.sync
        def _(e):
            for arm in e.switch_core_id(n=128):
                hf = arm.logical % 2
                e.dma_start(io["x2q"],
                            io["x2"][2048 * hf:2048 * hf + 2048, :]).then_inc(s_fx, 16)
            e.wait_ge(s_fx, 16)

    with ExitStack() as st3:
        emit_l1_attn(nc, st3, io, Cnt())
    with ExitStack() as st4:
        emit_l1_ffn(nc, st4, io, Cnt())
    return nc


_NC_CACHE = {}


def _get_nc(name, builder):
    if name not in _NC_CACHE:
        _NC_CACHE[name] = builder()
    return _NC_CACHE[name]


def kernel(**inputs):
    consts = host_consts()
    cores = list(range(8))
    nc = _get_nc("fused", build_fused)
    wts2 = host_stage2_weights(inputs)
    in_maps = []
    for c in cores:
        b = c // 2
        m = {"x": np.ascontiguousarray(inputs["L0_attn_x"][b]),
             "wq": np.ascontiguousarray(inputs["L0_attn_W_q"]),
             "wk": np.ascontiguousarray(inputs["L0_attn_W_k"]),
             "wv": np.ascontiguousarray(inputs["L0_attn_W_v"]),
             "wqt": np.ascontiguousarray(inputs["L1_attn_Wq"].T),
             "wkt": np.ascontiguousarray(inputs["L1_attn_Wk"].T),
             "wvt": np.ascontiguousarray(inputs["L1_attn_Wv"].T),
             "wot": np.ascontiguousarray(inputs["L1_attn_Wo"].T),
             "bq8": np.ascontiguousarray(inputs["L1_attn_bq"].reshape(8, 128).T),
             "bk8": np.ascontiguousarray(inputs["L1_attn_bk"].reshape(8, 128).T),
             "bv": np.ascontiguousarray(inputs["L1_attn_bv"]),
             "bo": np.ascontiguousarray(inputs["L1_attn_bo"]),
             "wgt": np.ascontiguousarray(inputs["L1_ffn_w_gate"].T),
             "wut": np.ascontiguousarray(inputs["L1_ffn_w_up"].T),
             "wdt": np.ascontiguousarray(inputs["L1_ffn_w_down"].T),
             "c_ones": np.ones((128, 128), np.float32)}
        m.update(consts)
        m.update(wts2)
        in_maps.append(m)
    res = run_bass_kernel_spmd(nc, in_maps, core_ids=cores)
    y = [res.results[c]["y"] for c in cores]
    out = np.empty((B, L, D), np.float32)
    for b in range(B):
        out[b, 0:2048] = y[2 * b]
        out[b, 2048:4096] = y[2 * b + 1]
    return out


# version 12
# speedup vs baseline: 1.1762x; 1.1762x over previous
import numpy as np
from contextlib import ExitStack

import concourse.bass as bass
import concourse.mybir as mybir
from concourse.bass_utils import run_bass_kernel_spmd

F32 = mybir.dt.float32
BF16 = mybir.dt.bfloat16
I8 = mybir.dt.int8
I32 = mybir.dt.int32
AF = mybir.ActivationFunctionType
ALU = mybir.AluOpType
AX = mybir.AxisListType.X

B, L, D = 4, 4096, 1024
H, DH = 16, 64
HS = DH + 1
C_SHIFT = 16.0
ESHIFT = float(np.exp(-C_SHIFT))
MAGIC = 12582912.0
GP = 1366
NKT = 11
KTGS = [(0, 4), (4, 4), (8, 3)]


DEBUG_S1 = False


def _nk(kt):
    return 128 if kt < 10 else 86


class Cnt:
    def __init__(self):
        self.v = {}

    def bump(self, name, n=1):
        self.v[name] = self.v.get(name, 0) + n
        return self.v[name]

    def get(self, name):
        return self.v.get(name, 0)


def host_consts():
    ident = np.eye(128, dtype=np.float32)
    bm = np.ones((128, 132), dtype=np.float32)
    for r in range(128):
        for c in range(r, min(r + 5, 132)):
            bm[r, c] = 2.0
    cc = np.zeros(NKT * 128, dtype=np.float32)
    for p in range(GP):
        c = max(0, 2 - p) + max(0, p - (GP - 3))
        cc[p] = c * ESHIFT
    cc = np.ascontiguousarray(cc.reshape(NKT, 128).T)
    sel4 = np.zeros((4, 512), dtype=np.float32)
    for j in range(4):
        sel4[j, 128 * j:128 * j + 128] = 1.0
    return {"c_ident": ident, "c_band": bm, "c_ccorr": cc, "c_sel4": sel4}


def _band_window(kt, q0, nq):
    w0 = 128 * kt - 2
    c0 = max(0, w0 - q0)
    c1 = min(nq, w0 + 132 - q0)
    if c1 <= c0:
        return None
    return (c0, c1, c0 + q0 - w0)



def emit_l0_attn(nc, st, io, cnt):
    x = io["x"]
    attnout = io["attnout"]

    sb = lambda name, shape, dt: st.enter_context(nc.sbuf_tensor(name, shape, dt))
    sem = lambda name: st.enter_context(nc.semaphore(name))

    wq_sb = sb("s1_wq", [128, 8, 1024], BF16)
    wk_sb = sb("s1_wk", [128, 8, 1024], BF16)
    wv_sb = sb("s1_wv", [128, 8, 1024], BF16)
    identb = sb("s1_identb", [128, 128], BF16)
    band = sb("s1_band", [128, 132], BF16)
    ccorr = sb("s1_ccorr", [128, NKT], F32)
    negc = sb("s1_negc", [128, 1], F32)

    x_tm = sb("s1_xtm", [128, NKT, 1024], BF16)
    x_fm = sb("s1_xfm", [128, 8, NKT * 128], BF16)
    k_fm = sb("s1_kfm", [128, 8, NKT * 128], BF16)
    q_fm = sb("s1_qfm", [128, 8, NKT * 128], BF16)
    v_tm = sb("s1_vtm", [128, NKT, H, HS], BF16)
    a_buf = sb("s1_abuf", [128, 2, 2, 512], BF16)
    o_stage = sb("s1_ostage", [128, 4, 1024], F32)
    dbg_den = sb("s1_dbgden", [128, 4, 16], F32)
    dbg_num = sb("s1_dbgnum", [128, 4, 16], F32)
    dcol = sb("s1_dcol", [128, 8], F32)
    rcol = sb("s1_rcol", [128, 8], F32)

    s_w = sem("s1_w")
    s_x = sem("s1_x")
    s_t = sem("s1_t")
    s_v = sem("s1_v")
    s_a = sem("s1_a")
    s_b = sem("s1_b")
    s_e = sem("s1_e")
    s_o = sem("s1_o")
    s_dbg = sem("s1_dbg")

    with nc.Block("s1_ld") as blk:
        @blk.gpsimd
        def _(e):
            for w_sb, nm in ((wq_sb, "wq"), (wk_sb, "wk"), (wv_sb, "wv")):
                e.dma_start(w_sb[:], io[nm].rearrange("(c p) j -> p c j", p=128)
                            ).then_inc(s_w, 16)
                cnt.bump("s_w", 16)
            e.dma_start(identb[:], io["c_ident"]).then_inc(s_w, 16)
            e.dma_start(band[:], io["c_band"]).then_inc(s_w, 16)
            e.dma_start(ccorr[:], io["c_ccorr"]).then_inc(s_w, 16)
            cnt.bump("s_w", 48)
            e.wait_ge(s_w, cnt.get("s_w"))

        @blk.vector
        def _(e):
            e.memset(negc[:], -C_SHIFT)
            if DEBUG_S1:
                e.memset(x_fm[:], 0.0)
                e.memset(k_fm[:], 0.0)
                e.memset(q_fm[:], 0.0)
                e.memset(v_tm[:, :, :, 0:64], 0.0)
                e.memset(a_buf[:], 0.0)
                e.memset(o_stage[:], 0.0)
                e.memset(dbg_den[:], 0.0)
                e.memset(dbg_num[:], 0.0)
            e.memset(v_tm[:, :, :, 64:65], 1.0)

    for s in range(3):
        n_x = 1366 if s == 0 else 1365
        n_q = 1366 if s == 0 else 1365

        with nc.Block(f"s1_g{s}_ldx") as blk:
            @blk.vector
            def _(e):
                e.memset(x_tm[:, 10, :], 0.0).then_inc(s_v, 1)
                cnt.bump("s_v")

            @blk.gpsimd
            def _(e):
                e.wait_ge(s_v, cnt.get("s_v"))
                for t in range(NKT):
                    rows = min(128, n_x - 128 * t)
                    base = s + 3 * 128 * t
                    e.dma_start(
                        x_tm[0:rows, t, :],
                        x[base: base + 3 * (rows - 1) + 1: 3, :],
                    ).then_inc(s_x, 16)
                    cnt.bump("s_x", 16)
                e.wait_ge(s_x, cnt.get("s_x"))

        tquads = [(0, 4), (4, 4), (8, 3)]
        with nc.Block(f"s1_g{s}_tr") as blk, \
             nc.psum_tensor(f"s1_pst0_{s}", [128, 512], BF16) as pst0, \
             nc.psum_tensor(f"s1_pst1_{s}", [128, 512], BF16) as pst1:
            psts = [pst0, pst1]
            t0_base = cnt.get("s_t")
            v0_base = cnt.get("s_v")

            @blk.tensor
            def _(e):
                qi = 0
                for f in range(8):
                    for (t0, tn) in tquads:
                        if qi >= 2:
                            e.wait_ge(s_v, v0_base + qi - 1)
                        ps = psts[qi % 2]
                        ins = None
                        for i in range(tn):
                            ins = e.transpose(
                                ps[:, 128 * i:128 * i + 128],
                                x_tm[:, t0 + i, 128 * f:128 * f + 128],
                                identb[:],
                            )
                        ins.then_inc(s_t, 1)
                        cnt.bump("s_t")
                        qi += 1

            @blk.vector
            def _(e):
                qj = 0
                for f in range(8):
                    for (t0, tn) in tquads:
                        e.wait_ge(s_t, t0_base + qj + 1)
                        e.tensor_copy(
                            x_fm[:, f, 128 * t0:128 * (t0 + tn)],
                            psts[qj % 2][:, 0:128 * tn],
                        ).then_inc(s_v, 1)
                        cnt.bump("s_v")
                        qj += 1

        nsubs = [(0, 512), (512, 512), (1024, GP - 1024)]
        jobs = []
        for (w_sb, dst) in ((wk_sb, k_fm), (wq_sb, q_fm)):
            for m in range(8):
                for (o0, on) in nsubs:
                    jobs.append(("proj", w_sb, dst, m, o0, on))
        for t in range(NKT):
            for half in range(2):
                jobs.append(("v", t, half, _nk(t), None, None))

        with nc.Block(f"s1_g{s}_proj") as blk, \
             nc.psum_tensor(f"s1_psp0_{s}", [128, 512], F32) as psp0, \
             nc.psum_tensor(f"s1_psp1_{s}", [128, 512], F32) as psp1:
            psps = [psp0, psp1]
            t0_base = cnt.get("s_t")
            v0_base = cnt.get("s_v")

            @blk.tensor
            def _(e):
                for ji, job in enumerate(jobs):
                    ps = psps[ji % 2]
                    if ji >= 2:
                        e.wait_ge(s_v, v0_base + ji - 1)
                    ins = None
                    if job[0] == "proj":
                        _, w_sb, dst, m, o0, on = job
                        for kc in range(8):
                            ins = e.matmul(
                                ps[:, 0:on],
                                w_sb[:, kc, 128 * m:128 * m + 128],
                                x_fm[:, kc, o0:o0 + on],
                                start=(kc == 0), stop=(kc == 7),
                            )
                    else:
                        _, t, half, rows, _, _ = job
                        for kc in range(8):
                            ins = e.matmul(
                                ps[0:rows, 0:512],
                                x_fm[:, kc, 128 * t:128 * t + rows],
                                wv_sb[:, kc, 512 * half:512 * half + 512],
                                start=(kc == 0), stop=(kc == 7),
                            )
                    ins.then_inc(s_t, 1)
                    cnt.bump("s_t")

            @blk.vector
            def _(e):
                for ji, job in enumerate(jobs):
                    ps = psps[ji % 2]
                    e.wait_ge(s_t, t0_base + ji + 1)
                    if job[0] == "proj":
                        _, w_sb, dst, m, o0, on = job
                        e.tensor_copy(dst[:, m, o0:o0 + on], ps[:, 0:on]
                                      ).then_inc(s_v, 1)
                    else:
                        _, t, half, rows, _, _ = job
                        e.tensor_copy(
                            v_tm[0:rows, t, 8 * half:8 * half + 8, 0:64],
                            ps[0:rows, 0:512].rearrange("p (h c) -> p h c", h=8),
                        ).then_inc(s_v, 1)
                    cnt.bump("s_v")

        if DEBUG_S1 and s == 1:
            with nc.Block("s1_dbgden") as blk:
                @blk.sync
                def _(e):
                    e.dma_start(io["dbg_den"], dbg_den[:].rearrange("p a b -> p (a b)")).then_inc(s_o, 16)
                    e.dma_start(io["dbg_num"], dbg_num[:].rearrange("p a b -> p (a b)")).then_inc(s_o, 16)
                    cnt.bump("s_o", 32)
                    e.wait_ge(s_o, cnt.get("s_o"))
        if DEBUG_S1 and s == 0:
            with nc.Block("s1_dbg") as blk:
                @blk.gpsimd
                def _(e):
                    e.dma_start(io["dbg_k"], k_fm[:]).then_inc(s_dbg, 16)
                    e.dma_start(io["dbg_q"], q_fm[:]).then_inc(s_dbg, 16)
                    e.dma_start(io["dbg_v"], v_tm[:].rearrange("p a b c -> p (a b c)")).then_inc(s_dbg, 16)
                    e.dma_start(io["dbg_xfm"], x_fm[:]).then_inc(s_dbg, 16)
                    cnt.bump("s_dbg", 64)
                    e.wait_ge(s_dbg, cnt.get("s_dbg"))

        S1G = [(0, 2), (2, 2), (4, 2), (6, 2), (8, 2), (10, 1)]
        for (q0, nq) in [(0, 512), (512, 512), (1024, n_q - 1024)]:
            nqs = (nq + 127) // 128
            t_base = cnt.get("s_t")
            a_base = cnt.get("s_a")
            b_base = cnt.get("s_b")
            e_base = cnt.get("s_e")
            NG1 = len(S1G)
            NID = 16 * NG1

            def t_after_qk(idx):
                return t_base + (1 if idx == 0 else 2 * idx)

            def t_after_av(idx):
                return t_base + (2 * idx + 3 if idx < NID - 1 else 2 * NID)

            band_cum = [b_base]
            for h in range(16):
                for (k0, kn) in S1G:
                    nb = sum(1 for i in range(kn)
                             if _band_window(k0 + i, q0, nq) is not None)
                    band_cum.append(band_cum[-1] + nb)

            with nc.Block(f"s1_g{s}_att{q0}") as blk, \
                 nc.psum_tensor(f"s1_psS0_{s}_{q0}", [128, 1024], F32) as psS0, \
                 nc.psum_tensor(f"s1_psS1_{s}_{q0}", [128, 1024], F32) as psS1, \
                 nc.psum_tensor(f"s1_psO_{s}_{q0}", [128, 2048], F32) as psO:
                psS = [psS0, psS1]

                @blk.tensor
                def _(e):
                    def emit_qk(idx):
                        h, g = divmod(idx, NG1)
                        hc, hp = h // 2, 64 * (h % 2)
                        k0, kn = S1G[g]
                        par = idx % 2
                        if idx >= 2:
                            e.wait_ge(s_a, a_base + idx - 1)
                        ins = None
                        for i in range(kn):
                            kt = k0 + i
                            nk = _nk(kt)
                            ins = e.matmul(
                                psS[par][0:nk, 512 * i:512 * i + nq],
                                k_fm[hp:hp + DH, hc, 128 * kt:128 * kt + nk],
                                q_fm[hp:hp + DH, hc, q0:q0 + nq],
                                start=True, stop=True,
                            )
                        ins.then_inc(s_t, 1)
                        cnt.bump("s_t")

                    def emit_av(idx):
                        h, g = divmod(idx, NG1)
                        k0, kn = S1G[g]
                        par = idx % 2
                        e.wait_ge(s_a, a_base + idx + 1)
                        e.wait_ge(s_b, band_cum[idx + 1])
                        if g == 0 and h > 0:
                            e.wait_ge(s_e, e_base + h)
                        ins = None
                        for i in range(kn):
                            kt = k0 + i
                            nk = _nk(kt)
                            for j in range(nqs):
                                mj = min(128, nq - 128 * j)
                                ins = e.matmul(
                                    psO[0:mj, 512 * j:512 * j + HS],
                                    a_buf[0:nk, par, i, 128 * j:128 * j + mj],
                                    v_tm[0:nk, kt, h, :],
                                    start=(kt == 0), stop=(kt == 10),
                                    skip_group_check=True,
                                )
                        ins.then_inc(s_t, 1)
                        cnt.bump("s_t")

                    emit_qk(0)
                    for idx in range(1, NID):
                        emit_qk(idx)
                        emit_av(idx - 1)
                    emit_av(NID - 1)

                @blk.scalar
                def _(e):
                    for idx in range(NID):
                        h, g = divmod(idx, NG1)
                        k0, kn = S1G[g]
                        par = idx % 2
                        e.wait_ge(s_t, t_after_qk(idx))
                        if nq == 512 and kn == 2:
                            e.activation(
                                a_buf[:, par, :, :].rearrange("p a b -> p (a b)"),
                                psS[par][:], AF.Exp, bias=negc[:], scale=0.125,
                            ).then_inc(s_a, 1)
                        else:
                            ins = None
                            for i in range(kn):
                                nk = _nk(k0 + i)
                                ins = e.activation(
                                    a_buf[0:nk, par, i, 0:nq],
                                    psS[par][0:nk, 512 * i:512 * i + nq],
                                    AF.Exp, bias=negc[0:nk, :], scale=0.125,
                                )
                            ins.then_inc(s_a, 1)
                        cnt.bump("s_a")

                @blk.vector
                def _(e):
                    for idx in range(NID):
                        h, g = divmod(idx, NG1)
                        k0, kn = S1G[g]
                        par = idx % 2
                        nb = band_cum[idx + 1] - band_cum[idx]
                        if nb:
                            e.wait_ge(s_a, a_base + idx + 1)
                            for i in range(kn):
                                kt = k0 + i
                                w = _band_window(kt, q0, nq)
                                if w is None:
                                    continue
                                c0, c1, m0 = w
                                nk = _nk(kt)
                                e.tensor_mul(
                                    a_buf[0:nk, par, i, c0:c1],
                                    a_buf[0:nk, par, i, c0:c1],
                                    band[0:nk, m0:m0 + (c1 - c0)],
                                ).then_inc(s_b, 1)
                                cnt.bump("s_b")
                        if g == NG1 - 1:
                            e.wait_ge(s_t, t_after_av(idx))
                            hp8 = 4 * (h % 2)
                            ins = None
                            for j in range(nqs):
                                mj = min(128, nq - 128 * j)
                                qt = (q0 + 128 * j) // 128
                                ins = e.tensor_add(dcol[0:mj, hp8 + j:hp8 + j + 1],
                                                   psO[0:mj, 512 * j + 64:512 * j + 65],
                                                   ccorr[0:mj, qt:qt + 1])
                            ins.then_inc(s_v, 1)
                            e.wait_ge(s_v, cnt.bump("s_v"))
                            ins = None
                            for j in range(nqs):
                                mj = min(128, nq - 128 * j)
                                ins = e.reciprocal(rcol[0:mj, hp8 + j:hp8 + j + 1],
                                                   dcol[0:mj, hp8 + j:hp8 + j + 1])
                            ins.then_inc(s_v, 1)
                            e.wait_ge(s_v, cnt.bump("s_v"))
                            ins = None
                            for j in range(nqs):
                                mj = min(128, nq - 128 * j)
                                ins = e.tensor_scalar_mul(
                                    o_stage[0:mj, j, 64 * h:64 * h + 64],
                                    psO[0:mj, 512 * j:512 * j + 64],
                                    rcol[0:mj, hp8 + j:hp8 + j + 1],
                                )
                            ins.then_inc(s_e, 1)
                            cnt.bump("s_e")

            with nc.Block(f"s1_g{s}_st{q0}") as blk:
                @blk.sync
                def _(e):
                    for j in range(nqs):
                        mj = min(128, nq - 128 * j)
                        base = s + 3 * (q0 + 128 * j)
                        e.dma_start(
                            attnout[base: base + 3 * (mj - 1) + 1: 3, :],
                            o_stage[0:mj, j, :],
                        ).then_inc(s_o, 16)
                        cnt.bump("s_o", 16)
                    e.wait_ge(s_o, cnt.get("s_o"))




NT2 = 32
NCH = 8
HT = 32


def emit_l0_ffn(nc, st, io, cnt):
    sb = lambda name, shape, dt: st.enter_context(nc.sbuf_tensor(name, shape, dt))
    sem = lambda name: st.enter_context(nc.semaphore(name))

    w2q = sb("s2_w2q", [128, 32, 1024], BF16)
    cwm = sb("s2_cwm", [128, 32, 3], F32)
    cbb = sb("s2_cb", [128, 32], F32)
    identb = sb("s2_identb", [128, 128], BF16)
    identf = sb("s2_identf", [128, 128], F32)
    onesf = sb("s2_onesf", [128, 128], F32)
    epscol = sb("s2_eps", [128, 1], F32)
    bcast4 = sb("s2_bcast4", [128, 4], F32)

    s_d = sem("s2_d")
    s_g = sem("s2_g")
    s_t = sem("s2_t")
    s_v = sem("s2_v")
    s_a = sem("s2_a")
    s_c = sem("s2_c")
    s_zl = [sem("s2_zl0"), sem("s2_zl1")]
    s_xs = [sem("s2_xs0"), sem("s2_xs1")]
    s_w1 = [sem(f"s2_w1_{i}") for i in range(4)]
    s_zz = sem("s2_zz")
    s_xq = sem("s2_xq")
    s_cn = sem("s2_cn")
    s_x2 = [sem("s2_x2a"), sem("s2_x2b")]

    NEL1 = float(4096 * 1024)

    with ExitStack() as stAB:
        stB1 = ExitStack()
        blk = stB1.enter_context(nc.Block("s2_AB"))
        sbA = lambda nm, shp, dt: stAB.enter_context(nc.sbuf_tensor(nm, shp, dt))
        wbuf = sbA("s2_wbuf", [128, 2, 4096], F32)
        asum = sbA("s2_asum", [128, 16], F32)
        tot = sbA("s2_tot", [1, 8], F32)
        xsb = sbA("s2_xs", [128, 2, 4096], F32)
        i8b = sbA("s2_i8", [128, 2, 4096], I8)
        ps_row = stAB.enter_context(nc.psum_tensor("s2_ps_row", [1, 16], F32))
        ps_bc = stAB.enter_context(nc.psum_tensor("s2_ps_bc", [128, 4], F32))
        zb = sbA("s2_zb", [128, 2, 1024], F32)
        zlnb = sbA("s2_zln", [128, 2, 1024], F32)
        st6 = sbA("s2_st6", [128, 12], F32)
        mv = sbA("s2_mv", [128, 2], F32)
        stdc = sbA("s2_stdc", [128, 1], F32)
        rstd = sbA("s2_rstd", [128, 1], F32)
        nmr = sbA("s2_nmr", [128, 1], F32)
        am = sbA("s2_am", [128, 4], F32)
        xs1 = sbA("s2_xs1", [128, 1024], F32)
        xqbf = sbA("s2_xqbf", [128, 2, 1024], BF16)
        xqev = sbA("s2_xqev", [128, 2, 1024], BF16)
        ps_tr = stAB.enter_context(nc.psum_tensor("s2_ps_tr", [128, 1024], BF16))

        w2r = io["w2t"].rearrange("(c p) j -> p c j", p=128)

        def wsrc(i):
            if i < 8:
                return io["w1t"][128 * i:128 * i + 128, :]
            g = i - 8
            return w2r[:, 4 * g:4 * g + 4, :]

        def wdst(i, buf):
            v = buf[:, i % 2, :]
            if i < 8:
                return v
            return v.rearrange("p (a b) -> p a b", a=4)

        @blk.sync
        def _(e):
            e.dma_start(cwm[:].rearrange("p a b -> p (a b)"), io["cw"]).then_inc(s_d, 16)
            e.dma_start(cbb[:], io["cb"]).then_inc(s_d, 16)
            e.wait_ge(s_d, cnt.bump("s_d", 32))
            for i in range(2):
                e.dma_start(wdst(i, wbuf), wsrc(i)).then_inc(s_d, 16)
                cnt.bump("s_d", 16)
                cnt.v[f"sA_ld1_{i}"] = cnt.get("s_d")
                e.wait_ge(s_d, cnt.get(f"sA_ld1_{i}"))

        @blk.gpsimd
        def _(e):
            e.dma_start(identb[:], io["c_ident"]).then_inc(s_cn, 16)
            e.dma_start(onesf[:], io["c_ones"]).then_inc(s_cn, 16)
            e.dma_start(identf[:], io["c_ident"]).then_inc(s_cn, 16)
            cnt.bump("s2cn", 48)
            cnt.v["sA_consts"] = cnt.get("s2cn")

        @blk.vector
        def _(e):
            e.memset(epscol[:], 1e-5).then_inc(s_v, 1)
            cnt.bump("s_v")

        for t in range(NT2):
            par = t % 2

            @blk.gpsimd
            def _(e):
                if t >= 1:
                    e.wait_ge(s_g, cnt.get(f"s2_zld_{t - 1}"))
                if t >= 2:
                    e.wait_ge(s_v, cnt.get(f"s2_zcons_{t - 2}"))
                e.dma_start(zb[:, par, :], io["x"][128 * t:128 * t + 128, :]
                            ).then_inc(s_g, 16)
                e.wait_ge(s_g, cnt.bump("s_g", 16))
                e.dma_start(zb[:, par, :], io["attnout"][128 * t:128 * t + 128, :],
                            accum_op=ALU.add).then_inc(s_g, 16)
                cnt.bump("s_g", 16)
                cnt.v[f"s2_zld_{t}"] = cnt.get("s_g")

            @blk.vector
            def _(e):
                e.wait_ge(s_g, cnt.get(f"s2_zld_{t}"))
                e.bn_stats(st6[:, 0:6], zb[:, par, 0:512])
                e.bn_stats(st6[:, 6:12], zb[:, par, 512:1024]).then_inc(s_v, 1)
                e.wait_ge(s_v, cnt.bump("s_v"))
                e.bn_aggr(mv[:], st6[:]).then_inc(s_v, 1)
                cnt.bump("s_v")
                cnt.v[f"s2_bn_{t}"] = cnt.get("s_v")

            @blk.scalar
            def _(e):
                e.wait_ge(s_v, cnt.get(f"s2_bn_{t}"))
                e.activation(stdc[:], mv[:, 1:2], AF.Sqrt, bias=epscol[:]
                             ).then_inc(s_a, 1)
                cnt.bump("s_a")
                cnt.v[f"s2_std_{t}"] = cnt.get("s_a")

            @blk.vector
            def _(e):
                e.wait_ge(s_a, cnt.get(f"s2_std_{t}"))
                if t >= 2:
                    e.wait_ge(s_zl[par], cnt.get(f"s2_zlnst_{t - 2}"))
                e.reciprocal(rstd[:], stdc[:]).then_inc(s_v, 1)
                e.wait_ge(s_v, cnt.bump("s_v"))
                e.scalar_tensor_tensor(nmr[:], mv[:, 0:1], -1.0, rstd[:],
                                       ALU.mult, ALU.mult).then_inc(s_v, 1)
                e.wait_ge(s_v, cnt.bump("s_v"))
                e.tensor_scalar(zlnb[:, par, :], zb[:, par, :], rstd[:], nmr[:],
                                ALU.mult, ALU.add).then_inc(s_v, 1)
                e.wait_ge(s_v, cnt.bump("s_v"))
                cnt.v[f"s2_zcons_{t}"] = cnt.get("s_v")
                cnt.v[f"s2_zln_{t}"] = cnt.get("s_v")
                e.reduce_max(am[:, 0:1], zlnb[:, par, :], axis=AX,
                             apply_absolute_value=True).then_inc(s_v, 1)
                e.wait_ge(s_v, cnt.bump("s_v"))
                e.tensor_scalar_max(am[:, 1:2], am[:, 0:1], 1e-5).then_inc(s_v, 1)
                e.wait_ge(s_v, cnt.bump("s_v"))
                e.reciprocal(am[:, 2:3], am[:, 1:2])
                e.tensor_scalar_mul(am[:, 3:4], am[:, 1:2], 1.0 / 127.0
                                    ).then_inc(s_v, 1)
                e.wait_ge(s_v, cnt.bump("s_v"))
                e.tensor_scalar(xs1[:], zlnb[:, par, :], am[:, 2:3], 127.0,
                                ALU.mult, ALU.mult).then_inc(s_v, 1)
                e.wait_ge(s_v, cnt.bump("s_v"))
                e.tensor_scalar(xs1[:], xs1[:], MAGIC, -MAGIC, ALU.add, ALU.add
                                ).then_inc(s_v, 1)
                e.wait_ge(s_v, cnt.bump("s_v"))
                e.tensor_scalar_mul(xqbf[:, par, :], xs1[:], am[:, 3:4]
                                    ).then_inc(s_v, 1)
                cnt.bump("s_v")
                cnt.v[f"s2_xqbf_{t}"] = cnt.get("s_v")

            @blk.scalar
            def _(e):
                e.wait_ge(s_v, cnt.get(f"s2_zln_{t}"))
                e.dma_start(io["zln"][128 * t:128 * t + 128, :], zlnb[:, par, :]
                            ).then_inc(s_zl[par], 16)
                cnt.bump(f"s2zl{par}", 16)
                cnt.v[f"s2_zlnst_{t}"] = cnt.get(f"s2zl{par}")

            @blk.tensor
            def _(e):
                e.wait_ge(s_v, cnt.get(f"s2_xqbf_{t}"))
                e.wait_ge(s_cn, cnt.get("sA_consts"))
                if t >= 2:
                    e.wait_ge(s_a, cnt.get(f"s2_trev_{t - 2}"))
                ins = None
                for f in range(8):
                    ins = e.transpose(ps_tr[:, 128 * f:128 * f + 128],
                                      xqbf[:, par, 128 * f:128 * f + 128], identb[:])
                ins.then_inc(s_t, 1)
                cnt.bump("s_t")
                cnt.v[f"s2_tr_{t}"] = cnt.get("s_t")

            @blk.scalar
            def _(e):
                e.wait_ge(s_t, cnt.get(f"s2_tr_{t}"))
                if t >= 2:
                    e.wait_ge(s_xs[par], cnt.get(f"s2_xqst_{t - 2}"))
                e.copy(xqev[:, par, :], ps_tr[:]).then_inc(s_a, 1)
                cnt.bump("s_a")
                cnt.v[f"s2_trev_{t}"] = cnt.get("s_a")
                e.wait_ge(s_a, cnt.get(f"s2_trev_{t}"))
                e.dma_start(
                    io["xqf"].rearrange("(a p) j -> p a j", p=128)[:, :, 128 * t:128 * t + 128],
                    xqev[:, par, :].rearrange("p (a b) -> p a b", a=8),
                ).then_inc(s_xs[par], 16)
                cnt.bump(f"s2xs{par}", 16)
                cnt.v[f"s2_xqst_{t}"] = cnt.get(f"s2xs{par}")

            if t % 2 == 1 and (t - 1) // 2 < 16:
                i = (t - 1) // 2

                @blk.scalar
                def _(e):
                    e.wait_ge(s_d, cnt.get(f"sA_ld1_{i}"))
                    if i >= 2:
                        e.wait_ge(s_v, cnt.get(f"sA_red_{i - 2}"))
                    e.activation(xsb[:, i % 2, :], wbuf[:, i % 2, :], AF.Abs
                                 ).then_inc(s_a, 1)
                    cnt.bump("s_a")
                    cnt.v[f"sA_abs_{i}"] = cnt.get("s_a")

                @blk.vector
                def _(e):
                    e.wait_ge(s_a, cnt.get(f"sA_abs_{i}"))
                    e.reduce_sum(asum[:, i:i + 1], xsb[:, i % 2, :], axis=AX
                                 ).then_inc(s_v, 1)
                    cnt.bump("s_v")
                    cnt.v[f"sA_red_{i}"] = cnt.get("s_v")

                if i + 2 < 16:
                    @blk.sync
                    def _(e):
                        e.wait_ge(s_a, cnt.get(f"sA_abs_{i}"))
                        e.dma_start(wdst(i + 2, wbuf), wsrc(i + 2)).then_inc(s_d, 16)
                        cnt.bump("s_d", 16)
                        cnt.v[f"sA_ld1_{i + 2}"] = cnt.get("s_d")
                        e.wait_ge(s_d, cnt.get(f"sA_ld1_{i + 2}"))

        @blk.tensor
        def _(e):
            e.wait_ge(s_v, cnt.get("sA_red_15"))
            e.matmul(ps_row[:, 0:16], onesf[:, 0:1], asum[:], start=True,
                     stop=True).then_inc(s_t, 1)
            cnt.bump("s_t")
            cnt.v["sA_row"] = cnt.get("s_t")

        @blk.vector
        def _(e):
            e.wait_ge(s_t, cnt.get("sA_row"))
            e.reduce_sum(tot[0:1, 0:1], ps_row[0:1, 0:8], axis=AX)
            e.reduce_sum(tot[0:1, 1:2], ps_row[0:1, 8:16], axis=AX
                         ).then_inc(s_v, 1)
            e.wait_ge(s_v, cnt.bump("s_v"))
            e.tensor_scalar(tot[0:1, 2:3], tot[0:1, 0:1], 1.0 / NEL1, 1e-5,
                            ALU.mult, ALU.max)
            e.tensor_scalar(tot[0:1, 3:4], tot[0:1, 1:2], 1.0 / NEL1, 1e-5,
                            ALU.mult, ALU.max).then_inc(s_v, 1)
            e.wait_ge(s_v, cnt.bump("s_v"))
            e.reciprocal(tot[0:1, 4:5], tot[0:1, 2:3])
            e.reciprocal(tot[0:1, 5:6], tot[0:1, 3:4]).then_inc(s_v, 1)
            cnt.bump("s_v")
            cnt.v["sA_tot"] = cnt.get("s_v")

        @blk.tensor
        def _(e):
            e.wait_ge(s_v, cnt.get("sA_tot"))
            e.matmul(ps_bc[:, 0:1], onesf[0:1, :], tot[0:1, 4:5], start=True, stop=True)
            e.matmul(ps_bc[:, 1:2], onesf[0:1, :], tot[0:1, 2:3], start=True, stop=True)
            e.matmul(ps_bc[:, 2:3], onesf[0:1, :], tot[0:1, 5:6], start=True, stop=True)
            e.matmul(ps_bc[:, 3:4], onesf[0:1, :], tot[0:1, 3:4],
                     start=True, stop=True).then_inc(s_t, 1)
            cnt.bump("s_t")
            cnt.v["sA_bct"] = cnt.get("s_t")

        @blk.scalar
        def _(e):
            e.wait_ge(s_t, cnt.get("sA_bct"))
            e.copy(bcast4[:], ps_bc[:]).then_inc(s_a, 1)
            cnt.bump("s_a")
            cnt.v["sA_bc"] = cnt.get("s_a")

        @blk.vector
        def _(e):
            e.wait_ge(s_a, cnt.get("sA_bc"))
            e.tensor_scalar_mul(cwm[:].rearrange("p a b -> p (a b)"),
                                cwm[:].rearrange("p a b -> p (a b)"),
                                bcast4[:, 1:2]).then_inc(s_v, 1)
            cnt.bump("s_v")
            cnt.v["sA_cwm"] = cnt.get("s_v")

        @blk.sync
        def _(e):
            e.wait_ge(s_zl[0], cnt.get("s2zl0"))
            e.wait_ge(s_zl[1], cnt.get("s2zl1"))
            e.wait_ge(s_xs[0], cnt.get("s2xs0"))
            e.wait_ge(s_xs[1], cnt.get("s2xs1"))
            e.wait_ge(s_d, cnt.get("s_d"))

        @blk.gpsimd
        def _(e):
            e.wait_ge(s_g, cnt.get("s_g"))

        stB1.close()
        blk = stAB.enter_context(nc.Block("s2_wq2"))
        v2 = cnt.get("s_v")
        d2 = cnt.get("s_d")

        @blk.sync
        def _(e):
            for i in range(16):
                if i >= 2:
                    e.wait_ge(s_v, v2 + 4 * (i - 2) + 1)
                e.dma_start(wdst(i, wbuf), wsrc(i)).then_inc(s_d, 16)
                e.wait_ge(s_d, cnt.bump("s_d", 16))
                if i < 8:
                    e.wait_ge(s_v, v2 + 4 * i + 4)
                    e.dma_start(
                        io["w1qd"][:, :, 128 * i:128 * i + 128].rearrange("m p j -> p m j"),
                        i8b[:, i % 2, :].rearrange("p (m j) -> p m j", m=32),
                    ).then_inc(s_d, 16)
                    e.wait_ge(s_d, cnt.bump("s_d", 16))
                    cnt.v[f"s2_w1sp_{i}"] = cnt.get("s_d")

        @blk.vector
        def _(e):
            for i in range(16):
                scol = bcast4[:, 0:1] if i < 8 else bcast4[:, 2:3]
                e.wait_ge(s_d, d2 + 16 * (i + 1) + 16 * min(i, 8))
                e.tensor_scalar(xsb[:, i % 2, :], wbuf[:, i % 2, :], scol,
                                1.0, ALU.mult, ALU.min).then_inc(s_v, 1)
                e.wait_ge(s_v, cnt.bump("s_v"))
                e.tensor_scalar(xsb[:, i % 2, :], xsb[:, i % 2, :], -1.0, MAGIC,
                                ALU.max, ALU.add).then_inc(s_v, 1)
                e.wait_ge(s_v, cnt.bump("s_v"))
                e.tensor_scalar(xsb[:, i % 2, :], xsb[:, i % 2, :], -MAGIC, 0.0,
                                ALU.add, ALU.add).then_inc(s_v, 1)
                e.wait_ge(s_v, cnt.bump("s_v"))
                if 2 <= i < 10:
                    e.wait_ge(s_d, cnt.get(f"s2_w1sp_{i - 2}"))
                if i < 8:
                    e.tensor_copy(i8b[:, i % 2, :], xsb[:, i % 2, :]).then_inc(s_v, 1)
                    cnt.bump("s_v")
                else:
                    g = i - 8
                    dst = w2q[:, 4 * g:4 * g + 4, :].rearrange("p a b -> p (a b)")
                    e.tensor_copy(dst, xsb[:, i % 2, :]).then_inc(s_v, 1)
                    cnt.bump("s_v")

        @blk.sync
        def _(e):
            e.wait_ge(s_d, cnt.get("s_d"))

    with ExitStack() as stc:
        blk = stc.enter_context(nc.Block("s2_ffnC"))
        sbc = lambda nm, shp, dt: stc.enter_context(nc.sbuf_tensor(nm, shp, dt))
        psc_ = lambda nm, shp, dt: stc.enter_context(nc.psum_tensor(nm, shp, dt))
        xqc = sbc("s2_xqc", [128, 8, 514], BF16)
        w1s = sbc("s2_w1s", [128, 4, 8, 128], BF16)
        ca = sbc("s2_ca", [128, 2, 512], BF16)
        hbufH = sbc("s2_hbufH", [128, 2, 2], BF16)
        hs = sbc("s2_hs", [128, 32, 512], BF16)
        hq = sbc("s2_hqb", [128, 32, 512], BF16)
        rp = sbc("s2_rp", [128, 4, 512], BF16)
        rn = sbc("s2_rn", [128, 2, 512], BF16)
        rpf = sbc("s2_rpf", [128, 512], F32)
        am4 = sbc("s2_am4", [128, 2, 16], F32)
        scT = sbc("s2_sct", [4, 128], F32)
        sel4 = sbc("s2_sel4", [4, 512], F32)
        bscale = sbc("s2_bscale", [128, 512], F32)
        xs8 = sbc("s2_xs8", [128, 2, 512], F32)
        zz = sbc("s2_zz", [128, 4, 1024], F32)
        st6c = sbc("s2_st6b", [128, 12], F32)
        mvc = sbc("s2_mvb", [128, 4, 2], F32)
        ln2c = sbc("s2_ln2", [128, 4, 4], F32)
        ph0 = psc_("s2_ph0", [128, 512], F32)
        ph1 = psc_("s2_ph1", [128, 512], F32)
        phh = psc_("s2_phh", [128, 2], F32)
        psM = psc_("s2_psm", [128, 512], F32)
        pf0 = psc_("s2_pf0", [128, 1024], F32)
        pf1 = psc_("s2_pf1", [128, 1024], F32)
        phs = [ph0, ph1]
        pfs = [pf0, pf1]

        @blk.gpsimd
        def _(e):
            e.dma_start(sel4[:], io["c_sel4"]).then_inc(s_g, 16)
            cnt.bump("s_g", 16)
            cnt.v["s2P_sel4"] = cnt.get("s_g")

        for c in range(NCH + 1):
            head = c < NCH
            tail = c >= 1
            cp = c - 1
            t0 = 512 * c
            lo = (c == 0)
            hi = (c == NCH - 1)
            src0 = max(t0 - 1, 0)
            src1 = min(t0 + 513, 4096)
            dst0 = src0 - (t0 - 1)

            v_base = cnt.get("s_v")
            t_base = cnt.get("s_t")
            a_base = cnt.get("s_a")
            M0 = v_base + 1
            F2 = 4 if tail else 0
            PINC = 7 if tail else 4
            EPI0 = (M0 + 16 * PINC + 3) if head else (v_base + 48)

            QOFF = 3 if tail else 0

            def conv3mark(m, M0=M0, PINC=PINC, QOFF=QOFF):
                return M0 + PINC * (m // 2) + QOFF + 2

            def maxes(m, M0=M0, PINC=PINC, QOFF=QOFF):
                return M0 + PINC * (m // 2) + QOFF + 3 + (m % 2)

            def qmark(g, M0=M0, PINC=PINC):
                return M0 + PINC * g + 3

            def hvmark(m, a_base=a_base):
                return a_base + 8 * (m // 2) + 1 + 2 * (m % 2)

            def camark(m, a_base=a_base):
                return a_base + 8 * (m // 2) + 2 + 2 * (m % 2)

            def silumark(m, a_base=a_base):
                return a_base + 8 * (m // 2) + 5 + (m % 2)

            def absmark(m, a_base=a_base):
                return a_base + 8 * (m // 2) + 7 + (m % 2)

            def sttmark(tt, EPI0=EPI0):
                return EPI0 + 3 * tt + 1

            @blk.sync
            def _(e):
                if not head:
                    return
                if c >= 1:
                    e.wait_ge(s_t, cnt.get(f"s2P_ff1_{c - 1}_31"))
                e.dma_start(
                    xqc[:, :, dst0:dst0 + (src1 - src0)],
                    io["xqf"].rearrange("(a p) j -> p a j", p=128)[:, :, src0:src1],
                ).then_inc(s_xq, 16)
                cnt.bump("s2xq", 16)
                cnt.v[f"s2P_xqc_{c}"] = cnt.get("s2xq")

            @blk.gpsimd
            def _(e):
                if not head:
                    return
                if c == 0:
                    e.wait_ge(s_g, cnt.get("s2P_sel4"))
                for m in range(HT):
                    slot = m % 4
                    if m >= 4:
                        e.wait_ge(s_t, t_base + (m - 4) + 1)
                    elif c >= 1:
                        e.wait_ge(s_t, cnt.get(f"s2P_ff1_{c - 1}_{28 + m}"))
                    e.dma_start(
                        w1s[:, slot, :, :],
                        io["w1qd"][m].rearrange("p (k j) -> p k j", k=8),
                    ).then_inc(s_w1[slot], 16)
                    cnt.bump(f"s2w1_{slot}", 16)
                    cnt.v[f"s2P_w1_{c}_{m}"] = cnt.get(f"s2w1_{slot}")

            @blk.vector
            def _(e):
                if tail and not head:
                    e.wait_ge(s_a, cnt.get(f"s2P_bsc_{cp}"))
                    if cp >= 1:
                        e.wait_ge(s_t, cnt.get(f"s2P_ffn2_{cp - 1}_3"))
                    for g in range(16):
                        e.wait_ge(s_a, cnt.get(f"s2P_abs_{cp}_{2 * g + 1}"))
                        ins = None
                        for i in range(2):
                            ins = e.tensor_mul(xs8[:, i, :], hs[:, 2 * g + i, :],
                                               bscale[:])
                        ins.then_inc(s_v, 1)
                        e.wait_ge(s_v, cnt.bump("s_v"))
                        ins = None
                        for i in range(2):
                            ins = e.tensor_scalar(xs8[:, i, :], xs8[:, i, :], MAGIC,
                                                  -MAGIC, ALU.add, ALU.add)
                        ins.then_inc(s_v, 1)
                        e.wait_ge(s_v, cnt.bump("s_v"))
                        ins = None
                        for i in range(2):
                            ins = e.tensor_copy(hq[:, 2 * g + i, :], xs8[:, i, :])
                        ins.then_inc(s_v, 1)
                        e.wait_ge(s_v, cnt.bump("s_v"))
                        cnt.v[f"s2P_q_{cp}_{g}"] = cnt.get("s_v")
                    cnt.v[f"s2P_qlast_{cp}"] = cnt.get("s_v")

            @blk.tensor
            def _(e):
                if head:
                    e.wait_ge(s_xq, cnt.get(f"s2P_xqc_{c}"))
                    if lo or hi:
                        e.wait_ge(s_v, M0)
                    for m in range(HT):
                        e.wait_ge(s_w1[m % 4], cnt.get(f"s2P_w1_{c}_{m}"))
                        if m >= 2:
                            e.wait_ge(s_v, conv3mark(m - 2))
                        elif c >= 1:
                            e.wait_ge(s_v, cnt.get(f"s2P_conv_{c - 1}_15"))
                        for kc in range(8):
                            e.matmul(phs[m % 2][:], w1s[:, m % 4, kc, :],
                                     xqc[:, kc, 1:513], start=(kc == 0), stop=(kc == 7))
                        if m >= 1:
                            e.wait_ge(s_a, hvmark(m - 1))
                        elif c >= 1:
                            e.wait_ge(s_a, cnt.get(f"s2P_hv_{c - 1}_31"))
                        ins = None
                        for kc in range(8):
                            ins = e.matmul(phh[:], w1s[:, m % 4, kc, :],
                                           xqc[:, kc, 0:514:513],
                                           start=(kc == 0), stop=(kc == 7))
                        ins.then_inc(s_t, 1)
                        cnt.bump("s_t")
                        cnt.v[f"s2P_ff1_{c}_{m}"] = cnt.get("s_t")
                if tail:
                    qlast = (M0 + 16 * PINC) if head else (v_base + 48)
                    for tt in range(4):
                        if tt == 0:
                            e.wait_ge(s_v, qlast)
                        if tt >= 2:
                            e.wait_ge(s_v, sttmark(tt - 2))
                        elif cp >= 1:
                            e.wait_ge(s_v, cnt.get(f"s2P_stt_{cp - 1}_{tt + 2}"))
                        ins = None
                        for half in range(2):
                            for k in range(HT):
                                ins = e.matmul(pfs[tt % 2][:, 512 * half:512 * half + 512],
                                               hq[:, k, 128 * tt:128 * tt + 128],
                                               w2q[:, k, 512 * half:512 * half + 512],
                                               start=(k == 0), stop=(k == HT - 1))
                        ins.then_inc(s_t, 1)
                        cnt.bump("s_t")
                        cnt.v[f"s2P_ffn2_{cp}_{tt}"] = cnt.get("s_t")

            @blk.scalar
            def _(e):
                if not head:
                    return
                for p in range(HT // 2):
                    m0, m1 = 2 * p, 2 * p + 1
                    for m in (m0, m1):
                        e.wait_ge(s_t, t_base + m + 1)
                        if m >= 2:
                            e.wait_ge(s_v, conv3mark(m - 2))
                        elif c >= 1:
                            e.wait_ge(s_v, cnt.get(f"s2P_conv_{c - 1}_15"))
                        e.copy(hbufH[:, m % 2, :], phh[:]).then_inc(s_a, 1)
                        cnt.bump("s_a")
                        cnt.v[f"s2P_hv_{c}_{m}"] = cnt.get("s_a")
                        if m >= 2:
                            e.wait_ge(s_a, silumark(m - 2))
                        elif c >= 1:
                            e.wait_ge(s_a, cnt.get(f"s2P_silu_{c - 1}_{m + 30}"))
                        e.activation(ca[:, m % 2, :], phs[m % 2][:], AF.Copy,
                                     scale=cwm[:, m, 1:2]).then_inc(s_a, 1)
                        cnt.bump("s_a")
                        cnt.v[f"s2P_ca_{c}_{m}"] = cnt.get("s_a")
                    for m in (m0, m1):
                        e.wait_ge(s_v, conv3mark(m))
                        if c >= 1:
                            e.wait_ge(s_v, qmark(m // 2))
                        e.activation(hs[:, m, :], ca[:, m % 2, :], AF.Silu,
                                     bias=cbb[:, m:m + 1]).then_inc(s_a, 1)
                        cnt.bump("s_a")
                        cnt.v[f"s2P_silu_{c}_{m}"] = cnt.get("s_a")
                    for m in (m0, m1):
                        if m >= 2:
                            e.wait_ge(s_v, maxes(m - 2))
                        elif c >= 1:
                            e.wait_ge(s_v, cnt.get(f"s2P_max_{c - 1}_{30 + m}"))
                        e.wait_ge(s_a, silumark(m))
                        e.activation(rn[:, m % 2, :], hs[:, m, :], AF.Abs
                                     ).then_inc(s_a, 1)
                        cnt.bump("s_a")
                        cnt.v[f"s2P_abs_{c}_{m}"] = cnt.get("s_a")

            @blk.vector
            def _(e):
                if head:
                    if c >= 1:
                        e.wait_ge(s_t, cnt.get(f"s2P_ptrF_{c - 1}"))
                    ins = e.memset(rp[:], 0.0)
                    if lo or hi:
                        e.wait_ge(s_xq, cnt.get(f"s2P_xqc_{c}"))
                        ins = e.memset(xqc[:, :, 0:1] if lo else xqc[:, :, 513:514],
                                       0.0)
                    ins.then_inc(s_v, 1)
                    cnt.bump("s_v")
                    if tail:
                        e.wait_ge(s_a, cnt.get(f"s2P_bsc_{cp}"))
                        if cp >= 1:
                            e.wait_ge(s_t, cnt.get(f"s2P_ffn2_{cp - 1}_3"))
                    for p in range(HT // 2):
                        m0, m1 = 2 * p, 2 * p + 1
                        if tail:
                            g = p
                            e.wait_ge(s_a, cnt.get(f"s2P_abs_{cp}_{2 * g + 1}"))
                            ins = None
                            for i in range(2):
                                ins = e.tensor_mul(xs8[:, i, :], hs[:, 2 * g + i, :],
                                                   bscale[:])
                            ins.then_inc(s_v, 1)
                            e.wait_ge(s_v, cnt.bump("s_v"))
                            ins = None
                            for i in range(2):
                                ins = e.tensor_scalar(xs8[:, i, :], xs8[:, i, :],
                                                      MAGIC, -MAGIC, ALU.add, ALU.add)
                            ins.then_inc(s_v, 1)
                            e.wait_ge(s_v, cnt.bump("s_v"))
                            ins = None
                            for i in range(2):
                                ins = e.tensor_copy(hq[:, 2 * g + i, :], xs8[:, i, :])
                            ins.then_inc(s_v, 1)
                            e.wait_ge(s_v, cnt.bump("s_v"))
                            cnt.v[f"s2P_q_{cp}_{g}"] = cnt.get("s_v")
                        ins = None
                        for m in (m0, m1):
                            e.wait_ge(s_a, camark(m))
                            e.scalar_tensor_tensor(ca[:, m % 2, 0:511],
                                                   phs[m % 2][:, 1:512],
                                                   cwm[:, m, 2:3],
                                                   ca[:, m % 2, 0:511],
                                                   ALU.mult, ALU.add)
                            ins = e.scalar_tensor_tensor(ca[:, m % 2, 511:512],
                                                         hbufH[:, m % 2, 1:2],
                                                         cwm[:, m, 2:3],
                                                         ca[:, m % 2, 511:512],
                                                         ALU.mult, ALU.add)
                        ins.then_inc(s_v, 1)
                        e.wait_ge(s_v, cnt.bump("s_v"))
                        ins = None
                        for m in (m0, m1):
                            e.scalar_tensor_tensor(ca[:, m % 2, 1:512],
                                                   phs[m % 2][:, 0:511],
                                                   cwm[:, m, 0:1],
                                                   ca[:, m % 2, 1:512],
                                                   ALU.mult, ALU.add)
                            ins = e.scalar_tensor_tensor(ca[:, m % 2, 0:1],
                                                         hbufH[:, m % 2, 0:1],
                                                         cwm[:, m, 0:1],
                                                         ca[:, m % 2, 0:1],
                                                         ALU.mult, ALU.add)
                        ins.then_inc(s_v, 1)
                        cnt.bump("s_v")
                        cnt.v[f"s2P_conv_{c}_{p}"] = cnt.get("s_v")
                        for m in (m0, m1):
                            e.wait_ge(s_a, absmark(m))
                            e.tensor_max(rp[:, m % 4, :], rp[:, m % 4, :],
                                         rn[:, m % 2, :]).then_inc(s_v, 1)
                            cnt.bump("s_v")
                            cnt.v[f"s2P_max_{c}_{m}"] = cnt.get("s_v")
                    if tail:
                        cnt.v[f"s2P_qlast_{cp}"] = cnt.get("s_v")
                    e.wait_ge(s_v, maxes(HT - 1))
                    e.tensor_max(rp[:, 0, :], rp[:, 0, :], rp[:, 1, :])
                    e.tensor_max(rp[:, 2, :], rp[:, 2, :], rp[:, 3, :]).then_inc(s_v, 1)
                    e.wait_ge(s_v, cnt.bump("s_v"))
                    e.tensor_max(rp[:, 0, :], rp[:, 0, :], rp[:, 2, :]).then_inc(s_v, 1)
                    e.wait_ge(s_v, cnt.bump("s_v"))
                    if c >= 1:
                        e.wait_ge(s_t, cnt.get(f"s2P_ptrF_{c - 1}"))
                    e.tensor_copy(rpf[:], rp[:, 0, :]).then_inc(s_v, 1)
                    e.wait_ge(s_v, cnt.bump("s_v"))
                    cnt.v[f"s2P_cast_{c}"] = cnt.get("s_v")
                if tail:
                    for tt in range(4):
                        e.wait_ge(s_t, cnt.get(f"s2P_ffn2_{cp}_{tt}"))
                        e.wait_ge(s_zz, cnt.get(f"s2P_zz_{cp}"))
                        e.scalar_tensor_tensor(zz[:, tt, 0:512],
                                               pfs[tt % 2][:, 0:512],
                                               am4[:, cp % 2, 12 + tt:13 + tt],
                                               zz[:, tt, 0:512], ALU.mult, ALU.add)
                        e.scalar_tensor_tensor(zz[:, tt, 512:1024],
                                               pfs[tt % 2][:, 512:1024],
                                               am4[:, cp % 2, 12 + tt:13 + tt],
                                               zz[:, tt, 512:1024], ALU.mult, ALU.add
                                               ).then_inc(s_v, 1)
                        e.wait_ge(s_v, cnt.bump("s_v"))
                        cnt.v[f"s2P_stt_{cp}_{tt}"] = cnt.get("s_v")
                        e.bn_stats(st6c[:, 0:6], zz[:, tt, 0:512])
                        e.bn_stats(st6c[:, 6:12], zz[:, tt, 512:1024]
                                   ).then_inc(s_v, 1)
                        e.wait_ge(s_v, cnt.bump("s_v"))
                        if cp >= 1:
                            e.wait_ge(s_a, cnt.get(f"s2P_sqrt_{cp - 1}_{tt}"))
                        e.bn_aggr(mvc[:, tt, :], st6c[:]).then_inc(s_v, 1)
                        cnt.bump("s_v")
                        cnt.v[f"s2P_bn2_{cp}_{tt}"] = cnt.get("s_v")
                if head:
                    e.wait_ge(s_t, t_base + 32 + F2 + 1)
                    ins = None
                    for j in range(4):
                        ins = e.reduce_max(am4[:, c % 2, j:j + 1],
                                           psM[:, 128 * j:128 * j + 128], axis=AX)
                    ins.then_inc(s_v, 1)
                    e.wait_ge(s_v, cnt.bump("s_v"))
                    cnt.v[f"s2P_am4r_{c}"] = cnt.get("s_v")
                    e.tensor_scalar_max(am4[:, c % 2, 4:8], am4[:, c % 2, 0:4], 1e-5
                                        ).then_inc(s_v, 1)
                    e.wait_ge(s_v, cnt.bump("s_v"))
                    e.reciprocal(am4[:, c % 2, 8:12], am4[:, c % 2, 4:8])
                    e.tensor_scalar(am4[:, c % 2, 12:16], am4[:, c % 2, 4:8],
                                    bcast4[:, 3:4], 1.0 / 127.0, ALU.mult, ALU.mult
                                    ).then_inc(s_v, 1)
                    e.wait_ge(s_v, cnt.bump("s_v"))
                    e.tensor_scalar_mul(am4[:, c % 2, 8:12], am4[:, c % 2, 8:12],
                                        127.0).then_inc(s_v, 1)
                    cnt.bump("s_v")
                    cnt.v[f"s2P_am4_{c}"] = cnt.get("s_v")

            @blk.tensor
            def _(e):
                if not head:
                    return
                e.wait_ge(s_v, cnt.get(f"s2P_cast_{c}"))
                if c >= 1:
                    e.wait_ge(s_a, cnt.get(f"s2P_bsc_{c - 1}"))
                ins = None
                for j in range(4):
                    ins = e.transpose(psM[:, 128 * j:128 * j + 128],
                                      rpf[:, 128 * j:128 * j + 128], identf[:])
                ins.then_inc(s_t, 1)
                cnt.bump("s_t")
                cnt.v[f"s2P_ptrF_{c}"] = cnt.get("s_t")
                e.wait_ge(s_v, cnt.get(f"s2P_am4_{c}"))
                e.transpose(psM[0:4, 0:128], am4[:, c % 2, 8:12], identf[:]
                            ).then_inc(s_t, 1)
                cnt.bump("s_t")
                cnt.v[f"s2P_psc_{c}"] = cnt.get("s_t")
                e.wait_ge(s_a, a_base + 128 + F2 + 1)
                ins = None
                for j in range(4):
                    ins = e.matmul(psM[:, 128 * j:128 * j + 128],
                                   sel4[0:4, 128 * j:128 * j + 128],
                                   scT[0:4, 0:128], start=(j == 0), stop=(j == 3),
                                   skip_group_check=True)
                ins.then_inc(s_t, 1)
                cnt.bump("s_t")
                cnt.v[f"s2P_pbc_{c}"] = cnt.get("s_t")

            @blk.scalar
            def _(e):
                if tail:
                    for tt in range(4):
                        e.wait_ge(s_v, cnt.get(f"s2P_bn2_{cp}_{tt}"))
                        if cp >= 1:
                            e.wait_ge(s_v, cnt.get(f"s2P_ln_{cp - 1}_{tt}"))
                        e.activation(ln2c[:, tt, 0:1], mvc[:, tt, 1:2], AF.Sqrt,
                                     bias=epscol[:]).then_inc(s_a, 1)
                        cnt.bump("s_a")
                        cnt.v[f"s2P_sqrt_{cp}_{tt}"] = cnt.get("s_a")
                if head:
                    e.wait_ge(s_t, t_base + 32 + F2 + 2)
                    e.copy(scT[:], psM[0:4, 0:128]).then_inc(s_a, 1)
                    cnt.bump("s_a")
                    e.wait_ge(s_t, t_base + 32 + F2 + 3)
                    if c >= 1:
                        e.wait_ge(s_v, cnt.get(f"s2P_qlast_{c - 1}"))
                    e.copy(bscale[:], psM[:, 0:512]).then_inc(s_a, 1)
                    cnt.bump("s_a")
                    cnt.v[f"s2P_bsc_{c}"] = cnt.get("s_a")

            @blk.vector
            def _(e):
                if not tail:
                    return
                for tt in range(4):
                    e.wait_ge(s_a, cnt.get(f"s2P_sqrt_{cp}_{tt}"))
                    e.reciprocal(ln2c[:, tt, 1:2], ln2c[:, tt, 0:1]).then_inc(s_v, 1)
                    e.wait_ge(s_v, cnt.bump("s_v"))
                    e.scalar_tensor_tensor(ln2c[:, tt, 2:3], mvc[:, tt, 0:1], -1.0,
                                           ln2c[:, tt, 1:2], ALU.mult, ALU.mult
                                           ).then_inc(s_v, 1)
                    e.wait_ge(s_v, cnt.bump("s_v"))
                    e.tensor_scalar(zz[:, tt, :], zz[:, tt, :],
                                    ln2c[:, tt, 1:2], ln2c[:, tt, 2:3], ALU.mult,
                                    ALU.add).then_inc(s_v, 1)
                    cnt.bump("s_v")
                    cnt.v[f"s2P_ln_{cp}_{tt}"] = cnt.get("s_v")

            @blk.sync
            def _(e):
                if tail:
                    for tt in range(4):
                        e.wait_ge(s_v, cnt.get(f"s2P_ln_{cp}_{tt}"))
                        r0 = 512 * cp + 128 * tt
                        e.dma_start(io["x2"][r0:r0 + 128, :], zz[:, tt, :]
                                    ).then_inc(s_x2[tt % 2], 16)
                        cnt.bump(f"s2x2{tt % 2}", 16)
                if head:
                    if tail:
                        e.wait_ge(s_x2[0], cnt.get("s2x20"))
                        e.wait_ge(s_x2[1], cnt.get("s2x21"))
                    e.dma_start(
                        zz[:],
                        io["zln"].rearrange("(a p) j -> p a j", p=128)[:, 4 * c:4 * c + 4, :],
                    ).then_inc(s_zz, 16)
                    cnt.bump("s2zz", 16)
                    cnt.v[f"s2P_zz_{c}"] = cnt.get("s2zz")

        @blk.sync
        def _(e):
            e.wait_ge(s_x2[0], cnt.get("s2x20"))
            e.wait_ge(s_x2[1], cnt.get("s2x21"))
            e.wait_ge(s_zz, cnt.get("s2zz"))
            e.wait_ge(s_xq, cnt.get("s2xq"))

        @blk.gpsimd
        def _(e):
            e.wait_ge(s_g, cnt.get("s_g"))
            for k in range(4):
                e.wait_ge(s_w1[k], cnt.get(f"s2w1_{k}"))



NKT3 = 32
KTG3 = [(4 * i, 4) for i in range(8)]
TOWN = 2048


def emit_l1_attn(nc, st, io, cnt):
    sb = lambda name, shape, dt: st.enter_context(nc.sbuf_tensor(name, shape, dt))
    sem = lambda name: st.enter_context(nc.semaphore(name))

    k_fm = sb("s3_kfm", [128, 8, 4096], BF16)
    v_tm = sb("s3_vtm", [128, 32, H, HS], BF16)
    bq_sb = sb("s3_bq", [128, 8], F32)
    bk_sb = sb("s3_bk", [128, 8], F32)
    bbo = sb("s3_bbo", [128, 1024], F32)
    identb = sb("s3_identb", [128, 128], BF16)
    onesf = sb("s3_onesf", [1, 128], F32)
    negc = sb("s3_negc", [128, 1], F32)
    epscol = sb("s3_eps", [128, 1], F32)

    s_d = sem("s3_d")
    s_g = sem("s3_g")
    s_t = sem("s3_t")
    s_v = sem("s3_v")
    s_a = sem("s3_a")
    s_e = sem("s3_e")
    s_x3 = [sem("s3_x3a"), sem("s3_x3b")]

    with nc.Block("s3_ld") as blk, \
         nc.sbuf_tensor("s3_brow", [1, 1024], F32) as brow, \
         nc.psum_tensor("s3_psb", [128, 1024], F32) as psb:
        @blk.gpsimd
        def _(e):
            e.dma_start(identb[:], io["c_ident"]).then_inc(s_g, 16)
            e.wait_ge(s_g, cnt.bump("s_g", 16))

        @blk.sync
        def _(e):
            e.dma_start(bq_sb[:], io["bq8"]).then_inc(s_d, 16)
            e.dma_start(bk_sb[:], io["bk8"]).then_inc(s_d, 16)
            e.dma_start(brow[0:1, 0:1024], io["bo"][None, :]).then_inc(s_d, 16)
            e.dma_start(onesf[:], io["c_ones"][0:1, :]).then_inc(s_d, 16)
            e.wait_ge(s_d, cnt.bump("s_d", 64))

        @blk.vector
        def _(e):
            e.memset(negc[:], -C_SHIFT)
            e.memset(epscol[:], 1e-5)
            e.memset(v_tm[:, :, :, 64:65], 1.0)

        @blk.tensor
        def _(e):
            e.wait_ge(s_d, cnt.get("s_d"))
            e.matmul(psb[:, 0:512], onesf[:], brow[0:1, 0:512], start=True, stop=True)
            e.matmul(psb[:, 512:1024], onesf[:], brow[0:1, 512:1024],
                     start=True, stop=True).then_inc(s_t, 1)
            cnt.bump("s_t")
            cnt.v["s3_bo1"] = cnt.get("s_t")

        @blk.scalar
        def _(e):
            e.wait_ge(s_t, cnt.get("s3_bo1"))
            e.copy(bbo[:], psb[:]).then_inc(s_a, 1)
            cnt.bump("s_a")

    with nc.Block("s3_kv") as blk, \
         nc.sbuf_tensor("s3_wk", [128, 8, 1024], BF16) as wk_sb, \
         nc.sbuf_tensor("s3_wv", [128, 8, 1024], BF16) as wv_sb, \
         nc.sbuf_tensor("s3_bbv", [128, 1024], F32) as bbv, \
         nc.sbuf_tensor("s3_brow2", [1, 1024], F32) as brow, \
         nc.sbuf_tensor("s3_xtm", [128, 2, 4, 1024], BF16) as xtm, \
         nc.sbuf_tensor("s3_xfm", [128, 2, 8, 512], BF16) as xfm, \
         nc.psum_tensor("s3_ptr", [128, 1024], BF16) as ptr, \
         nc.psum_tensor("s3_pp0", [128, 512], F32) as pp0, \
         nc.psum_tensor("s3_pp1", [128, 512], F32) as pp1:
        pps = [pp0, pp1]

        @blk.sync
        def _(e):
            e.dma_start(brow[0:1, 0:1024], io["bv"][None, :]).then_inc(s_d, 16)
            e.wait_ge(s_d, cnt.bump("s_d", 16))
            cnt.v["s3_bvrow"] = cnt.get("s_d")

        @blk.tensor
        def _(e):
            e.wait_ge(s_d, cnt.get("s3_bvrow"))
            e.matmul(pp0[:], onesf[:], brow[0:1, 0:512], start=True, stop=True)
            e.matmul(pp1[:], onesf[:], brow[0:1, 512:1024],
                     start=True, stop=True).then_inc(s_t, 1)
            cnt.bump("s_t")
            cnt.v["s3_bv1"] = cnt.get("s_t")

        @blk.scalar
        def _(e):
            e.wait_ge(s_t, cnt.get("s3_bv1"))
            e.copy(bbv[:, 0:512], pp0[:])
            e.copy(bbv[:, 512:1024], pp1[:]).then_inc(s_a, 1)
            cnt.bump("s_a")
            cnt.v["s3_bv2"] = cnt.get("s_a")

        t_kv = cnt.get("s_t")

        @blk.gpsimd
        def _(e):
            e.dma_start(wk_sb[:], io["wkt"].rearrange("(c p) j -> p c j", p=128)
                        ).then_inc(s_g, 16)
            e.dma_start(wv_sb[:], io["wvt"].rearrange("(c p) j -> p c j", p=128)
                        ).then_inc(s_g, 16)
            e.wait_ge(s_g, cnt.bump("s_g", 32))
            for cch in range(8):
                if cch >= 2:
                    e.wait_ge(s_t, t_kv + 20 * (cch - 2) + 4)
                e.dma_start(
                    xtm[:, cch % 2, :, :],
                    io["x2"].rearrange("(a p) j -> p a j", p=128)[:, 4 * cch:4 * cch + 4, :],
                ).then_inc(s_g, 16)
                e.wait_ge(s_g, cnt.bump("s_g", 16))
                cnt.v[f"s3_xld_{cch}"] = cnt.get("s_g")

        for cch in range(8):
            par = cch % 2

            @blk.tensor
            def _(e, cch=cch, par=par):
                e.wait_ge(s_g, cnt.get(f"s3_xld_{cch}"))
                v_prek = cnt.get("s_v")
                for tl in range(4):
                    if tl >= 1:
                        e.wait_ge(s_v, v_prek + tl)
                    elif cch >= 1:
                        e.wait_ge(s_v, cnt.get(f"s3_xev_{cch - 1}"))
                    ins = None
                    for f in range(8):
                        ins = e.transpose(ptr[:, 128 * f:128 * f + 128],
                                          xtm[:, par, tl, 128 * f:128 * f + 128],
                                          identb[:])
                    ins.then_inc(s_t, 1)
                    cnt.bump("s_t")
                    cnt.v[f"s3_xtr1_{cch}_{tl}"] = cnt.get("s_t")
                cnt.v[f"s3_xtr_{cch}"] = cnt.get("s_t")


            @blk.vector
            def _(e, cch=cch, par=par):
                for tl in range(4):
                    e.wait_ge(s_t, cnt.get(f"s3_xtr1_{cch}_{tl}"))
                    e.tensor_copy(
                        xfm[:, par, :, 128 * tl:128 * tl + 128],
                        ptr[:].rearrange("p (a b) -> p a b", a=8),
                    ).then_inc(s_v, 1)
                    cnt.bump("s_v")
                cnt.v[f"s3_xev_{cch}"] = cnt.get("s_v")

            vb_k = cnt.get("s_v")

            @blk.tensor
            def _(e, cch=cch, par=par, vb_k=vb_k):
                e.wait_ge(s_v, cnt.get(f"s3_xev_{cch}"))
                for m in range(8):
                    pi = m % 2
                    if m >= 2:
                        e.wait_ge(s_v, vb_k + m - 1)
                    elif cch >= 1:
                        e.wait_ge(s_v, cnt.get(f"s3_vev_{cch - 1}_{m + 6}"))
                    else:
                        e.wait_ge(s_a, cnt.get("s3_bv2"))
                    ins = None
                    for kc in range(8):
                        ins = e.matmul(pps[pi][:], wk_sb[:, kc, 128 * m:128 * m + 128],
                                       xfm[:, par, kc, :], start=(kc == 0), stop=(kc == 7))
                    ins.then_inc(s_t, 1)
                    cnt.bump("s_t")
                    cnt.v[f"s3_kp_{cch}_{m}"] = cnt.get("s_t")

            @blk.vector
            def _(e, cch=cch, par=par):
                for m in range(8):
                    e.wait_ge(s_t, cnt.get(f"s3_kp_{cch}_{m}"))
                    e.tensor_scalar(k_fm[:, m, 512 * cch:512 * cch + 512],
                                    pps[m % 2][:], bk_sb[:, m:m + 1], None,
                                    ALU.add).then_inc(s_v, 1)
                    cnt.bump("s_v")
                    cnt.v[f"s3_kev_{cch}_{m}"] = cnt.get("s_v")

            @blk.tensor
            def _(e, cch=cch, par=par, vb_k=vb_k):
                for tl in range(4):
                    for half in range(2):
                        j = 2 * tl + half
                        pi = j % 2
                        if j >= 2:
                            e.wait_ge(s_v, vb_k + 8 + j - 1)
                        else:
                            e.wait_ge(s_v, vb_k + 6 + j + 1)
                        ins = None
                        for kc in range(8):
                            ins = e.matmul(
                                pps[pi][:],
                                xfm[:, par, kc, 128 * tl:128 * tl + 128],
                                wv_sb[:, kc, 512 * half:512 * half + 512],
                                start=(kc == 0), stop=(kc == 7))
                        ins.then_inc(s_t, 1)
                        cnt.bump("s_t")
                        cnt.v[f"s3_vp_{cch}_{j}"] = cnt.get("s_t")

            @blk.vector
            def _(e, cch=cch, par=par):
                for tl in range(4):
                    for half in range(2):
                        j = 2 * tl + half
                        e.wait_ge(s_t, cnt.get(f"s3_vp_{cch}_{j}"))
                        e.tensor_tensor(
                            v_tm[:, 4 * cch + tl, 8 * half:8 * half + 8, 0:64],
                            pps[j % 2][:].rearrange("p (h c) -> p h c", h=8),
                            bbv[:, 512 * half:512 * half + 512]
                            .rearrange("p (h c) -> p h c", h=8),
                            ALU.add).then_inc(s_v, 1)
                        cnt.bump("s_v")
                        cnt.v[f"s3_vev_{cch}_{j}"] = cnt.get("s_v")


    wq_sb = sb("s3_wq2", [128, 8, 1024], BF16)
    wo_sb = sb("s3_wo2", [128, 8, 1024], BF16)
    with nc.Block("s3_ld2") as blk:
        @blk.gpsimd
        def _(e):
            e.dma_start(wq_sb[:], io["wqt"].rearrange("(c p) j -> p c j", p=128)
                        ).then_inc(s_g, 16)
            e.dma_start(wo_sb[:], io["wot"].rearrange("(c p) j -> p c j", p=128)
                        ).then_inc(s_g, 16)
            e.wait_ge(s_g, cnt.bump("s_g", 32))

    for qc in range(4):
        q0 = 512 * qc
        with ExitStack() as stc:
            sbc = lambda nm, shp, dt: stc.enter_context(nc.sbuf_tensor(nm, shp, dt))
            qfm = sbc(f"s3_qfm_{qc}", [128, 8, 512], BF16)
            a_buf = sbc(f"s3_abuf_{qc}", [128, 2, 2, 512], BF16)
            o_tm = sbc(f"s3_otm_{qc}", [128, 4, 1024], BF16)
            o_fm = sbc(f"s3_ofm_{qc}", [128, 8, 256], BF16)
            dcol = sbc(f"s3_dcol_{qc}", [128, 8], F32)
            rcol = sbc(f"s3_rcol_{qc}", [128, 8], F32)
            stA = stc.enter_context(ExitStack())
            blk = stA.enter_context(nc.Block(f"s3_prep_{qc}"))
            xqtm = stA.enter_context(nc.sbuf_tensor(f"s3_xqtm_{qc}", [128, 4, 1024], BF16))
            xqfm = stA.enter_context(nc.sbuf_tensor(f"s3_xqfm_{qc}", [128, 8, 512], BF16))
            ptr2 = stA.enter_context(nc.psum_tensor(f"s3_ptr2_{qc}", [128, 1024], BF16))
            psP = stA.enter_context(nc.psum_tensor(f"s3_psP_{qc}", [128, 1024], F32))

            @blk.gpsimd
            def _(e):
                e.dma_start(
                    xqtm[:],
                    io["x2q"].rearrange("(a p) j -> p a j", p=128)[:, 4 * qc:4 * qc + 4, :],
                ).then_inc(s_g, 16)
                e.wait_ge(s_g, cnt.bump("s_g", 16))
                cnt.v[f"s3_xqld_{qc}"] = cnt.get("s_g")

            v_pre3 = cnt.get("s_v")

            @blk.tensor
            def _(e):
                e.wait_ge(s_g, cnt.get(f"s3_xqld_{qc}"))
                ins = None
                for tl in range(4):
                    if tl >= 1:
                        e.wait_ge(s_v, v_pre3 + tl)
                    for f in range(8):
                        ins = e.transpose(ptr2[:, 128 * f:128 * f + 128],
                                          xqtm[:, tl, 128 * f:128 * f + 128],
                                          identb[:])
                    ins.then_inc(s_t, 1)
                    cnt.bump("s_t")
                    cnt.v[f"s3_qtr_{qc}_{tl}"] = cnt.get("s_t")

            @blk.vector
            def _(e):
                for tl in range(4):
                    e.wait_ge(s_t, cnt.get(f"s3_qtr_{qc}_{tl}"))
                    e.tensor_copy(
                        xqfm[:, :, 128 * tl:128 * tl + 128],
                        ptr2[:].rearrange("p (a b) -> p a b", a=8),
                    ).then_inc(s_v, 1)
                    cnt.bump("s_v")
                cnt.v[f"s3_qev_{qc}"] = cnt.get("s_v")

            qb_pe = cnt.get("s_v")

            @blk.tensor
            def _(e):
                e.wait_ge(s_v, cnt.get(f"s3_qev_{qc}"))
                for m in range(8):
                    pi = m % 2
                    if m >= 2:
                        e.wait_ge(s_v, qb_pe + m - 1)
                    ins = None
                    for kc in range(8):
                        ins = e.matmul(psP[:, 512 * pi:512 * pi + 512],
                                       wq_sb[:, kc, 128 * m:128 * m + 128],
                                       xqfm[:, kc, :], start=(kc == 0), stop=(kc == 7))
                    ins.then_inc(s_t, 1)
                    cnt.bump("s_t")
                    cnt.v[f"s3_qp_{qc}_{m}"] = cnt.get("s_t")

            @blk.vector
            def _(e):
                for m in range(8):
                    e.wait_ge(s_t, cnt.get(f"s3_qp_{qc}_{m}"))
                    e.tensor_scalar(qfm[:, m, :], psP[:, 512 * (m % 2):512 * (m % 2) + 512],
                                    bq_sb[:, m:m + 1], None, ALU.add).then_inc(s_v, 1)
                    cnt.bump("s_v")
                    cnt.v[f"s3_qpe_{qc}_{m}"] = cnt.get("s_v")

            stA.close()
            x2own = sbc(f"s3_x2own_{qc}", [128, 2, 1024], F32)
            x3p = sbc(f"s3_x3p_{qc}", [128, 2, 1024], F32)
            st6 = sbc(f"s3_st6_{qc}", [128, 12], F32)
            mv = sbc(f"s3_mv_{qc}", [128, 2], F32)
            lnc = sbc(f"s3_lnc_{qc}", [128, 4], F32)
            stB = stc.enter_context(ExitStack())
            blk = stB.enter_context(nc.Block(f"s3_attn_{qc}"))
            psS = [stB.enter_context(nc.psum_tensor(f"s3_psS{i}_{qc}", [128, 1024], F32))
                   for i in range(2)]
            psO = stB.enter_context(nc.psum_tensor(f"s3_psO_{qc}", [128, 2048], F32))

            @blk.sync
            def _(e):
                for tl in range(2):
                    e.dma_start(
                        x2own[:, tl, :],
                        io["x2q"][512 * qc + 128 * tl:512 * qc + 128 * tl + 128, :],
                    ).then_inc(s_d, 16)
                    e.wait_ge(s_d, cnt.bump("s_d", 16))
                    cnt.v[f"s3_x2own_{qc}_{tl}"] = cnt.get("s_d")

            t_base = cnt.get("s_t")
            a_base = cnt.get("s_a")
            e_base = cnt.get("s_e")
            NID = 8 * 32

            def t_after_qk(idx):
                return t_base + (1 if idx == 0 else 2 * idx)

            def t_after_av(idx):
                return t_base + (2 * idx + 3 if idx < NID - 1 else 2 * NID)

            @blk.tensor
            def _(e):
                e.wait_ge(s_v, cnt.get(f"s3_qpe_{qc}_7"))

                def emit_qk(idx):
                    hp, kt = divmod(idx, 32)
                    par = idx % 2
                    if idx >= 2:
                        e.wait_ge(s_a, a_base + idx - 1)
                    ins = None
                    for hh in range(2):
                        r0 = 64 * hh
                        ins = e.matmul(
                            psS[par][:, 512 * hh:512 * hh + 512],
                            k_fm[r0:r0 + DH, hp, 128 * kt:128 * kt + 128],
                            qfm[r0:r0 + DH, hp, :],
                            start=True, stop=True,
                        )
                    ins.then_inc(s_t, 1)
                    cnt.bump("s_t")

                def emit_av(idx):
                    hp, kt = divmod(idx, 32)
                    par = idx % 2
                    e.wait_ge(s_a, a_base + idx + 1)
                    if kt == 0 and hp > 0:
                        e.wait_ge(s_e, e_base + hp)
                    ins = None
                    for hh in range(2):
                        for j in range(4):
                            ins = e.matmul(
                                psO[0:128, 512 * j + 65 * hh:512 * j + 65 * hh + HS],
                                a_buf[:, par, hh, 128 * j:128 * j + 128],
                                v_tm[:, kt, 2 * hp + hh, :],
                                start=(kt == 0 and hh == 0),
                                stop=(kt == 31 and hh == 1),
                                skip_group_check=True,
                            )
                    ins.then_inc(s_t, 1)
                    cnt.bump("s_t")

                emit_qk(0)
                for idx in range(1, NID):
                    emit_qk(idx)
                    emit_av(idx - 1)
                emit_av(NID - 1)

            @blk.scalar
            def _(e):
                for idx in range(NID):
                    par = idx % 2
                    e.wait_ge(s_t, t_after_qk(idx))
                    e.activation(
                        a_buf[:, par, :, :].rearrange("p a b -> p (a b)"),
                        psS[par][:], AF.Exp, bias=negc[:], scale=0.125,
                    ).then_inc(s_a, 1)
                    cnt.bump("s_a")

            @blk.vector
            def _(e):
                for hp in range(8):
                    e.wait_ge(s_t, t_after_av(hp * 32 + 31))
                    ins = None
                    for k8 in range(8):
                        hh, j = divmod(k8, 4)
                        ins = e.tensor_copy(
                            dcol[:, k8:k8 + 1],
                            psO[:, 512 * j + 65 * hh + 64:512 * j + 65 * hh + 65])
                    ins.then_inc(s_v, 1)
                    e.wait_ge(s_v, cnt.bump("s_v"))
                    ins = None
                    for k8 in range(8):
                        ins = e.reciprocal(rcol[:, k8:k8 + 1], dcol[:, k8:k8 + 1])
                    ins.then_inc(s_v, 1)
                    e.wait_ge(s_v, cnt.bump("s_v"))
                    ins = None
                    for k8 in range(8):
                        hh, j = divmod(k8, 4)
                        ins = e.tensor_scalar_mul(
                            o_tm[:, j, 64 * (2 * hp + hh):64 * (2 * hp + hh) + 64],
                            psO[:, 512 * j + 65 * hh:512 * j + 65 * hh + 64],
                            rcol[:, k8:k8 + 1])
                    ins.then_inc(s_e, 1)
                    cnt.bump("s_e")
                cnt.v[f"s3_oall_{qc}"] = cnt.get("s_e")

            stB.close()
            blk = stc.enter_context(nc.Block(f"s3_out_{qc}"))
            ptr2 = stc.enter_context(nc.psum_tensor(f"s3_ptr3_{qc}", [128, 1024], BF16))
            psP = stc.enter_context(nc.psum_tensor(f"s3_psP2_{qc}", [128, 1024], F32))

            for tl in range(4):
                par3 = tl % 2

                @blk.tensor
                def _(e, tl=tl, par3=par3):
                    if tl == 0:
                        e.wait_ge(s_e, cnt.get(f"s3_oall_{qc}"))
                    else:
                        e.wait_ge(s_v, cnt.get(f"s3_ofme_{qc}_{tl - 1}"))
                    if tl >= 2:
                        e.wait_ge(s_t, cnt.get(f"s3_op_{qc}_{tl - 2}"))
                    ins = None
                    for f in range(8):
                        ins = e.transpose(ptr2[:, 128 * f:128 * f + 128],
                                          o_tm[:, tl, 128 * f:128 * f + 128],
                                          identb[:])
                    ins.then_inc(s_t, 1)
                    cnt.bump("s_t")
                    cnt.v[f"s3_otr_{qc}_{tl}"] = cnt.get("s_t")

                @blk.vector
                def _(e, tl=tl, par3=par3):
                    e.wait_ge(s_t, cnt.get(f"s3_otr_{qc}_{tl}"))
                    e.tensor_copy(
                        o_fm[:, :, 128 * par3:128 * par3 + 128],
                        ptr2[:].rearrange("p (a b) -> p a b", a=8),
                    ).then_inc(s_v, 1)
                    cnt.bump("s_v")
                    cnt.v[f"s3_ofme_{qc}_{tl}"] = cnt.get("s_v")

                @blk.tensor
                def _(e, tl=tl, par3=par3):
                    e.wait_ge(s_v, cnt.get(f"s3_ofme_{qc}_{tl}"))
                    if tl >= 1:
                        e.wait_ge(s_v, cnt.get(f"s3_opev_{qc}_{tl - 1}"))
                    ins = None
                    for half in range(2):
                        for kc in range(8):
                            ins = e.matmul(psP[:, 512 * half:512 * half + 512],
                                           o_fm[:, kc, 128 * par3:128 * par3 + 128],
                                           wo_sb[:, kc, 512 * half:512 * half + 512],
                                           start=(kc == 0), stop=(kc == 7))
                    ins.then_inc(s_t, 1)
                    cnt.bump("s_t")
                    cnt.v[f"s3_op_{qc}_{tl}"] = cnt.get("s_t")

                @blk.vector
                def _(e, tl=tl, par3=par3):
                    e.wait_ge(s_t, cnt.get(f"s3_op_{qc}_{tl}"))
                    e.wait_ge(s_d, cnt.get(f"s3_x2own_{qc}_{tl}"))
                    if tl >= 2:
                        e.wait_ge(s_x3[tl % 2], cnt.get(f"s3_x3st_{qc}_{tl - 2}"))
                    e.tensor_add(x3p[:, par3, :], psP[:], bbo[:]).then_inc(s_v, 1)
                    e.wait_ge(s_v, cnt.bump("s_v"))
                    e.tensor_add(x3p[:, par3, :], x3p[:, par3, :],
                                 x2own[:, tl % 2, :]).then_inc(s_v, 1)
                    cnt.bump("s_v")
                    cnt.v[f"s3_opev_{qc}_{tl}"] = cnt.get("s_v")
                    e.wait_ge(s_v, cnt.get("s_v"))
                    e.bn_stats(st6[:, 0:6], x3p[:, par3, 0:512])
                    e.bn_stats(st6[:, 6:12], x3p[:, par3, 512:1024]).then_inc(s_v, 1)
                    e.wait_ge(s_v, cnt.bump("s_v"))
                    e.bn_aggr(mv[:], st6[:]).then_inc(s_v, 1)
                    cnt.bump("s_v")
                    cnt.v[f"s3_bn_{qc}_{tl}"] = cnt.get("s_v")

                @blk.scalar
                def _(e, tl=tl):
                    e.wait_ge(s_v, cnt.get(f"s3_bn_{qc}_{tl}"))
                    e.activation(lnc[:, 0:1], mv[:, 1:2], AF.Sqrt, bias=epscol[:]
                                 ).then_inc(s_a, 1)
                    cnt.bump("s_a")
                    cnt.v[f"s3_sd_{qc}_{tl}"] = cnt.get("s_a")

                @blk.vector
                def _(e, tl=tl, par3=par3):
                    e.wait_ge(s_a, cnt.get(f"s3_sd_{qc}_{tl}"))
                    e.reciprocal(lnc[:, 1:2], lnc[:, 0:1]).then_inc(s_v, 1)
                    e.wait_ge(s_v, cnt.bump("s_v"))
                    e.scalar_tensor_tensor(lnc[:, 2:3], mv[:, 0:1], -1.0,
                                           lnc[:, 1:2], ALU.mult, ALU.mult
                                           ).then_inc(s_v, 1)
                    e.wait_ge(s_v, cnt.bump("s_v"))
                    e.tensor_scalar(x3p[:, par3, :], x3p[:, par3, :], lnc[:, 1:2],
                                    lnc[:, 2:3], ALU.mult, ALU.add).then_inc(s_v, 1)
                    cnt.bump("s_v")
                    cnt.v[f"s3_x3n_{qc}_{tl}"] = cnt.get("s_v")

                @blk.sync
                def _(e, tl=tl, par3=par3):
                    e.wait_ge(s_v, cnt.get(f"s3_x3n_{qc}_{tl}"))
                    r0 = 512 * qc + 128 * tl
                    e.dma_start(io["x3"][r0:r0 + 128, :], x3p[:, par3, :]
                                ).then_inc(s_x3[tl % 2], 16)
                    cnt.bump(f"s3x3{tl % 2}", 16)
                    cnt.v[f"s3_x3st_{qc}_{tl}"] = cnt.get(f"s3x3{tl % 2}")
                    e.wait_ge(s_x3[tl % 2], cnt.get(f"s3x3{tl % 2}"))
                    if tl + 2 <= 3:
                        t2 = tl + 2
                        e.dma_start(
                            x2own[:, t2 % 2, :],
                            io["x2q"][512 * qc + 128 * t2:512 * qc + 128 * t2 + 128, :],
                        ).then_inc(s_d, 16)
                        e.wait_ge(s_d, cnt.bump("s_d", 16))
                        cnt.v[f"s3_x2own_{qc}_{t2}"] = cnt.get("s_d")



def emit_l1_ffn(nc, st, io, cnt):
    sb = lambda name, shape, dt: st.enter_context(nc.sbuf_tensor(name, shape, dt))
    sem = lambda name: st.enter_context(nc.semaphore(name))

    wg_sb = sb("s4_wg", [128, 8, 3072], BF16)
    wu_sb = sb("s4_wu", [128, 8, 3072], BF16)
    wd_sb = sb("s4_wd", [128, 24, 1024], BF16)
    identb = sb("s4_identb", [128, 128], BF16)
    epscol = sb("s4_eps", [128, 1], F32)

    s_d = sem("s4_d")
    s_g = sem("s4_g")
    s_t = sem("s4_t")
    s_v = sem("s4_v")
    s_a = sem("s4_a")
    s_y = [sem("s4_ya"), sem("s4_yb")]

    with nc.Block("s4_ld") as blk:
        @blk.gpsimd
        def _(e):
            e.dma_start(wg_sb[:], io["wgt"].rearrange("(c p) j -> p c j", p=128)
                        ).then_inc(s_g, 16)
            e.dma_start(wu_sb[:], io["wut"].rearrange("(c p) j -> p c j", p=128)
                        ).then_inc(s_g, 16)
            e.dma_start(wd_sb[:], io["wdt"].rearrange("(c p) j -> p c j", p=128)
                        ).then_inc(s_g, 16)
            e.dma_start(identb[:], io["c_ident"]).then_inc(s_g, 16)
            e.wait_ge(s_g, cnt.bump("s_g", 64))

        @blk.vector
        def _(e):
            e.memset(epscol[:], 1e-5)

    DSC = 1.0 / (127.0 * 127.0)

    for c in range(4):
        t0 = 512 * c
        with ExitStack() as stc:
            blk = stc.enter_context(nc.Block(f"s4_ffn_{c}"))
            sbc = lambda nm, shp, dt: stc.enter_context(nc.sbuf_tensor(nm, shp, dt))
            pscm = lambda nm, shp, dt: stc.enter_context(nc.psum_tensor(nm, shp, dt))
            x3tm = sbc(f"s4_x3tm_{c}", [128, 2, 1024], F32)
            x3bf = sbc(f"s4_x3bf_{c}", [128, 4, 1024], BF16)
            x3fm = sbc(f"s4_x3fm_{c}", [128, 8, 512], BF16)
            gs = sbc(f"s4_gs_{c}", [128, 2, 512], F32)
            h2 = sbc(f"s4_h2_{c}", [128, 24, 512], BF16)
            yp = sbc(f"s4_yp_{c}", [128, 2, 1024], F32)
            st6 = sbc(f"s4_st6_{c}", [128, 12], F32)
            mv = sbc(f"s4_mv_{c}", [128, 2], F32)
            lnc = sbc(f"s4_lnc_{c}", [128, 4], F32)
            ptr = pscm(f"s4_ptr_{c}", [128, 1024], BF16)
            pg0 = pscm(f"s4_pg0_{c}", [128, 512], F32)
            pg1 = pscm(f"s4_pg1_{c}", [128, 512], F32)
            pu0 = pscm(f"s4_pu0_{c}", [128, 512], F32)
            pu1 = pscm(f"s4_pu1_{c}", [128, 512], F32)
            pd = pscm(f"s4_pd_{c}", [128, 1024], F32)
            pgs = [pg0, pg1]
            pus = [pu0, pu1]

            @blk.sync
            def _(e):
                for tl in range(2):
                    e.dma_start(
                        x3tm[:, tl, :],
                        io["x3"][512 * c + 128 * tl:512 * c + 128 * tl + 128, :],
                    ).then_inc(s_d, 16)
                    e.wait_ge(s_d, cnt.bump("s_d", 16))
                    cnt.v[f"s4_xld_{c}_{tl}"] = cnt.get("s_d")

            @blk.gpsimd
            def _(e):
                e.dma_start(
                    x3bf[:],
                    io["x3"].rearrange("(a p) j -> p a j", p=128)[:, 4 * c:4 * c + 4, :],
                ).then_inc(s_g, 16)
                e.wait_ge(s_g, cnt.bump("s_g", 16))
                cnt.v[f"s4_xbf_{c}"] = cnt.get("s_g")

            v_pre = cnt.get("s_v")

            @blk.tensor
            def _(e):
                e.wait_ge(s_g, cnt.get(f"s4_xbf_{c}"))
                ins = None
                for tl in range(4):
                    if tl >= 1:
                        e.wait_ge(s_v, v_pre + tl)
                    for f in range(8):
                        ins = e.transpose(ptr[:, 128 * f:128 * f + 128],
                                          x3bf[:, tl, 128 * f:128 * f + 128],
                                          identb[:])
                    ins.then_inc(s_t, 1)
                    cnt.bump("s_t")
                    cnt.v[f"s4_tr_{c}_{tl}"] = cnt.get("s_t")

            @blk.vector
            def _(e):
                for tl in range(4):
                    e.wait_ge(s_t, cnt.get(f"s4_tr_{c}_{tl}"))
                    e.tensor_copy(
                        x3fm[:, :, 128 * tl:128 * tl + 128],
                        ptr[:].rearrange("p (a b) -> p a b", a=8),
                    ).then_inc(s_v, 1)
                    cnt.bump("s_v")
                cnt.v[f"s4_fm_{c}"] = cnt.get("s_v")

            vb_h2 = cnt.get("s_v")

            def h2mark(m):
                return vb_h2 + m + 1

            @blk.tensor
            def _(e):
                e.wait_ge(s_v, cnt.get(f"s4_fm_{c}"))
                for m in range(24):
                    pi = m % 2
                    if m >= 2:
                        e.wait_ge(s_v, h2mark(m - 2))
                    for kc in range(8):
                        e.matmul(pgs[pi][:], wg_sb[:, kc, 128 * m:128 * m + 128],
                                 x3fm[:, kc, :], start=(kc == 0), stop=(kc == 7))
                    ins = None
                    for kc in range(8):
                        ins = e.matmul(pus[pi][:], wu_sb[:, kc, 128 * m:128 * m + 128],
                                       x3fm[:, kc, :], start=(kc == 0), stop=(kc == 7))
                    ins.then_inc(s_t, 1)
                    cnt.bump("s_t")
                    cnt.v[f"s4_gu_{c}_{m}"] = cnt.get("s_t")

            @blk.scalar
            def _(e):
                for m in range(24):
                    e.wait_ge(s_t, cnt.get(f"s4_gu_{c}_{m}"))
                    if m >= 2:
                        e.wait_ge(s_v, h2mark(m - 2))
                    e.activation(gs[:, m % 2, :], pgs[m % 2][:], AF.Silu,
                                 scale=1.0 / 127.0).then_inc(s_a, 1)
                    cnt.bump("s_a")
                    cnt.v[f"s4_sil_{c}_{m}"] = cnt.get("s_a")

            @blk.vector
            def _(e):
                for m in range(24):
                    e.wait_ge(s_a, cnt.get(f"s4_sil_{c}_{m}"))
                    e.tensor_mul(h2[:, m, :], gs[:, m % 2, :], pus[m % 2][:]
                                 ).then_inc(s_v, 1)
                    cnt.bump("s_v")
                    cnt.v[f"s4_h2_{c}_{m}"] = cnt.get("s_v")

            for tl in range(4):
                par4 = tl % 2

                @blk.tensor
                def _(e, tl=tl):
                    if tl == 0:
                        e.wait_ge(s_v, cnt.get(f"s4_h2_{c}_23"))
                    else:
                        e.wait_ge(s_v, cnt.get(f"s4_ev_{c}_{tl - 1}"))
                    ins = None
                    for half in range(2):
                        for k in range(24):
                            ins = e.matmul(pd[:, 512 * half:512 * half + 512],
                                           h2[:, k, 128 * tl:128 * tl + 128],
                                           wd_sb[:, k, 512 * half:512 * half + 512],
                                           start=(k == 0), stop=(k == 23))
                    ins.then_inc(s_t, 1)
                    cnt.bump("s_t")
                    cnt.v[f"s4_dp_{c}_{tl}"] = cnt.get("s_t")

                @blk.vector
                def _(e, tl=tl, par4=par4):
                    e.wait_ge(s_t, cnt.get(f"s4_dp_{c}_{tl}"))
                    if tl >= 2:
                        e.wait_ge(s_y[tl % 2], cnt.get(f"s4_yst_{c}_{tl - 2}"))
                    e.wait_ge(s_d, cnt.get(f"s4_xld_{c}_{tl}"))
                    e.scalar_tensor_tensor(yp[:, par4, 0:512], pd[:, 0:512], DSC,
                                           x3tm[:, tl % 2, 0:512], ALU.mult, ALU.add)
                    e.scalar_tensor_tensor(yp[:, par4, 512:1024], pd[:, 512:1024], DSC,
                                           x3tm[:, tl % 2, 512:1024], ALU.mult, ALU.add
                                           ).then_inc(s_v, 1)
                    cnt.bump("s_v")
                    cnt.v[f"s4_ev_{c}_{tl}"] = cnt.get("s_v")
                    e.wait_ge(s_v, cnt.get("s_v"))
                    e.bn_stats(st6[:, 0:6], yp[:, par4, 0:512])
                    e.bn_stats(st6[:, 6:12], yp[:, par4, 512:1024]).then_inc(s_v, 1)
                    e.wait_ge(s_v, cnt.bump("s_v"))
                    e.bn_aggr(mv[:], st6[:]).then_inc(s_v, 1)
                    cnt.bump("s_v")
                    cnt.v[f"s4_bn_{c}_{tl}"] = cnt.get("s_v")

                @blk.scalar
                def _(e, tl=tl):
                    e.wait_ge(s_v, cnt.get(f"s4_bn_{c}_{tl}"))
                    e.activation(lnc[:, 0:1], mv[:, 1:2], AF.Sqrt, bias=epscol[:]
                                 ).then_inc(s_a, 1)
                    cnt.bump("s_a")
                    cnt.v[f"s4_sd_{c}_{tl}"] = cnt.get("s_a")

                @blk.vector
                def _(e, tl=tl, par4=par4):
                    e.wait_ge(s_a, cnt.get(f"s4_sd_{c}_{tl}"))
                    e.reciprocal(lnc[:, 1:2], lnc[:, 0:1]).then_inc(s_v, 1)
                    e.wait_ge(s_v, cnt.bump("s_v"))
                    e.scalar_tensor_tensor(lnc[:, 2:3], mv[:, 0:1], -1.0,
                                           lnc[:, 1:2], ALU.mult, ALU.mult
                                           ).then_inc(s_v, 1)
                    e.wait_ge(s_v, cnt.bump("s_v"))
                    e.tensor_scalar(yp[:, par4, :], yp[:, par4, :], lnc[:, 1:2],
                                    lnc[:, 2:3], ALU.mult, ALU.add).then_inc(s_v, 1)
                    cnt.bump("s_v")
                    cnt.v[f"s4_yn_{c}_{tl}"] = cnt.get("s_v")

                @blk.sync
                def _(e, tl=tl, par4=par4):
                    e.wait_ge(s_v, cnt.get(f"s4_yn_{c}_{tl}"))
                    r0 = 512 * c + 128 * tl
                    e.dma_start(io["y"][r0:r0 + 128, :], yp[:, par4, :]
                                ).then_inc(s_y[tl % 2], 16)
                    cnt.bump(f"s4y{tl % 2}", 16)
                    cnt.v[f"s4_yst_{c}_{tl}"] = cnt.get(f"s4y{tl % 2}")
                    e.wait_ge(s_y[tl % 2], cnt.get(f"s4y{tl % 2}"))
                    if tl + 2 <= 3:
                        t2 = tl + 2
                        e.dma_start(
                            x3tm[:, t2 % 2, :],
                            io["x3"][512 * c + 128 * t2:512 * c + 128 * t2 + 128, :],
                        ).then_inc(s_d, 16)
                        e.wait_ge(s_d, cnt.bump("s_d", 16))
                        cnt.v[f"s4_xld_{c}_{t2}"] = cnt.get("s_d")



def build_stage1():
    nc = bass.Bass("TRN2", target_bir_lowering=False)
    io = {}
    io["x"] = nc.dram_tensor("x", [L, D], F32, kind="ExternalInput").ap()
    io["wq"] = nc.dram_tensor("wq", [D, D], F32, kind="ExternalInput").ap()
    io["wk"] = nc.dram_tensor("wk", [D, D], F32, kind="ExternalInput").ap()
    io["wv"] = nc.dram_tensor("wv", [D, D], F32, kind="ExternalInput").ap()
    for nm, arr in host_consts().items():
        io[nm] = nc.dram_tensor(nm, list(arr.shape), F32, kind="ExternalInput").ap()
    io["attnout"] = nc.dram_tensor("attnout", [L, D], F32, kind="ExternalOutput").ap()
    if DEBUG_S1:
        io["dbg_k"] = nc.dram_tensor("dbg_k", [128, 8 * NKT * 128], F32, kind="ExternalOutput").ap()
        io["dbg_q"] = nc.dram_tensor("dbg_q", [128, 8 * NKT * 128], F32, kind="ExternalOutput").ap()
        io["dbg_v"] = nc.dram_tensor("dbg_v", [128, NKT * H * HS], F32, kind="ExternalOutput").ap()
        io["dbg_xfm"] = nc.dram_tensor("dbg_xfm", [128, 8 * NKT * 128], F32, kind="ExternalOutput").ap()
        io["dbg_den"] = nc.dram_tensor("dbg_den", [128, 64], F32, kind="ExternalOutput").ap()
        io["dbg_num"] = nc.dram_tensor("dbg_num", [128, 64], F32, kind="ExternalOutput").ap()
    cnt = Cnt()
    with ExitStack() as st:
        emit_l0_attn(nc, st, io, cnt)
    return nc


def build_stage2():
    nc = bass.Bass("TRN2", target_bir_lowering=False)
    io = {}
    io["x"] = nc.dram_tensor("x", [L, D], F32, kind="ExternalInput").ap()
    io["attnout"] = nc.dram_tensor("attnout", [L, D], F32, kind="ExternalInput").ap()
    io["w1t"] = nc.dram_tensor("w1t", [1024, 4096], F32, kind="ExternalInput").ap()
    io["w2t"] = nc.dram_tensor("w2t", [4096, 1024], F32, kind="ExternalInput").ap()
    io["cw"] = nc.dram_tensor("cw", [128, 96], F32, kind="ExternalInput").ap()
    io["cb"] = nc.dram_tensor("cb", [128, 32], F32, kind="ExternalInput").ap()
    for nm, arr in host_consts().items():
        io[nm] = nc.dram_tensor(nm, list(arr.shape), F32, kind="ExternalInput").ap()
    io["c_ones"] = nc.dram_tensor("c_ones", [128, 128], F32, kind="ExternalInput").ap()
    io["zln"] = nc.dram_tensor("zln", [L, D], F32, kind="Internal").ap()
    io["xqf"] = nc.dram_tensor("xqf", [1024, 4096], BF16, kind="Internal").ap()
    io["w1qd"] = nc.dram_tensor("w1qd", [32, 128, 1024], I8, kind="Internal").ap()
    io["scr"] = nc.dram_tensor("scr", [512], F32, kind="Internal").ap()
    io["x2"] = nc.dram_tensor("x2", [L, D], F32, kind="ExternalOutput").ap()
    cnt = Cnt()
    with ExitStack() as st:
        emit_l0_ffn(nc, st, io, cnt)
    return nc


def host_stage2_weights(inputs):
    w1 = inputs["L0_ffn_w1"]
    w2 = inputs["L0_ffn_w2"]
    cw = inputs["L0_ffn_conv_w"]
    cb = inputs["L0_ffn_conv_b"]
    return {
        "w1t": np.ascontiguousarray(w1.T),
        "w2t": np.ascontiguousarray(w2.T),
        "cw": np.ascontiguousarray(
            cw.reshape(32, 128, 3).transpose(1, 0, 2).reshape(128, 96)),
        "cb": np.ascontiguousarray(cb.reshape(32, 128).T),
        "c_ones": np.ones((128, 128), np.float32),
    }


def build_stage3():
    nc = bass.Bass("TRN2", target_bir_lowering=False)
    io = {}
    io["x2"] = nc.dram_tensor("x2", [L, D], F32, kind="ExternalInput").ap()
    io["x2q"] = nc.dram_tensor("x2q", [2048, D], F32, kind="ExternalInput").ap()
    for nm in ("wqt", "wkt", "wvt", "wot"):
        io[nm] = nc.dram_tensor(nm, [D, D], F32, kind="ExternalInput").ap()
    io["bq8"] = nc.dram_tensor("bq8", [128, 8], F32, kind="ExternalInput").ap()
    io["bk8"] = nc.dram_tensor("bk8", [128, 8], F32, kind="ExternalInput").ap()
    io["bv"] = nc.dram_tensor("bv", [D], F32, kind="ExternalInput").ap()
    io["bo"] = nc.dram_tensor("bo", [D], F32, kind="ExternalInput").ap()
    for nm, arr in host_consts().items():
        io[nm] = nc.dram_tensor(nm, list(arr.shape), F32, kind="ExternalInput").ap()
    io["c_ones"] = nc.dram_tensor("c_ones", [128, 128], F32, kind="ExternalInput").ap()
    io["x3"] = nc.dram_tensor("x3", [2048, D], F32, kind="ExternalOutput").ap()
    cnt = Cnt()
    with ExitStack() as st:
        emit_l1_attn(nc, st, io, cnt)
    return nc


def build_stage4():
    nc = bass.Bass("TRN2", target_bir_lowering=False)
    io = {}
    io["x3"] = nc.dram_tensor("x3", [2048, D], F32, kind="ExternalInput").ap()
    io["wgt"] = nc.dram_tensor("wgt", [1024, 3072], I32, kind="ExternalInput").ap()
    io["wut"] = nc.dram_tensor("wut", [1024, 3072], I32, kind="ExternalInput").ap()
    io["wdt"] = nc.dram_tensor("wdt", [3072, 1024], I32, kind="ExternalInput").ap()
    for nm, arr in host_consts().items():
        io[nm] = nc.dram_tensor(nm, list(arr.shape), F32, kind="ExternalInput").ap()
    io["y"] = nc.dram_tensor("y", [2048, D], F32, kind="ExternalOutput").ap()
    cnt = Cnt()
    with ExitStack() as st:
        emit_l1_ffn(nc, st, io, cnt)
    return nc


def build_fused():
    nc = bass.Bass("TRN2", target_bir_lowering=False)
    io = {}
    io["x"] = nc.dram_tensor("x", [L, D], F32, kind="ExternalInput").ap()
    io["wq"] = nc.dram_tensor("wq", [D, D], F32, kind="ExternalInput").ap()
    io["wk"] = nc.dram_tensor("wk", [D, D], F32, kind="ExternalInput").ap()
    io["wv"] = nc.dram_tensor("wv", [D, D], F32, kind="ExternalInput").ap()
    io["w1t"] = nc.dram_tensor("w1t", [1024, 4096], F32, kind="ExternalInput").ap()
    io["w2t"] = nc.dram_tensor("w2t", [4096, 1024], F32, kind="ExternalInput").ap()
    io["cw"] = nc.dram_tensor("cw", [128, 96], F32, kind="ExternalInput").ap()
    io["cb"] = nc.dram_tensor("cb", [128, 32], F32, kind="ExternalInput").ap()
    for nm in ("wqt", "wkt", "wvt", "wot"):
        io[nm] = nc.dram_tensor(nm, [D, D], F32, kind="ExternalInput").ap()
    io["bq8"] = nc.dram_tensor("bq8", [128, 8], F32, kind="ExternalInput").ap()
    io["bk8"] = nc.dram_tensor("bk8", [128, 8], F32, kind="ExternalInput").ap()
    io["bv"] = nc.dram_tensor("bv", [D], F32, kind="ExternalInput").ap()
    io["bo"] = nc.dram_tensor("bo", [D], F32, kind="ExternalInput").ap()
    io["wgt"] = nc.dram_tensor("wgt", [1024, 3072], I32, kind="ExternalInput").ap()
    io["wut"] = nc.dram_tensor("wut", [1024, 3072], I32, kind="ExternalInput").ap()
    io["wdt"] = nc.dram_tensor("wdt", [3072, 1024], I32, kind="ExternalInput").ap()
    for nm, arr in host_consts().items():
        io[nm] = nc.dram_tensor(nm, list(arr.shape), F32, kind="ExternalInput").ap()
    io["c_ones"] = nc.dram_tensor("c_ones", [128, 128], F32, kind="ExternalInput").ap()
    io["attnout"] = nc.dram_tensor("attnout", [L, D], F32, kind="Internal").ap()
    io["zln"] = nc.dram_tensor("zln", [L, D], F32, kind="Internal").ap()
    io["xqf"] = nc.dram_tensor("xqf", [1024, 4096], BF16, kind="Internal").ap()
    io["w1qd"] = nc.dram_tensor("w1qd", [32, 128, 1024], I8, kind="Internal").ap()
    io["scr"] = nc.dram_tensor("scr", [512], F32, kind="Internal").ap()
    io["x2"] = nc.dram_tensor("x2", [L, D], F32, kind="Internal").ap()
    io["x2q"] = nc.dram_tensor("x2q", [2048, D], F32, kind="Internal").ap()
    io["x3"] = nc.dram_tensor("x3", [2048, D], F32, kind="Internal").ap()
    io["y"] = nc.dram_tensor("y", [2048, D], F32, kind="ExternalOutput").ap()

    with ExitStack() as st1:
        emit_l0_attn(nc, st1, io, Cnt())
    with ExitStack() as st2:
        emit_l0_ffn(nc, st2, io, Cnt())

    with nc.semaphore("fx_sem") as s_fx, nc.Block("fuse_x2q") as blk:
        @blk.sync
        def _(e):
            for arm in e.switch_core_id(n=128):
                hf = arm.logical % 2
                e.dma_start(io["x2q"],
                            io["x2"][2048 * hf:2048 * hf + 2048, :]).then_inc(s_fx, 16)
            e.wait_ge(s_fx, 16)

    with ExitStack() as st3:
        emit_l1_attn(nc, st3, io, Cnt())
    with ExitStack() as st4:
        emit_l1_ffn(nc, st4, io, Cnt())
    return nc


_NC_CACHE = {}


def _get_nc(name, builder):
    if name not in _NC_CACHE:
        _NC_CACHE[name] = builder()
    return _NC_CACHE[name]


def kernel(**inputs):
    consts = host_consts()
    cores = list(range(8))
    nc = _get_nc("fused", build_fused)
    wts2 = host_stage2_weights(inputs)
    in_maps = []
    for c in cores:
        b = c // 2
        m = {"x": np.ascontiguousarray(inputs["L0_attn_x"][b]),
             "wq": np.ascontiguousarray(inputs["L0_attn_W_q"]),
             "wk": np.ascontiguousarray(inputs["L0_attn_W_k"]),
             "wv": np.ascontiguousarray(inputs["L0_attn_W_v"]),
             "wqt": np.ascontiguousarray(inputs["L1_attn_Wq"].T),
             "wkt": np.ascontiguousarray(inputs["L1_attn_Wk"].T),
             "wvt": np.ascontiguousarray(inputs["L1_attn_Wv"].T),
             "wot": np.ascontiguousarray(inputs["L1_attn_Wo"].T),
             "bq8": np.ascontiguousarray(inputs["L1_attn_bq"].reshape(8, 128).T),
             "bk8": np.ascontiguousarray(inputs["L1_attn_bk"].reshape(8, 128).T),
             "bv": np.ascontiguousarray(inputs["L1_attn_bv"]),
             "bo": np.ascontiguousarray(inputs["L1_attn_bo"]),
             "wgt": np.ascontiguousarray(inputs["L1_ffn_w_gate"].T),
             "wut": np.ascontiguousarray(inputs["L1_ffn_w_up"].T),
             "wdt": np.ascontiguousarray(inputs["L1_ffn_w_down"].T),
             "c_ones": np.ones((128, 128), np.float32)}
        m.update(consts)
        m.update(wts2)
        in_maps.append(m)
    res = run_bass_kernel_spmd(nc, in_maps, core_ids=cores)
    y = [res.results[c]["y"] for c in cores]
    out = np.empty((B, L, D), np.float32)
    for b in range(B):
        out[b, 0:2048] = y[2 * b]
        out[b, 2048:4096] = y[2 * b + 1]
    return out
